# version 13
# baseline (speedup 1.0000x reference)
import numpy as np
import ml_dtypes
import concourse.bass as bass
import concourse.mybir as mybir
from concourse.bass_utils import run_bass_kernel_spmd

F32 = mybir.dt.float32
BF16 = mybir.dt.bfloat16
AF = mybir.ActivationFunctionType
ALU = mybir.AluOpType
AX = mybir.AxisListType.X

NCORES = 8
D = 1024
DC = 8
NH = 16
NPAIR = 8
NE = 8
HF = 512
JCN = 4
EPS = 1e-5
BIG = 1.0e30

ENGS = ("sy", "gp", "pe", "ve", "ac")
DMA_ENGS = ("sy", "gp")


class Em:

    def __init__(self, nc):
        self.nc = nc
        self.sched = {k: [] for k in ENGS}
        self.cnt = {}
        self.sems = {}
        for k in ("pe", "ve", "ac"):
            self.sems[k] = nc.alloc_semaphore(f"sem_{k}")
            self.cnt[k] = 0
        self.waited = {k: {} for k in ENGS}
        self.wtok = {}
        self.rtok = {}

    def _sem(self, key):
        if key not in self.sems:
            self.sems[key] = self.nc.alloc_semaphore(f"sem_{key}")
            self.cnt[key] = 0
        return self.sems[key]

    def _wait(self, eng, tok):
        if tok is None:
            return
        src, v = tok
        if self.waited[eng].get(src, 0) >= v:
            return
        self.waited[eng][src] = v
        sem = self.sems[src]
        self.sched[eng].append(lambda e, sem=sem, v=v: e.wait_ge(sem, v))

    def _pre(self, eng, reads, writes):
        for r in reads:
            self._wait(eng, self.wtok.get(r))
        for w in writes:
            self._wait(eng, self.wtok.get(w))
            for src, v in self.rtok.get(w, {}).items():
                self._wait(eng, (src, v))

    def _post(self, tok, eng, reads, writes):
        for w in writes:
            self.wtok[w] = tok
            self.rtok[w] = {}
        for r in reads:
            d = self.rtok.setdefault(r, {})
            if d.get(tok[0], 0) < tok[1]:
                d[tok[0]] = tok[1]

    def op(self, eng, fn, reads=(), writes=(), ch=None):
        assert (eng in DMA_ENGS) == (ch is not None)
        self._pre(eng, reads, writes)
        key = ch if ch is not None else eng
        sem = self._sem(key) if ch is not None else self.sems[key]
        step = 16 if ch is not None else 1
        self.cnt[key] = self.cnt.get(key, 0) + step
        v = self.cnt[key]
        self.sched[eng].append(
            lambda e, fn=fn, sem=sem, step=step: fn(e).then_inc(sem, step))
        tok = (key, v)
        self._post(tok, eng, reads, writes)
        return tok

    def group(self, eng, fns, reads=(), writes=()):
        assert eng not in DMA_ENGS
        self._pre(eng, reads, writes)
        for fn in fns[:-1]:
            self.sched[eng].append(fn)
        self.cnt[eng] += 1
        v = self.cnt[eng]
        sem = self.sems[eng]
        fn = fns[-1]
        self.sched[eng].append(
            lambda e, fn=fn, sem=sem: fn(e).then_inc(sem, 1))
        tok = (eng, v)
        self._post(tok, eng, reads, writes)
        return tok

    def dma_series(self, eng, ch, items):
        allr, allw = [], []
        for fn, rd, wr in items:
            allr += list(rd)
            allw += list(wr)
        self._pre(eng, tuple(allr), tuple(allw))
        sem = self._sem(ch)
        tok = None
        for fn, rd, wr in items:
            self.cnt[ch] += 16
            v = self.cnt[ch]
            self.sched[eng].append(
                lambda e, fn=fn, sem=sem: fn(e).then_inc(sem, 16))
            tok = (ch, v)
        self._post(tok, eng, tuple(allr), tuple(allw))
        return tok

    def retag(self, bufs, tok):
        for b in bufs:
            self.wtok[b] = tok


def self_cnt(em):
    return em.cnt


def _wfg(c0, c1):
    return tuple(f"wf:{g}" for g in range(c0 // 1024, (c1 + 1023) // 1024))


def _whg(c0, c1):
    return tuple(f"wh:{g}" for g in range(c0 // 1024, (c1 + 1023) // 1024))


def _arg(c0, c1):
    return tuple(f"ar:{g}" for g in range(c0 // 1024, (c1 + 1023) // 1024))


def build_nc(nseq, S, phases=3, p3cut=9, dbg=False):
    nc = bass.Bass("TRN2", target_bir_lowering=False, detect_race_conditions=False)
    NSUB = S // 128
    NTOK = nseq * S
    CAP = min(((NTOK // 4 + 512) + 255) // 256 * 256,
              ((NTOK + 255) // 256) * 256)
    NUNIT = CAP // 256
    T2PS = S // 256
    NT2 = nseq * T2PS
    T3PS = S // 256
    NT3 = nseq * T3PS

    x_in = nc.dram_tensor("L0_attn_x", [nseq, S, D], F32, kind="ExternalInput").ap()
    Wq = nc.dram_tensor("L0_attn_Wq", [D, D], F32, kind="ExternalInput").ap()
    Wk = nc.dram_tensor("L0_attn_Wk", [D, D], F32, kind="ExternalInput").ap()
    Wv = nc.dram_tensor("L0_attn_Wv", [D, D], F32, kind="ExternalInput").ap()
    Wo = nc.dram_tensor("L0_attn_Wo", [D, D], F32, kind="ExternalInput").ap()
    gW = nc.dram_tensor("L0_ffn_gate_W", [D, NE], F32, kind="ExternalInput").ap()
    W1 = nc.dram_tensor("L0_ffn_W1", [NE, D, HF], F32, kind="ExternalInput").ap()
    b1 = nc.dram_tensor("L0_ffn_b1", [NE, HF], F32, kind="ExternalInput").ap()
    W2 = nc.dram_tensor("L0_ffn_W2", [NE, HF, D], F32, kind="ExternalInput").ap()
    b2 = nc.dram_tensor("L0_ffn_b2", [NE, D], F32, kind="ExternalInput").ap()
    Ws1 = nc.dram_tensor("L0_ffn_Ws1", [D, HF], F32, kind="ExternalInput").ap()
    bs1 = nc.dram_tensor("L0_ffn_bs1", [HF], F32, kind="ExternalInput").ap()
    Ws2 = nc.dram_tensor("L0_ffn_Ws2", [HF, D], F32, kind="ExternalInput").ap()
    bs2 = nc.dram_tensor("L0_ffn_bs2", [D], F32, kind="ExternalInput").ap()
    rs_in = nc.dram_tensor("route_scale_col", [NE, 1], F32, kind="ExternalInput").ap()
    out = nc.dram_tensor("out", [nseq, S, D], F32, kind="ExternalOutput").ap()

    xn_scr = nc.dram_tensor("xn_scr", [nseq, DC, 128, S], BF16).ap()
    xm_scr = nc.dram_tensor("xm_scr", [NTOK, 520], F32).ap()
    idx_scr = nc.dram_tensor("idx_scr", [8 * CAP, 1], mybir.dt.int32).ap()
    yab_scr = nc.dram_tensor("yab_scr", [2 * NTOK + 128, D], F32).ap()
    y_scr = nc.dram_tensor("y_scr", [nseq * S // 128, 128, D], F32).ap()
    gate_scr = nc.dram_tensor("gate_scr", [nseq, NE, S], F32).ap()

    ident_bf_c = nc.inline_tensor(np.eye(128, dtype=ml_dtypes.bfloat16), name="c_idbf").ap()
    ident_f_c = nc.inline_tensor(np.eye(128, dtype=np.float32), name="c_idf").ap()
    iota128_c = nc.inline_tensor(
        np.arange(128, dtype=np.float32).reshape(128, 1), name="c_iota").ap()
    iote_c = nc.inline_tensor(
        np.broadcast_to(np.arange(8, dtype=np.float32), (128, 8)).copy(),
        name="c_iote").ap()
    ecap_c = nc.inline_tensor(
        np.broadcast_to(np.arange(8, dtype=np.float32) * CAP - 1.0,
                        (128, 8)).copy(), name="c_ecap").ap()
    bigidx_c = nc.inline_tensor(
        np.full((128, 8 * NUNIT), 1 << 30, dtype=np.int32), name="c_big").ap()

    AR = nc.alloc_sbuf_tensor("AR", [128, 73728], BF16)
    WH = nc.alloc_sbuf_tensor("WH", [128, 8192], BF16)
    WF = nc.alloc_sbuf_tensor("WF", [128, 6144], F32)
    ASM = nc.alloc_sbuf_tensor("ASM", [128, nseq * 1024], BF16)
    RSM = nc.alloc_sbuf_tensor("RSM", [128, nseq * 8], F32)
    idbf = nc.alloc_sbuf_tensor("idbf", [128, 128], BF16)
    idf = nc.alloc_sbuf_tensor("idf", [128, 128], F32)
    iota128 = nc.alloc_sbuf_tensor("iota128", [128, 1], F32)
    iote = nc.alloc_sbuf_tensor("iote", [128, 8], F32)
    ecap = nc.alloc_sbuf_tensor("ecap", [128, 8], F32)
    T2B = nc.alloc_sbuf_tensor("T2B", [128, 72], F32)
    EQS = nc.alloc_sbuf_tensor("EQS", [128, 16 * NSUB], F32)
    META = nc.alloc_sbuf_tensor("META", [128, 16], F32)
    IDXS = nc.alloc_sbuf_tensor("IDXS", [128, 8 * CAP // 128], mybir.dt.int32)
    TKI = nc.alloc_sbuf_tensor("TKI", [128, 12], mybir.dt.int32)
    BASEE = nc.alloc_sbuf_tensor("BASEE", [8, 2], F32)
    gwsb = nc.alloc_sbuf_tensor("gwsb", [128, DC * NE], BF16)
    b1T = nc.alloc_sbuf_tensor("b1T", [128, JCN * NE], F32)
    bs1T = nc.alloc_sbuf_tensor("bs1T", [128, JCN], F32)
    rscol = nc.alloc_sbuf_tensor("rscol", [NE, 1], F32)
    b2all = nc.alloc_sbuf_tensor("b2all", [9, D], F32)
    GT9 = nc.alloc_sbuf_tensor("GT9", [9, 512], F32)
    STT = nc.alloc_sbuf_tensor("STT", [128, 24], F32)
    SMX = nc.alloc_sbuf_tensor("SMX", [128, 64], F32)
    ASB = nc.alloc_sbuf_tensor("ASB", [128, 1024], F32)

    A_W1, A_W2, A_WS1, A_WS2 = 0, 32768, 65536, 69632
    A_WQT, A_WKT, A_NAT1, A_NAT2 = 32768, 40960, 49152, 57344
    A_NATV, A_NATO, A_WVT, A_WOT = 0, 8192, 16384, 24576

    H_XNTM, H_XNF, H_Q, H_K = 0, 3072, 6144, 7168
    H2_XNF, H2_HSC, H2_XNG = 0, 4096, 6144
    H3_XNF, H3_V, H3_AT = 0, 4096, 6144

    F_X1, F_SC2, F_SCORES, F_GFM = 0, 3072, 4096, 4096
    F2_XG0, F2_Y, F2_YS, F2_XG1 = 0, 1024, 3072, 5120
    F2_B1N, F2_BS1N = 4096, 4608
    F3_X, F3_Y, F3_O = 0, 2048, 4096
    T2_RAW, T2_EQ1, T2_MSK, T2_EQ2 = 3072, 3200, 3328, 3456
    T2_GA, T2_GB, T2_GTM, T2_S = 3584, 3712, 3840, 3968

    pY0 = nc.alloc_psum_tensor("pY0", [128, 1024], F32)
    pY1 = nc.alloc_psum_tensor("pY1", [128, 1024], F32)
    pfa = nc.alloc_psum_tensor("pfa", [128, 512], F32)
    pfb = nc.alloc_psum_tensor("pfb", [128, 512], F32)
    pG = nc.alloc_psum_tensor("pG", [128, 512], F32)
    pbf = nc.alloc_psum_tensor("pbf", [128, 1024], BF16)

    em = Em(nc)
    op, grp = em.op, em.group
    rb_idx = nc.alloc_register(mybir.EngineType.Pool, "rb_idx")
    rb_tok = nc.alloc_register(mybir.EngineType.Pool, "rb_tok")
    rb_yab = nc.alloc_register(mybir.EngineType.Pool, "rb_yab")
    em.sched["gp"].append(lambda e: e.reg_mov(rb_idx, 8 * CAP - 1))
    em.sched["gp"].append(lambda e: e.reg_mov(rb_tok, NTOK - 1))
    em.sched["gp"].append(lambda e: e.reg_mov(rb_yab, 2 * NTOK - 1))

    em.dma_series("sy", "csml", [
        (lambda e: e.dma_start(idbf[:], ident_bf_c), (), ("idbf",)),
        (lambda e: e.dma_start(idf[:], ident_f_c), (), ("idf",)),
        (lambda e: e.dma_start(iota128[:], iota128_c), (), ("iota128",)),
        (lambda e: e.dma_start(iote[:], iote_c), (), ("iote",)),
        (lambda e: e.dma_start(ecap[:], ecap_c), (), ("ecap",)),
        (lambda e: e.dma_start(rscol[:], rs_in), (), ("rscol",)),
        (lambda e: e.dma_start(WF[0:NE, F2_B1N:F2_B1N + HF], b1), (),
         _wfg(F2_B1N, F2_B1N + HF)),
        (lambda e: e.dma_start(WF[0:1, F2_BS1N:F2_BS1N + HF],
                               bs1.rearrange("(o j) -> o j", o=1)), (),
         _wfg(F2_BS1N, F2_BS1N + HF)),
    ])
    op("gp", lambda e: e.dma_start(
        gwsb[:], gW.rearrange("(dc p) e -> p dc e", p=128)),
        writes=("gwsb",), ch="gwl")
    op("gp", lambda e: e.dma_start(IDXS[:, 0:8 * NUNIT], bigidx_c),
        writes=("idxbig",), ch="ibig")
    HALF_IDX = 4 * CAP
    for hh_ in range(2):
        op("gp", lambda e, hh_=hh_: e.dma_start(
            idx_scr[hh_ * HALF_IDX:(hh_ + 1) * HALF_IDX].rearrange(
                "(p k) o -> p k o", p=128),
            IDXS[:, 0:8 * NUNIT].rearrange("p (k o) -> p k o", o=1)),
            reads=("idxbig",), writes=("idxscr",), ch="idxp")
    op("ve", lambda e: e.memset(BASEE[:], 0.0), writes=("basee",))
    op("ve", lambda e: e.memset(META[:], 0.0), writes=("meta0", "meta1"))

    for jc in range(JCN):
        op("pe", lambda e, jc=jc: e.transpose(
            pG[0:128, 0:NE],
            WF[0:NE, F2_B1N + jc * 128:F2_B1N + (jc + 1) * 128],
            idf[0:NE, 0:NE]),
            reads=_wfg(F2_B1N, F2_B1N + HF) + ("idf",), writes=("pG:0",))
        op("ve", lambda e, jc=jc: e.tensor_copy(
            b1T[:, jc * 8:(jc + 1) * 8], pG[0:128, 0:NE]),
            reads=("pG:0",), writes=("b1T",))
    for jc in range(JCN):
        op("pe", lambda e, jc=jc: e.transpose(
            pG[0:128, 8:9],
            WF[0:1, F2_BS1N + jc * 128:F2_BS1N + (jc + 1) * 128],
            idf[0:1, 0:1]),
            reads=_wfg(F2_BS1N, F2_BS1N + HF) + ("idf",), writes=("pG:0",))
        op("ve", lambda e, jc=jc: e.tensor_copy(
            bs1T[:, jc:jc + 1], pG[0:128, 8:9]), reads=("pG:0",), writes=("bs1T",))

    def load_nat(dram, base, ch):
        op("gp", lambda e, dram=dram, base=base: e.dma_start(
            AR[:, base:base + 8192].rearrange("p (ic d) -> p ic d", d=D),
            dram.rearrange("(ic p) d -> p ic d", p=128)),
            writes=_arg(base, base + 8192), ch=ch)

    def transpose_weight(natbase, dstbase):
        for dc in range(DC):
            fns = []
            for ic in range(DC):
                fns.append(lambda e, dc=dc, ic=ic: e.transpose(
                    pbf[:, ic * 128:(ic + 1) * 128],
                    AR[:, natbase + ic * 1024 + dc * 128:
                       natbase + ic * 1024 + (dc + 1) * 128],
                    idbf[:]))
            grp("pe", fns, reads=_arg(natbase, natbase + 8192) + ("idbf",),
                writes=("pbf",))
            op("ac", lambda e, dc=dc: e.activation(
                AR[:, dstbase + dc * 1024:dstbase + (dc + 1) * 1024], pbf[:],
                AF.Copy), reads=("pbf",),
                writes=_arg(dstbase + dc * 1024, dstbase + (dc + 1) * 1024))

    op("ve", lambda e: e.memset(ASM[:], 0.0),
        writes=tuple(f"asm:{s}" for s in range(nseq)))
    load_nat(Wq, A_NAT1, "natq")
    load_nat(Wk, A_NAT2, "natk")
    transpose_weight(A_NAT1, A_WQT)
    transpose_weight(A_NAT2, A_WKT)

    w1items = []
    for e_ in range(NE):
        w1items.append((lambda e, e_=e_: e.dma_start(
            AR[:, A_W1 + e_ * 4096:A_W1 + (e_ + 1) * 4096].rearrange(
                "p (dc j) -> p dc j", j=HF),
            W1[e_].rearrange("(dc p) j -> p dc j", p=128)), (),
            _arg(A_W1 + e_ * 4096, A_W1 + (e_ + 1) * 4096)))
    w1items.append((lambda e: e.dma_start(
        AR[:, A_WS1:A_WS1 + 4096].rearrange("p (dc j) -> p dc j", j=HF),
        Ws1.rearrange("(dc p) j -> p dc j", p=128)), (),
        _arg(A_WS1, A_WS1 + 4096)))
    w1items.append((lambda e: e.dma_start(
        AR[:, A_WS2:A_WS2 + 4096].rearrange("p (jc d) -> p jc d", d=D),
        Ws2.rearrange("(jc p) d -> p jc d", p=128)), (),
        _arg(A_WS2, A_WS2 + 4096)))
    em.dma_series("gp", "w1s", w1items)

    if phases < 1:
        _finish(em)
        return _realize(nc, em)
    def stat(p3, k):
        return STT[:, p3 * 8 + k:p3 * 8 + k + 1]

    def lnpart(s, u):
        p3 = u % 3
        t0 = u * 128
        xg = _wfg(F_X1 + p3 * 1024, F_X1 + (p3 + 1) * 1024)
        xap = WF[:, F_X1 + p3 * 1024:F_X1 + (p3 + 1) * 1024]
        stn = f"st:{p3}"
        op("sy", lambda e: e.dma_start(
            xap, x_in[s, t0:t0 + 128, :]), writes=xg, ch=f"xl{p3}")
        op("ve", lambda e: e.reduce_sum(
            stat(p3, 0), xap, axis=AX), reads=xg, writes=(stn,))
        op("ac", lambda e: e.activation(
            WF[:, F_SC2:F_SC2 + 1024], xap, AF.Square,
            accum_out=stat(p3, 1)),
            reads=xg, writes=_wfg(F_SC2, F_SC2 + 1024) + (stn,))
        op("ve", lambda e: e.tensor_scalar(
            stat(p3, 2), stat(p3, 0), 1.0 / D, None, ALU.mult),
            reads=(stn,), writes=(stn,))
        op("ve", lambda e: e.tensor_scalar(
            stat(p3, 3), stat(p3, 1), 1.0 / D, None, ALU.mult),
            reads=(stn,), writes=(stn,))
        op("ve", lambda e: e.scalar_tensor_tensor(
            stat(p3, 4), stat(p3, 2), -1.0, stat(p3, 2),
            ALU.mult, ALU.mult), reads=(stn,), writes=(stn,))
        op("ve", lambda e: e.tensor_add(
            stat(p3, 3), stat(p3, 3), stat(p3, 4)),
            reads=(stn,), writes=(stn,))
        op("ac", lambda e: e.activation(
            stat(p3, 5), stat(p3, 3), AF.Ln, bias=np.float32(EPS)),
            reads=(stn,), writes=(stn,))
        op("ac", lambda e: e.activation(
            stat(p3, 6), stat(p3, 5), AF.Exp, scale=np.float32(-0.5)),
            reads=(stn,), writes=(stn,))
        op("ve", lambda e: e.tensor_scalar(
            WH[:, H_XNTM + p3 * 1024:H_XNTM + (p3 + 1) * 1024],
            xap, stat(p3, 2), stat(p3, 6), ALU.subtract, ALU.mult),
            reads=xg + (stn,),
            writes=_whg(H_XNTM + p3 * 1024, H_XNTM + (p3 + 1) * 1024))

    def txpart(s, u):
        p3 = u % 3
        t0 = u * 128
        fns = []
        for c in range(DC):
            fns.append(lambda e, c=c, p3=p3: e.transpose(
                pbf[:, c * 128:(c + 1) * 128],
                WH[:, H_XNTM + p3 * 1024 + c * 128:
                   H_XNTM + p3 * 1024 + (c + 1) * 128], idbf[:]))
        grp("pe", fns,
            reads=_whg(H_XNTM + p3 * 1024, H_XNTM + (p3 + 1) * 1024) + ("idbf",),
            writes=("pbf",))
        xnfg = _whg(H_XNF + p3 * 1024, H_XNF + (p3 + 1) * 1024)
        op("ac", lambda e: e.activation(
            WH[:, H_XNF + p3 * 1024:H_XNF + (p3 + 1) * 1024], pbf[:],
            AF.Copy), reads=("pbf",), writes=xnfg)
        op("gp", lambda e: e.dma_start(
            xn_scr[s, :, :, t0:t0 + 128].rearrange("c p t -> p c t"),
            WH[:, H_XNF + p3 * 1024:H_XNF + (p3 + 1) * 1024].rearrange(
                "p (c t) -> p c t", t=128)),
            reads=xnfg, writes=(f"xnscr:{s}:{u}",), ch=f"xnst{p3}")
        op("gp", lambda e: e.dma_start(
            xm_scr[s * S + t0:s * S + t0 + 128, 0:512],
            WH[:, H_XNTM + p3 * 1024:H_XNTM + (p3 + 1) * 1024].bitcast(F32)),
            reads=_whg(H_XNTM + p3 * 1024, H_XNTM + (p3 + 1) * 1024),
            writes=(f"xmscr:{s}:{u}",), ch=f"xmst{u % 2}")

    def pepart(s, u):
        p3 = u % 3
        t0 = u * 128
        xnfg = _whg(H_XNF + p3 * 1024, H_XNF + (p3 + 1) * 1024)
        xnf0 = H_XNF + p3 * 1024
        fns = []
        for h in range(2):
            for c in range(DC):
                fns.append(lambda e, h=h, c=c: e.matmul(
                    pY1[:, h * 512:(h + 1) * 512],
                    WH[:, xnf0 + c * 128:xnf0 + (c + 1) * 128],
                    AR[:, A_WQT + c * 1024 + h * 512:
                       A_WQT + c * 1024 + (h + 1) * 512],
                    start=(c == 0), stop=(c == DC - 1)))
        grp("pe", fns, reads=xnfg + _arg(A_WQT, A_WQT + 8192), writes=("pY1",))
        qg = _whg(H_Q, H_Q + 1024)
        op("ac", lambda e: e.activation(
            WH[:, H_Q:H_Q + 1024], pY1[:],
            AF.Copy, scale=np.float32(0.125)), reads=("pY1",), writes=qg)
        kg = _whg(H_K, H_K + 1024)
        for h, pp, pname in ((0, pfa, "pfa"), (1, pfb, "pfb")):
            fns = []
            for c in range(DC):
                fns.append(lambda e, h=h, c=c, pp=pp: e.matmul(
                    pp[:],
                    WH[:, xnf0 + c * 128:xnf0 + (c + 1) * 128],
                    AR[:, A_WKT + c * 1024 + h * 512:
                       A_WKT + c * 1024 + (h + 1) * 512],
                    start=(c == 0), stop=(c == DC - 1)))
            grp("pe", fns, reads=xnfg + _arg(A_WKT, A_WKT + 8192),
                writes=(pname,))
            op("ac", lambda e, h=h, pp=pp: e.activation(
                WH[:, H_K + h * 512:H_K + (h + 1) * 512], pp[:], AF.Copy),
                reads=(pname,), writes=kg)
    def pepart_a(s, u):
        qg = _whg(H_Q, H_Q + 1024)
        kg = _whg(H_K, H_K + 1024)
        fns = []
        for p in range(NPAIR):
            fns.append(lambda e, p=p: e.matmul(
                pY0[:, p * 128:(p + 1) * 128],
                WH[:, H_K + p * 128:H_K + (p + 1) * 128],
                WH[:, H_Q + p * 128:H_Q + (p + 1) * 128],
                start=True, stop=True, skip_group_check=True))
        grp("pe", fns, reads=kg + qg, writes=("pY0",))
        if u == 0:
            op("ve", lambda e: e.tensor_copy(ASB[:], pY0[:]),
                reads=("pY0",), writes=("asb",))
        else:
            op("ve", lambda e: e.tensor_add(ASB[:], ASB[:], pY0[:]),
                reads=("pY0", "asb"), writes=("asb",))
    def gatemm(s, u):
        t0 = u * 128
        p3 = u % 3
        xnfg = _whg(H_XNF + p3 * 1024, H_XNF + (p3 + 1) * 1024)
        xnf0 = H_XNF + p3 * 1024
        fns = []
        for c in range(DC):
            fns.append(lambda e, c=c: e.matmul(
                pG[0:NE, 0:128], gwsb[:, c * 8:(c + 1) * 8],
                WH[:, xnf0 + c * 128:xnf0 + (c + 1) * 128],
                start=(c == 0), stop=(c == DC - 1)))
        grp("pe", fns, reads=xnfg + ("gwsb",), writes=("pG:0",))
        op("ac", lambda e: e.activation(
            b2all[0:NE, 0:128], pG[0:NE, 0:128], AF.Exp, scale=np.float32(-1.0)),
            reads=("pG:0",), writes=("sig1",))
        op("ve", lambda e: e.tensor_scalar(
            b2all[0:NE, 128:256], b2all[0:NE, 0:128], 1.0, None, ALU.add),
            reads=("sig1",), writes=("sig2",))
        op("ve", lambda e: e.reciprocal(
            b2all[0:NE, 256:384], b2all[0:NE, 128:256]),
            reads=("sig2",), writes=("sig3",))
        op("ve", lambda e: e.tensor_scalar(
            WF[0:NE, F_SCORES + t0:F_SCORES + t0 + 128],
            b2all[0:NE, 256:384], rscol[:], None, ALU.mult),
            reads=("sig3", "rscol"),
            writes=_wfg(F_SCORES + t0, F_SCORES + t0 + 128))
    def gatepart(s, u):
        t0 = u * 128
        scg = _wfg(F_SCORES + t0, F_SCORES + t0 + 128)
        mpar = u % 2
        mslc = lambda a, b: META[:, mpar * 8 + a:mpar * 8 + b]
        tb = lambda a, b: T2B[:, a:b]
        op("pe", lambda e: e.transpose(
            pfa[:, 0:8], WF[0:NE, F_SCORES + t0:F_SCORES + t0 + 128],
            idf[0:NE, 0:NE]), reads=scg + ("idf",), writes=("pfa",))
        op("ve", lambda e: e.tensor_copy(tb(0, 8), pfa[:, 0:8]),
            reads=("pfa",), writes=("t2raw",))
        op("ve", lambda e: e.reduce_max(tb(32, 33), tb(0, 8), axis=AX),
            reads=("t2raw",), writes=("t2m",))
        op("ve", lambda e: e.tensor_scalar(
            tb(8, 16), tb(0, 8), tb(32, 33), None, ALU.is_equal),
            reads=("t2raw", "t2m"), writes=("t2eq1",))
        op("ve", lambda e: e.tensor_tensor_scan(
            tb(16, 24), tb(8, 16), tb(8, 16), 0.0, ALU.add, ALU.bypass),
            reads=("t2eq1",), writes=("t2msk",))
        op("ve", lambda e: e.tensor_scalar(
            tb(16, 24), tb(16, 24), 1.0, None, ALU.is_equal),
            reads=("t2msk",), writes=("t2msk",))
        op("ve", lambda e: e.tensor_mul(tb(8, 16), tb(8, 16), tb(16, 24)),
            reads=("t2eq1", "t2msk"), writes=("t2eq1",))
        op("ve", lambda e: e.scalar_tensor_tensor(
            tb(16, 24), tb(8, 16), -BIG, tb(0, 8), ALU.mult, ALU.add),
            reads=("t2eq1", "t2raw"), writes=("t2msk",))
        op("ve", lambda e: e.reduce_max(tb(33, 34), tb(16, 24), axis=AX),
            reads=("t2msk",), writes=("t2m",))
        op("ve", lambda e: e.tensor_scalar(
            tb(24, 32), tb(16, 24), tb(33, 34), None, ALU.is_equal),
            reads=("t2msk", "t2m"), writes=("t2eq2",))
        op("ve", lambda e: e.tensor_tensor_scan(
            tb(16, 24), tb(24, 32), tb(24, 32), 0.0, ALU.add, ALU.bypass),
            reads=("t2eq2",), writes=("t2msk",))
        op("ve", lambda e: e.tensor_scalar(
            tb(16, 24), tb(16, 24), 1.0, None, ALU.is_equal),
            reads=("t2msk",), writes=("t2msk",))
        op("ve", lambda e: e.tensor_mul(tb(24, 32), tb(24, 32), tb(16, 24)),
            reads=("t2eq2", "t2msk"), writes=("t2eq2",))
        op("ve", lambda e: e.tensor_add(tb(34, 35), tb(32, 33), tb(33, 34)),
            reads=("t2m",), writes=("t2den",))
        op("ve", lambda e: e.reciprocal(tb(35, 36), tb(34, 35)),
            reads=("t2den",), writes=("t2rde",))
        op("ve", lambda e: e.tensor_mul(mslc(0, 1), tb(32, 33), tb(35, 36)),
            reads=("t2m", "t2rde"), writes=(f"meta{mpar}",))
        op("ve", lambda e: e.tensor_mul(mslc(1, 2), tb(33, 34), tb(35, 36)),
            reads=("t2m", "t2rde"), writes=(f"meta{mpar}",))
        op("ve", lambda e: e.tensor_mul(tb(16, 24), tb(8, 16), iote[:]),
            reads=("t2eq1", "iote"), writes=("t2msk",))
        op("ve", lambda e: e.reduce_sum(mslc(2, 3), tb(16, 24), axis=AX),
            reads=("t2msk",), writes=(f"meta{mpar}",))
        op("ve", lambda e: e.tensor_mul(tb(16, 24), tb(24, 32), iote[:]),
            reads=("t2eq2", "iote"), writes=("t2msk",))
        op("ve", lambda e: e.reduce_sum(mslc(3, 4), tb(16, 24), axis=AX),
            reads=("t2msk",), writes=(f"meta{mpar}",))
        op("ve", lambda e: e.tensor_scalar(
            mslc(4, 5), iota128[:], float(s * S + t0), None, ALU.add),
            reads=("iota128",), writes=(f"meta{mpar}",))
        op("ve", lambda e: e.tensor_copy(EQS[:, u * 16:u * 16 + 8], tb(8, 16)),
            reads=("t2eq1",), writes=("eqs",))
        op("ve", lambda e: e.tensor_copy(EQS[:, u * 16 + 8:u * 16 + 16], tb(24, 32)),
            reads=("t2eq2",), writes=("eqs",))
        op("ve", lambda e: e.tensor_scalar(
            tb(0, 8), tb(8, 16), mslc(0, 1), None, ALU.mult),
            reads=("t2eq1", f"meta{mpar}"), writes=("t2raw",))
        op("ve", lambda e: e.tensor_scalar(
            tb(16, 24), tb(24, 32), mslc(1, 2), None, ALU.mult),
            reads=("t2eq2", f"meta{mpar}"), writes=("t2msk",))
        op("ve", lambda e: e.tensor_add(tb(0, 8), tb(0, 8), tb(16, 24)),
            reads=("t2raw", "t2msk"), writes=("t2raw",))
        op("gp", lambda e: e.dma_start(
            xm_scr[s * S + t0:s * S + t0 + 128, 512:520],
            META[:, mpar * 8:mpar * 8 + 8]),
            reads=(f"meta{mpar}",), writes=(f"xmscr2:{s}:{u}",),
            ch=f"xmm{mpar}")

    def gatepart_b(s, u):
        t0 = u * 128
        tb = lambda a, b: T2B[:, a:b]
        op("pe", lambda e: e.transpose(
            pG[0:NE, 128:256], tb(0, 8), idf[:]),
            reads=("t2raw", "idf", "pG:0"), writes=("pG:1",))
        gfg = _wfg(F_GFM + t0, F_GFM + t0 + 128)
        op("ac", lambda e: e.activation(
            WF[0:NE, F_GFM + t0:F_GFM + t0 + 128], pG[0:NE, 128:256],
            AF.Copy), reads=("pG:1", "pG:0"), writes=gfg)

    def idxpart(s, u):
        t0 = u * 128
        gfall = _wfg(F_GFM, F_GFM + S)
        mp = u % 2
        tki0 = 4 * (mp + 1)
        op("pe", lambda e, t0=t0: e.transpose(
            pY1[:, 0:8], WF[0:NE, F_GFM + t0:F_GFM + t0 + 128],
            idf[0:NE, 0:NE]), reads=gfall + ("idf",), writes=("pY1",))
        op("ac", lambda e: e.activation(T2B[:, 36:44], pY1[:, 0:8], AF.Copy),
            reads=("pY1",), writes=("eppos",))
        op("ve", lambda e: e.tensor_scalar(
            T2B[:, 44:52], T2B[:, 36:44], float(CAP) + 0.5, None, ALU.is_ge),
            reads=("eppos",), writes=("epge",))
        op("ve", lambda e: e.tensor_add(
            T2B[:, 52:60], T2B[:, 36:44], ecap[:]),
            reads=("eppos", "ecap"), writes=("epmsk",))
        op("ve", lambda e: e.scalar_tensor_tensor(
            T2B[:, 52:60], T2B[:, 44:52], float(1 << 30), T2B[:, 52:60],
            ALU.mult, ALU.add), reads=("epge", "epmsk"), writes=("epmsk",))
        op("ve", lambda e, s=s, t0=t0: e.tensor_scalar(
            T2B[:, 68:69], iota128[:], float(s * S + t0), None, ALU.add),
            reads=("iota128",), writes=("eptk",))
        op("ve", lambda e, tki0=tki0: e.tensor_copy(
            TKI[:, tki0:tki0 + 1], T2B[:, 68:69]),
            reads=("eptk",), writes=(f"eptok{mp}",))
        KTILES = 8 * CAP // 128
        for sl in range(2):
            eqap = EQS[:, u * 16 + sl * 8:u * 16 + sl * 8 + 8]
            op("ve", lambda e, eqap=eqap: e.tensor_mul(
                T2B[:, 60:68], eqap, T2B[:, 52:60]),
                reads=("eqs", "epmsk"), writes=("epeq",))
            op("ve", lambda e: e.reduce_sum(
                T2B[:, 69:70], T2B[:, 60:68], axis=AX),
                reads=("epeq",), writes=("epd",))
            op("ve", lambda e: e.tensor_scalar(
                T2B[:, 70:71], T2B[:, 69:70], 1.0 / 128, None, ALU.mult),
                reads=("epd",), writes=("epk",))
            op("ve", lambda e, tki0=tki0: e.tensor_copy(
                TKI[:, tki0 + 3:tki0 + 4], T2B[:, 70:71]),
                reads=("epk",), writes=(f"epki{mp}",))
            op("ve", lambda e, tki0=tki0: e.tensor_copy(
                T2B[:, 70:71], TKI[:, tki0 + 3:tki0 + 4]),
                reads=(f"epki{mp}",), writes=("epk",))
            op("ve", lambda e: e.scalar_tensor_tensor(
                T2B[:, 71:72], T2B[:, 70:71], -128.0, T2B[:, 69:70],
                ALU.mult, ALU.add), reads=("epk", "epd"), writes=("epr",))
            op("ve", lambda e: e.tensor_scalar(
                T2B[:, 68:69], T2B[:, 71:72], 0.0, None, ALU.is_lt),
                reads=("epr",), writes=("eptk",))
            op("ve", lambda e: e.tensor_sub(
                T2B[:, 70:71], T2B[:, 70:71], T2B[:, 68:69]),
                reads=("epk", "eptk"), writes=("epk",))
            op("ve", lambda e: e.scalar_tensor_tensor(
                T2B[:, 71:72], T2B[:, 68:69], 128.0, T2B[:, 71:72],
                ALU.mult, ALU.add), reads=("eptk", "epr"), writes=("epr",))
            op("ve", lambda e, KTILES=KTILES: e.scalar_tensor_tensor(
                T2B[:, 69:70], T2B[:, 71:72], float(KTILES), T2B[:, 70:71],
                ALU.mult, ALU.add), reads=("epr", "epk"), writes=("epd",))
            op("ve", lambda e, sl=sl, tki0=tki0: e.tensor_copy(
                TKI[:, tki0 + 1 + sl:tki0 + 2 + sl], T2B[:, 69:70]),
                reads=("epd",), writes=(f"epd{sl}{mp}",))
            op("gp", lambda e, sl=sl, tki0=tki0: e.indirect_dma_start(
                idx_scr, bass.IndirectOffsetOnAxis(
                    TKI[:, tki0 + 1 + sl:tki0 + 2 + sl], 0),
                TKI[:, tki0:tki0 + 1], None, bounds_check=rb_idx,
                oob_is_err=False),
                reads=(f"eptok{mp}", f"epd{sl}{mp}", "idxscr"),
                writes=("idxscr",),
                ch=f"ixs{sl}")

    for s in range(nseq):
        lnpart(s, 0)
        if NSUB > 1:
            lnpart(s, 1)
        txpart(s, 0)
        for u in range(NSUB):
            if u >= 3:
                gatepart(s, u - 3)
            if u + 2 < NSUB:
                lnpart(s, u + 2)
            if u + 1 < NSUB:
                txpart(s, u + 1)
            if s >= 1 and u == 0:
                idxpart(s - 1, 0)
            pepart(s, u)
            if u >= 3:
                gatepart_b(s, u - 3)
            gatemm(s, u)
            pepart_a(s, u)
            if s >= 1 and u + 1 < NSUB:
                idxpart(s - 1, u + 1)
        for u in range(max(0, NSUB - 3), NSUB):
            gatepart(s, u)
            gatepart_b(s, u)
        op("ve", lambda e: e.reduce_max(
            SMX[:, 0:16], ASB[:].rearrange("p (a q) -> p a q", q=64), axis=AX),
            reads=("asb",), writes=("smx_m",))
        op("ve", lambda e: e.tensor_scalar(
            SMX[:, 16:32], SMX[:, 0:16], -1.0, None, ALU.mult),
            reads=("smx_m",), writes=("smx_n",))
        op("ve", lambda e: e.memset(SMX[:, 32:48], 1.0), writes=("smx_s",))
        for hh in range(NH):
            p, o = hh // 2, hh % 2
            r0 = 64 * o
            a = 2 * p + o
            op("ac", lambda e, s=s, p=p, r0=r0, a=a, o=o: e.activation(
                ASM[r0:r0 + 64, s * 1024 + p * 128 + o * 64:
                    s * 1024 + p * 128 + o * 64 + 64],
                ASB[r0:r0 + 64, p * 128 + o * 64:p * 128 + o * 64 + 64],
                AF.Exp, bias=SMX[r0:r0 + 64, 16 + a:16 + a + 1],
                accum_out=SMX[r0:r0 + 64, 32 + a:32 + a + 1]),
                reads=("asb", "smx_n"),
                writes=(f"asm:{s}", "smx_s"))
        op("ve", lambda e, s=s: e.reciprocal(
            RSM[0:64, s * 8:s * 8 + 8],
            SMX[0:64, 32:48].rearrange("p (a two) -> p a two", two=2)[:, :, 0]),
            reads=("smx_s",), writes=(f"rsm:{s}",))
        op("ve", lambda e, s=s: e.reciprocal(
            RSM[64:128, s * 8:s * 8 + 8],
            SMX[64:128, 32:48].rearrange("p (a two) -> p a two", two=2)[:, :, 1]),
            reads=("smx_s",), writes=(f"rsm:{s}",))

        gfall = _wfg(F_GFM, F_GFM + S)
        op("gp", lambda e, s=s: e.dma_start(
            gate_scr[s], WF[0:NE, F_GFM:F_GFM + S]),
            reads=gfall, writes=(f"gscr:{s}",), ch="gst")
        NHALF = (S + 1023) // 1024
        for hh in range(NHALF):
            hc = hh * 1024
            hw = min(1024, S - hc)
            op("ve", lambda e, hc=hc, hw=hw: e.tensor_scalar(
                WF[0:NE, F_SC2:F_SC2 + hw],
                WF[0:NE, F_GFM + hc:F_GFM + hc + hw], 0.0, None, ALU.is_gt),
                reads=gfall, writes=_wfg(F_SC2, F_SC2 + 1024))
            if hh == 0:
                ini = 0.0
            else:
                ini = WF[0:NE, F_GFM + hc - 1:F_GFM + hc]
            op("ve", lambda e, hc=hc, hw=hw, ini=ini: e.tensor_tensor_scan(
                WF[0:NE, F_GFM + hc:F_GFM + hc + hw],
                WF[0:NE, F_SC2:F_SC2 + hw], WF[0:NE, F_SC2:F_SC2 + hw],
                ini, ALU.add, ALU.bypass),
                reads=_wfg(F_SC2, F_SC2 + 1024) + gfall, writes=gfall)
        op("ve", lambda e: e.tensor_scalar(
            WF[0:NE, F_GFM:F_GFM + S], WF[0:NE, F_GFM:F_GFM + S],
            BASEE[:, 0:1], None, ALU.add),
            reads=gfall + ("basee",), writes=gfall)
        op("ve", lambda e: e.tensor_copy(
            BASEE[:, 0:1], WF[0:NE, F_GFM + S - 1:F_GFM + S]),
            reads=gfall, writes=("basee",))
    for u in range(NSUB):
        idxpart(nseq - 1, u)

    if phases < 2:
        _finish(em)
        return _realize(nc, em)
    for sl in range(2):
        ch = f"ixs{sl}"
        if ch in em.cnt:
            em.wtok[f"idxfin{sl}"] = (ch, em.cnt[ch])
    op("sy", lambda e: e.dma_start(
        IDXS[:], idx_scr.rearrange("(p k) o -> p (k o)", p=128)),
        reads=("idxscr", "idxfin0", "idxfin1", "idxbig"),
        writes=("idxall",), ch="ixall")
    for i_, ch in enumerate(("xmst0", "xmst1", "xmm0", "xmm1")):
        if ch in em.cnt:
            em.retag([k for k in em.wtok if k.startswith("xmscr")
                      and em.wtok[k][0] == ch], (ch, em.cnt[ch]))
            em.wtok[f"xmfin{i_}"] = (ch, em.cnt[ch])
    for ch in ("xnst0", "xnst1", "xnst2"):
        if ch in self_cnt(em):
            tokf = (ch, em.cnt[ch])
            em.retag([k for k in em.wtok if k.startswith("xnscr:")
                      and em.wtok[k][0] == ch], tokf)
    if "gst" in em.cnt:
        em.retag([k for k in em.wtok if k.startswith("gscr:")],
                 ("gst", em.cnt["gst"]))
    w2items = []
    for e_ in range(NE):
        w2items.append((lambda e, e_=e_: e.dma_start(
            AR[:, A_W2 + e_ * 4096:A_W2 + (e_ + 1) * 4096].rearrange(
                "p (jc d) -> p jc d", d=D),
            W2[e_].rearrange("(jc p) d -> p jc d", p=128)), (),
            _arg(A_W2 + e_ * 4096, A_W2 + (e_ + 1) * 4096)))
    em.dma_series("gp", "w2s", w2items)
    em.dma_series("sy", "b2l", [
        (lambda e: e.dma_start(b2all[0:NE, :], b2), (),
         ("b2sb", "sig1", "sig2", "sig3")),
        (lambda e: e.dma_start(b2all[8:9, :], bs2.rearrange("(o d) -> o d", o=1)),
         (), ("bs2sb",)),
    ])

    op("ve", lambda e: e.memset(GT9[0:9, :], 1.0),
        writes=("gt9ones", "gt9:0", "gt9:1"))

    OOBF = float(1 << 30)
    NGU = NE * NUNIT

    def xg_col(gidx):
        return (F2_XG0, 1024, 2048, F2_XG1)[gidx % 4]

    for slot_ in range(4):
        xgc_ = (F2_XG0, 1024, 2048, F2_XG1)[slot_]
        op("ve", lambda e, xgc_=xgc_: e.memset(WF[:, xgc_:xgc_ + 520], 0.0),
            writes=(f"xg:{slot_}", f"wf:{xgc_ // 1024}"))

    def emit_gather(gidx):
        xgc = xg_col(gidx)
        xgg = (f"xg:{gidx % 4}", f"wf:{xgc // 1024}")
        op("gp", lambda e, xgc=xgc, gidx=gidx: e.indirect_dma_start(
            WF[:, xgc:xgc + 520], None, xm_scr,
            bass.IndirectOffsetOnAxis(IDXS[:, gidx:gidx + 1], 0),
            bounds_check=rb_tok, oob_is_err=False),
            reads=("idxall", "xmfin0", "xmfin1", "xmfin2", "xmfin3"),
            writes=xgg, ch=f"xgl{gidx % 4}")

    def emit_tx(guni):
        xngb = H2_XNF + (guni % 2) * 2048
        xng = _whg(xngb, xngb + 2048)
        for sub in range(2):
            gidx = guni * 2 + sub
            xgc = xg_col(gidx)
            xgg = (f"xg:{gidx % 4}", f"wf:{xgc // 1024}")
            fns = []
            for c in range(DC):
                fns.append(lambda e, c=c, xgc=xgc: e.transpose(
                    pbf[:, c * 128:(c + 1) * 128],
                    WF[:, xgc:xgc + 512].bitcast(BF16)[:, c * 128:(c + 1) * 128],
                    idbf[:]))
            grp("pe", fns, reads=xgg + ("idbf",), writes=("pbf",))
            op("ac", lambda e, sub=sub, xngb=xngb: e.activation(
                WH[:, xngb:xngb + 2048].rearrange(
                    "p (c two t) -> p c two t", two=2, t=128)[:, :, sub, :],
                pbf[:].rearrange("p (c t) -> p c t", t=128), AF.Copy),
                reads=("pbf",), writes=xng)

    def emit_w1(guni):
        e_x = guni // NUNIT
        xngb = H2_XNF + (guni % 2) * 2048
        xng = _whg(xngb, xngb + 2048)
        upar = guni % 2
        hsg = _whg(H2_HSC + upar * 1024, H2_HSC + (upar + 1) * 1024)
        for half, pp, pname in ((0, pfa, "pfa"), (1, pfb, "pfb")):
            fns = []
            for jo in range(2):
                jc = half * 2 + jo
                for c in range(DC):
                    a0 = A_W1 + e_x * 4096 + c * 512 + jc * 128
                    fns.append(lambda e, jo=jo, c=c, a0=a0, pp=pp, xngb=xngb:
                        e.matmul(
                            pp[:, jo * 256:(jo + 1) * 256],
                            AR[:, a0:a0 + 128],
                            WH[:, xngb + c * 256:xngb + (c + 1) * 256],
                            start=(c == 0), stop=(c == DC - 1)))
            grp("pe", fns,
                reads=xng + _arg(A_W1 + e_x * 4096, A_W1 + (e_x + 1) * 4096),
                writes=(pname,))
            for jo in range(2):
                jc = half * 2 + jo
                op("ac", lambda e, jc=jc, jo=jo, pp=pp, upar=upar, e_x=e_x:
                    e.activation(
                        WH[:, H2_HSC + upar * 1024 + jc * 256:
                           H2_HSC + upar * 1024 + (jc + 1) * 256],
                        pp[:, jo * 256:(jo + 1) * 256], AF.Gelu,
                        bias=b1T[:, jc * 8 + e_x:jc * 8 + e_x + 1]),
                    reads=(pname, "b1T"), writes=hsg)

    def emit_w2(guni):
        e_x = guni // NUNIT
        upar = guni % 2
        hsg = _whg(H2_HSC + upar * 1024, H2_HSC + (upar + 1) * 1024)
        for sub in range(2):
            gidx = guni * 2 + sub
            xgc = xg_col(gidx)
            xgg = (f"xg:{gidx % 4}", f"wf:{xgc // 1024}")
            pY, pYn = (pY0, "pY0") if sub == 0 else (pY1, "pY1")
            fns = []
            for h in range(2):
                for jc in range(JCN):
                    a0 = A_W2 + e_x * 4096 + jc * 1024 + h * 512
                    fns.append(lambda e, jc=jc, h=h, a0=a0, pY=pY,
                               upar=upar, sub=sub: e.matmul(
                        pY[:, h * 512:(h + 1) * 512],
                        WH[:, H2_HSC + upar * 1024 + jc * 256 + sub * 128:
                           H2_HSC + upar * 1024 + jc * 256 + sub * 128 + 128],
                        AR[:, a0:a0 + 512],
                        start=(jc == 0), stop=(jc == JCN - 1)))
            grp("pe", fns,
                reads=hsg + _arg(A_W2 + e_x * 4096, A_W2 + (e_x + 1) * 4096),
                writes=(pYn,))
            op("ve", lambda e, xgc=xgc, e_x=e_x: e.tensor_scalar(
                T2B[:, 0:1], WF[:, xgc + 514:xgc + 515], float(e_x), None,
                ALU.is_equal), reads=xgg, writes=("t2raw",))
            op("ve", lambda e, xgc=xgc, e_x=e_x: e.tensor_scalar(
                T2B[:, 1:2], WF[:, xgc + 515:xgc + 516], float(e_x), None,
                ALU.is_equal), reads=xgg, writes=("t2raw",))
            op("ve", lambda e, xgc=xgc: e.tensor_mul(
                T2B[:, 2:3], WF[:, xgc + 512:xgc + 513], T2B[:, 0:1]),
                reads=xgg + ("t2raw",), writes=("t2m",))
            op("ve", lambda e, xgc=xgc: e.tensor_mul(
                T2B[:, 3:4], WF[:, xgc + 513:xgc + 514], T2B[:, 1:2]),
                reads=xgg + ("t2raw",), writes=("t2m",))
            op("ve", lambda e: e.tensor_add(
                T2B[:, 2:3], T2B[:, 2:3], T2B[:, 3:4]),
                reads=("t2m",), writes=("t2m",))
            op("ve", lambda e, xgc=xgc: e.scalar_tensor_tensor(
                T2B[:, 4:5], T2B[:, 1:2], float(NTOK),
                WF[:, xgc + 516:xgc + 517], ALU.mult, ALU.add),
                reads=("t2raw",) + xgg, writes=("t2d",))
            op("ve", lambda e, gidx=gidx: e.tensor_copy(
                T2B[:, 5:6], IDXS[:, gidx:gidx + 1]),
                reads=("idxall",), writes=("t2v",))
            op("ve", lambda e: e.tensor_scalar(
                T2B[:, 6:7], T2B[:, 5:6], float(NTOK), None, ALU.is_lt),
                reads=("t2v",), writes=("t2v",))
            op("ve", lambda e: e.tensor_scalar(
                T2B[:, 4:5], T2B[:, 4:5], float(2 * NTOK), None, ALU.subtract),
                reads=("t2d",), writes=("t2d",))
            op("ve", lambda e: e.tensor_mul(
                T2B[:, 4:5], T2B[:, 4:5], T2B[:, 6:7]),
                reads=("t2d", "t2v"), writes=("t2d",))
            op("ve", lambda e: e.tensor_scalar(
                T2B[:, 4:5], T2B[:, 4:5], float(2 * NTOK), None, ALU.add),
                reads=("t2d",), writes=("t2d",))
            op("ve", lambda e, gidx=gidx: e.tensor_copy(
                TKI[:, 2 + gidx % 2:3 + gidx % 2], T2B[:, 4:5]),
                reads=("t2d",), writes=(f"ysd{gidx % 2}",))
            ysc = F2_YS + (gidx % 2) * 1024
            ysg = (f"ys:{gidx % 2}", "wf:3" if gidx % 2 == 0 else "wf:4")
            op("ac", lambda e, pY=pY, ysc=ysc: e.activation(
                WF[:, ysc:ysc + 1024], pY[:], AF.Copy,
                scale=T2B[:, 2:3]),
                reads=(pYn, "t2m"), writes=ysg)
            op("gp", lambda e, ysc=ysc, gidx=gidx: e.indirect_dma_start(
                yab_scr, bass.IndirectOffsetOnAxis(
                    TKI[:, 2 + gidx % 2:3 + gidx % 2], 0),
                WF[:, ysc:ysc + 1024], None),
                reads=ysg + (f"ysd{gidx % 2}",), writes=(f"yab{gidx % 2}",),
                ch=f"ysc{gidx % 2}")

    for sub in range(4):
        emit_gather(sub)
    emit_tx(0)
    for guni in range(NGU):
        emit_w1(guni)
        if guni + 1 < NGU:
            emit_tx(guni + 1)
        emit_w2(guni)
        if guni + 2 < NGU:
            for sub in range(2):
                emit_gather((guni + 2) * 2 + sub)

    def sh_load(ti):
        si, c0 = ti // T2PS, (ti % T2PS) * 256
        par = ti % 2
        u0 = c0 // 128
        xn2g = _whg(H2_XNF + par * 2048, H2_XNF + (par + 1) * 2048)
        op("sy", lambda e, si=si, c0=c0, par=par: e.dma_start(
            WH[:, H2_XNF + par * 2048:H2_XNF + (par + 1) * 2048].rearrange(
                "p (c t) -> p c t", t=256),
            xn_scr[si, :, :, c0:c0 + 256].rearrange("c p t -> p c t")),
            reads=(f"xnscr:{si}:{u0}", f"xnscr:{si}:{u0 + 1}"), writes=xn2g,
            ch=f"xn2l{par}")
        op("sy", lambda e, si=si, c0=c0, par=par: e.dma_start(
            GT9[0:NE, par * 256:(par + 1) * 256],
            gate_scr[si, :, c0:c0 + 256]),
            reads=(f"gscr:{si}",), writes=(f"gt9:{par}",), ch=f"gtl{par}")

    sh_load(0)
    for ti in range(NT2):
        if ti + 1 < NT2:
            sh_load(ti + 1)
        si, c0 = ti // T2PS, (ti % T2PS) * 256
        par = ti % 2
        xn2g = _whg(H2_XNF + par * 2048, H2_XNF + (par + 1) * 2048)
        gtg = (f"gt9:{par}",)
        hsg = _whg(H2_HSC + par * 1024, H2_HSC + (par + 1) * 1024)
        for half, pp, pname in ((0, pfa, "pfa"), (1, pfb, "pfb")):
            fns = []
            for jo in range(2):
                jc = half * 2 + jo
                for c in range(DC):
                    a0 = A_WS1 + c * 512 + jc * 128
                    fns.append(lambda e, jo=jo, c=c, a0=a0, pp=pp, par=par:
                        e.matmul(
                            pp[:, jo * 256:(jo + 1) * 256],
                            AR[:, a0:a0 + 128],
                            WH[:, H2_XNF + par * 2048 + c * 256:
                               H2_XNF + par * 2048 + (c + 1) * 256],
                            start=(c == 0), stop=(c == DC - 1)))
            grp("pe", fns, reads=xn2g + _arg(A_WS1, A_WS1 + 4096),
                writes=(pname,))
            for jo in range(2):
                jc = half * 2 + jo
                op("ac", lambda e, jc=jc, jo=jo, pp=pp, par=par: e.activation(
                    WH[:, H2_HSC + par * 1024 + jc * 256:
                       H2_HSC + par * 1024 + (jc + 1) * 256],
                    pp[:, jo * 256:(jo + 1) * 256], AF.Gelu,
                    bias=bs1T[:, jc:jc + 1]),
                    reads=(pname, "bs1T"), writes=hsg)
        for sub, pY, pYn in ((0, pY0, "pY0"), (1, pY1, "pY1")):
            fns = []
            for h in range(2):
                for jc in range(JCN):
                    a0 = A_WS2 + jc * 1024 + h * 512
                    fns.append(lambda e, jc=jc, h=h, a0=a0, pY=pY, par=par,
                               sub=sub: e.matmul(
                        pY[:, h * 512:(h + 1) * 512],
                        WH[:, H2_HSC + par * 1024 + jc * 256 + sub * 128:
                           H2_HSC + par * 1024 + jc * 256 + sub * 128 + 128],
                        AR[:, a0:a0 + 512], start=(jc == 0), stop=False))
                fns.append(lambda e, h=h, pY=pY, par=par, sub=sub: e.matmul(
                    pY[:, h * 512:(h + 1) * 512],
                    GT9[0:9, par * 256 + sub * 128:par * 256 + sub * 128 + 128],
                    b2all[0:9, h * 512:(h + 1) * 512], start=False, stop=True))
            grp("pe", fns,
                reads=hsg + _arg(A_WS2, A_WS2 + 4096) + gtg +
                ("b2sb", "bs2sb", "gt9ones"), writes=(pYn,))
            yc = F2_Y + sub * 1024
            yg = _wfg(yc, yc + 1024)
            op("ac", lambda e, pY=pY, yc=yc: e.activation(
                WF[:, yc:yc + 1024], pY[:], AF.Copy), reads=(pYn,), writes=yg)
            gsub = (si * S + c0) // 128 + sub
            op("sy", lambda e, yc=yc, gsub=gsub: e.dma_start(
                y_scr[gsub], WF[:, yc:yc + 1024]),
                reads=yg, writes=(f"yscr:{gsub}",), ch=f"yst{sub}")

    if phases < 3:
        _finish(em)
        return _realize(nc, em)
    for sub_ in range(2):
        ch = f"yst{sub_}"
        if ch in em.cnt:
            em.retag([k for k in em.wtok if k.startswith("yscr:")
                      and em.wtok[k][0] == ch], (ch, em.cnt[ch]))
    load_nat(Wv, A_NATV, "natq")
    load_nat(Wo, A_NATO, "natk")
    transpose_weight(A_NATV, A_WVT)
    transpose_weight(A_NATO, A_WOT)

    def p3load_xn(ti):
        si, c0 = ti // T3PS, (ti % T3PS) * 256
        par = ti % 2
        u0 = c0 // 128
        xn3g = _whg(H3_XNF + par * 2048, H3_XNF + (par + 1) * 2048)
        op("sy", lambda e, si=si, c0=c0, par=par: e.dma_start(
            WH[:, H3_XNF + par * 2048:H3_XNF + (par + 1) * 2048].rearrange(
                "p (c t) -> p c t", t=256),
            xn_scr[si, :, :, c0:c0 + 256].rearrange("c p t -> p c t")),
            reads=(f"xnscr:{si}:{u0}", f"xnscr:{si}:{u0 + 1}"), writes=xn3g,
            ch=f"xn2l{par}")

    def p3load_sub(ti, sub):
        si, c0 = ti // T3PS, (ti % T3PS) * 256
        gsub = (si * S + c0) // 128 + sub
        spar = gsub % 2
        xg = _wfg(F3_X + spar * 1024, F3_X + (spar + 1) * 1024)
        ygd = _wfg(F3_Y + spar * 1024, F3_Y + (spar + 1) * 1024)
        op("sy", lambda e, gsub=gsub, spar=spar: e.dma_start(
            WF[:, F3_Y + spar * 1024:F3_Y + (spar + 1) * 1024],
            y_scr[gsub]), reads=(f"yscr:{gsub}",), writes=ygd,
            ch=f"yl{spar}")
        op("gp", lambda e, gsub=gsub, spar=spar: e.dma_start(
            WF[:, F3_Y + spar * 1024:F3_Y + (spar + 1) * 1024],
            yab_scr[gsub * 128:gsub * 128 + 128, :],
            accum_op=ALU.add),
            reads=("yab0", "yab1"), writes=ygd, ch=f"yaa{spar}")
        op("gp", lambda e, gsub=gsub, spar=spar: e.dma_start(
            WF[:, F3_Y + spar * 1024:F3_Y + (spar + 1) * 1024],
            yab_scr[NTOK + gsub * 128:NTOK + gsub * 128 + 128, :],
            accum_op=ALU.add),
            reads=("yab0", "yab1"), writes=ygd, ch=f"yab{spar}")
        op("sy", lambda e, si=si, c0=c0, sub=sub, spar=spar: e.dma_start(
            WF[:, F3_X + spar * 1024:F3_X + (spar + 1) * 1024],
            x_in[si, c0 + sub * 128:c0 + sub * 128 + 128, :]), writes=xg,
            ch=f"xl{spar}")

    p3load_xn(0)
    p3load_sub(0, 0)
    p3load_sub(0, 1)
    for ti in range(NT3):
        si, c0 = ti // T3PS, (ti % T3PS) * 256
        par = ti % 2
        xn3g = _whg(H3_XNF + par * 2048, H3_XNF + (par + 1) * 2048)
        if p3cut < 2:
            continue
        vg = _whg(H3_V, H3_V + 2048)
        for ic in range(DC):
            pp, pname = (pfa, "pfa") if ic % 2 == 0 else (pfb, "pfb")
            fns = []
            for c in range(DC):
                fns.append(lambda e, ic=ic, c=c, par=par, pp=pp: e.matmul(
                    pp[:, 0:256],
                    AR[:, A_WVT + c * 1024 + ic * 128:
                       A_WVT + c * 1024 + (ic + 1) * 128],
                    WH[:, H3_XNF + par * 2048 + c * 256:
                       H3_XNF + par * 2048 + (c + 1) * 256],
                    start=(c == 0), stop=(c == DC - 1)))
            grp("pe", fns, reads=xn3g + _arg(A_WVT, A_WVT + 8192),
                writes=(pname,))
            op("ac", lambda e, ic=ic, pp=pp, pname=pname, si=si: e.activation(
                WH[:, H3_V + ic * 256:H3_V + (ic + 1) * 256], pp[:, 0:256],
                AF.Copy, scale=RSM[:, si * 8 + ic:si * 8 + ic + 1]),
                reads=(pname, f"rsm:{si}"), writes=vg)
        if p3cut < 3:
            continue
        atg = _whg(H3_AT, H3_AT + 2048)
        for p in range(NPAIR):
            pp, pname = (pfa, "pfa") if p % 2 == 0 else (pfb, "pfb")
            fns = [lambda e, p=p, si=si, pp=pp: e.matmul(
                pp[:, 0:256],
                ASM[:, si * 1024 + p * 128:si * 1024 + (p + 1) * 128],
                WH[:, H3_V + p * 256:H3_V + (p + 1) * 256],
                start=True, stop=True)]
            grp("pe", fns, reads=(f"asm:{si}",) + vg, writes=(pname,))
            if p3cut == 30:
                continue
            op("ac", lambda e, p=p, pp=pp, pname=pname: e.activation(
                WH[:, H3_AT + p * 256:H3_AT + (p + 1) * 256],
                pp[:, 0:256], AF.Copy),
                reads=(pname,), writes=atg)
        if p3cut < 4 or p3cut == 30:
            continue
        for sub in range(2):
            gsub = (si * S + c0) // 128 + sub
            spar = gsub % 2
            pY, pYn = (pY0, "pY0") if sub == 0 else (pY1, "pY1")
            xg = _wfg(F3_X + spar * 1024, F3_X + (spar + 1) * 1024)
            ygd = _wfg(F3_Y + spar * 1024, F3_Y + (spar + 1) * 1024)
            og = _wfg(F3_O + spar * 1024, F3_O + (spar + 1) * 1024)
            fns = []
            for p in range(NPAIR):
                for h in range(2):
                    fns.append(lambda e, p=p, h=h, sub=sub, pY=pY: e.matmul(
                        pY[:, h * 512:(h + 1) * 512],
                        WH[:, H3_AT + p * 256 + sub * 128:
                           H3_AT + p * 256 + sub * 128 + 128],
                        AR[:, A_WOT + p * 1024 + h * 512:
                           A_WOT + p * 1024 + (h + 1) * 512],
                        start=(p == 0), stop=(p == NPAIR - 1)))
            grp("pe", fns, reads=atg + _arg(A_WOT, A_WOT + 8192), writes=(pYn,))
            if p3cut < 5:
                continue
            op("ac", lambda e, spar=spar: e.activation(
                WF[:, F3_O + spar * 1024:F3_O + (spar + 1) * 1024],
                WF[:, F3_Y + spar * 1024:F3_Y + (spar + 1) * 1024], AF.Silu),
                reads=ygd, writes=og)
            if p3cut < 6:
                continue
            op("ve", lambda e, spar=spar, pY=pY: e.tensor_tensor(
                WF[:, F3_O + spar * 1024:F3_O + (spar + 1) * 1024],
                WF[:, F3_O + spar * 1024:F3_O + (spar + 1) * 1024],
                pY[:], ALU.mult), reads=og + (pYn,), writes=og)
            op("ve", lambda e, spar=spar: e.tensor_tensor(
                WF[:, F3_O + spar * 1024:F3_O + (spar + 1) * 1024],
                WF[:, F3_O + spar * 1024:F3_O + (spar + 1) * 1024],
                WF[:, F3_X + spar * 1024:F3_X + (spar + 1) * 1024], ALU.add),
                reads=og + xg, writes=og)
            op("gp", lambda e, si=si, c0=c0, sub=sub, spar=spar: e.dma_start(
                out[si, c0 + sub * 128:c0 + sub * 128 + 128, :],
                WF[:, F3_O + spar * 1024:F3_O + (spar + 1) * 1024]),
                reads=og, ch=f"outst{spar}")
            if ti + 1 < NT3:
                if sub == 0:
                    p3load_xn(ti + 1)
                p3load_sub(ti + 1, sub)

    if dbg:
        _dbg_dump(nc, em, op, locals())
    _finish(em)
    return _realize(nc, em)


def _dbg_dump(nc, em, op, L):
    xm_scr, idx_scr, yab_scr, y_scr, gate_scr = (
        L["xm_scr"], L["idx_scr"], L["yab_scr"], L["y_scr"], L["gate_scr"])
    for name, ap, shp, dt in (
            ("d_xm", xm_scr, list(xm_scr.shape), F32),
            ("d_idx", idx_scr, list(idx_scr.shape), mybir.dt.int32),
            ("d_yab", yab_scr, list(yab_scr.shape), F32),
            ("d_ysc", y_scr, list(y_scr.shape), F32),
            ("d_gate", gate_scr, list(gate_scr.shape), F32)):
        o = nc.dram_tensor(name, shp, dt, kind="ExternalOutput").ap()
        op("gp", lambda e, o=o, ap=ap: e.dma_start(o, ap),
            reads=(), writes=(), ch="dbgd")


def _finish(em):
    for ch, v in sorted(em.cnt.items()):
        if ch in ("pe", "ve", "ac"):
            continue
        em._wait("gp", (ch, v))
        em._wait("sy", (ch, v))


def _realize(nc, em):
    with nc.Block() as block:
        block.sync(lambda e: [f(e) for f in em.sched["sy"]])
        block.gpsimd(lambda e: [f(e) for f in em.sched["gp"]])
        block.tensor(lambda e: [f(e) for f in em.sched["pe"]])
        block.vector(lambda e: [f(e) for f in em.sched["ve"]])
        block.scalar(lambda e: [f(e) for f in em.sched["ac"]])
    return nc


def kernel(**inputs):
    f = np.float32
    x = np.ascontiguousarray(np.asarray(inputs["L0_attn_x"], dtype=f))
    rs = np.asarray(inputs["L0_ffn_route_scale"], dtype=f).reshape(1, 1)
    rs_col = np.ascontiguousarray(np.broadcast_to(rs, (NE, 1)))
    shared = {}
    for k in ("L0_attn_Wq", "L0_attn_Wk", "L0_attn_Wv", "L0_attn_Wo",
              "L0_ffn_gate_W", "L0_ffn_W1", "L0_ffn_b1", "L0_ffn_W2",
              "L0_ffn_b2", "L0_ffn_Ws1", "L0_ffn_bs1", "L0_ffn_Ws2",
              "L0_ffn_bs2"):
        shared[k] = np.ascontiguousarray(np.asarray(inputs[k], dtype=f))
    nseq = x.shape[0] // NCORES
    S = x.shape[1]
    nc = build_nc(nseq, S)
    in_maps = []
    for c in range(NCORES):
        m = dict(shared)
        m["L0_attn_x"] = np.ascontiguousarray(x[c * nseq:(c + 1) * nseq])
        m["route_scale_col"] = rs_col
        in_maps.append(m)
    res = run_bass_kernel_spmd(nc, in_maps, core_ids=list(range(NCORES)))
    outs = [np.asarray(r["out"]) for r in res.results]
    return np.concatenate(outs, axis=0).astype(np.float32)


# version 14
# speedup vs baseline: 1.0454x; 1.0454x over previous
import numpy as np
import ml_dtypes
import concourse.bass as bass
import concourse.mybir as mybir
from concourse.bass_utils import run_bass_kernel_spmd

F32 = mybir.dt.float32
BF16 = mybir.dt.bfloat16
AF = mybir.ActivationFunctionType
ALU = mybir.AluOpType
AX = mybir.AxisListType.X

NCORES = 8
D = 1024
DC = 8
NH = 16
NPAIR = 8
NE = 8
HF = 512
JCN = 4
EPS = 1e-5
BIG = 1.0e30

ENGS = ("sy", "gp", "pe", "ve", "ac")
DMA_ENGS = ("sy", "gp")


class Em:

    def __init__(self, nc):
        self.nc = nc
        self.sched = {k: [] for k in ENGS}
        self.cnt = {}
        self.sems = {}
        for k in ("pe", "ve", "ac"):
            self.sems[k] = nc.alloc_semaphore(f"sem_{k}")
            self.cnt[k] = 0
        self.waited = {k: {} for k in ENGS}
        self.wtok = {}
        self.rtok = {}

    def _sem(self, key):
        if key not in self.sems:
            self.sems[key] = self.nc.alloc_semaphore(f"sem_{key}")
            self.cnt[key] = 0
        return self.sems[key]

    def _wait(self, eng, tok):
        if tok is None:
            return
        src, v = tok
        if self.waited[eng].get(src, 0) >= v:
            return
        self.waited[eng][src] = v
        sem = self.sems[src]
        self.sched[eng].append(lambda e, sem=sem, v=v: e.wait_ge(sem, v))

    def _pre(self, eng, reads, writes):
        for r in reads:
            self._wait(eng, self.wtok.get(r))
        for w in writes:
            self._wait(eng, self.wtok.get(w))
            for src, v in self.rtok.get(w, {}).items():
                self._wait(eng, (src, v))

    def _post(self, tok, eng, reads, writes):
        for w in writes:
            self.wtok[w] = tok
            self.rtok[w] = {}
        for r in reads:
            d = self.rtok.setdefault(r, {})
            if d.get(tok[0], 0) < tok[1]:
                d[tok[0]] = tok[1]

    def op(self, eng, fn, reads=(), writes=(), ch=None):
        assert (eng in DMA_ENGS) == (ch is not None)
        self._pre(eng, reads, writes)
        key = ch if ch is not None else eng
        sem = self._sem(key) if ch is not None else self.sems[key]
        step = 16 if ch is not None else 1
        self.cnt[key] = self.cnt.get(key, 0) + step
        v = self.cnt[key]
        self.sched[eng].append(
            lambda e, fn=fn, sem=sem, step=step: fn(e).then_inc(sem, step))
        tok = (key, v)
        self._post(tok, eng, reads, writes)
        return tok

    def group(self, eng, fns, reads=(), writes=()):
        assert eng not in DMA_ENGS
        self._pre(eng, reads, writes)
        for fn in fns[:-1]:
            self.sched[eng].append(fn)
        self.cnt[eng] += 1
        v = self.cnt[eng]
        sem = self.sems[eng]
        fn = fns[-1]
        self.sched[eng].append(
            lambda e, fn=fn, sem=sem: fn(e).then_inc(sem, 1))
        tok = (eng, v)
        self._post(tok, eng, reads, writes)
        return tok

    def dma_series(self, eng, ch, items):
        allr, allw = [], []
        for fn, rd, wr in items:
            allr += list(rd)
            allw += list(wr)
        self._pre(eng, tuple(allr), tuple(allw))
        sem = self._sem(ch)
        tok = None
        for fn, rd, wr in items:
            self.cnt[ch] += 16
            v = self.cnt[ch]
            self.sched[eng].append(
                lambda e, fn=fn, sem=sem: fn(e).then_inc(sem, 16))
            tok = (ch, v)
        self._post(tok, eng, tuple(allr), tuple(allw))
        return tok

    def retag(self, bufs, tok):
        for b in bufs:
            self.wtok[b] = tok


def self_cnt(em):
    return em.cnt


def _wfg(c0, c1):
    return tuple(f"wf:{g}" for g in range(c0 // 1024, (c1 + 1023) // 1024))


def _whg(c0, c1):
    return tuple(f"wh:{g}" for g in range(c0 // 1024, (c1 + 1023) // 1024))


def _arg(c0, c1):
    return tuple(f"ar:{g}" for g in range(c0 // 1024, (c1 + 1023) // 1024))


def build_nc(nseq, S, phases=3, p3cut=9, dbg=False):
    nc = bass.Bass("TRN2", target_bir_lowering=False, detect_race_conditions=False)
    NSUB = S // 128
    NTOK = nseq * S
    CAP = min(((NTOK // 4 + 512) + 255) // 256 * 256,
              ((NTOK + 255) // 256) * 256)
    NUNIT = CAP // 256
    T2PS = S // 256
    NT2 = nseq * T2PS
    T3PS = S // 256
    NT3 = nseq * T3PS

    x_in = nc.dram_tensor("L0_attn_x", [nseq, S, D], F32, kind="ExternalInput").ap()
    Wq = nc.dram_tensor("L0_attn_Wq", [D, D], F32, kind="ExternalInput").ap()
    Wk = nc.dram_tensor("L0_attn_Wk", [D, D], F32, kind="ExternalInput").ap()
    Wv = nc.dram_tensor("L0_attn_Wv", [D, D], F32, kind="ExternalInput").ap()
    Wo = nc.dram_tensor("L0_attn_Wo", [D, D], F32, kind="ExternalInput").ap()
    gW = nc.dram_tensor("L0_ffn_gate_W", [D, NE], F32, kind="ExternalInput").ap()
    W1 = nc.dram_tensor("L0_ffn_W1", [NE, D, HF], F32, kind="ExternalInput").ap()
    b1 = nc.dram_tensor("L0_ffn_b1", [NE, HF], F32, kind="ExternalInput").ap()
    W2 = nc.dram_tensor("L0_ffn_W2", [NE, HF, D], F32, kind="ExternalInput").ap()
    b2 = nc.dram_tensor("L0_ffn_b2", [NE, D], F32, kind="ExternalInput").ap()
    Ws1 = nc.dram_tensor("L0_ffn_Ws1", [D, HF], F32, kind="ExternalInput").ap()
    bs1 = nc.dram_tensor("L0_ffn_bs1", [HF], F32, kind="ExternalInput").ap()
    Ws2 = nc.dram_tensor("L0_ffn_Ws2", [HF, D], F32, kind="ExternalInput").ap()
    bs2 = nc.dram_tensor("L0_ffn_bs2", [D], F32, kind="ExternalInput").ap()
    rs_in = nc.dram_tensor("route_scale_col", [NE, 1], F32, kind="ExternalInput").ap()
    out = nc.dram_tensor("out", [nseq, S, D], F32, kind="ExternalOutput").ap()

    xn_scr = nc.dram_tensor("xn_scr", [nseq, DC, 128, S], BF16).ap()
    xm_scr = nc.dram_tensor("xm_scr", [NTOK, 520], F32).ap()
    idx_scr = nc.dram_tensor("idx_scr", [8 * CAP, 1], mybir.dt.int32).ap()
    yab_scr = nc.dram_tensor("yab_scr", [2 * NTOK + 128, D], F32).ap()
    y_scr = nc.dram_tensor("y_scr", [nseq * S // 128, 128, D], F32).ap()
    gate_scr = nc.dram_tensor("gate_scr", [nseq, NE, S], F32).ap()

    ident_bf_c = nc.inline_tensor(np.eye(128, dtype=ml_dtypes.bfloat16), name="c_idbf").ap()
    ident_f_c = nc.inline_tensor(np.eye(128, dtype=np.float32), name="c_idf").ap()
    iota128_c = nc.inline_tensor(
        np.arange(128, dtype=np.float32).reshape(128, 1), name="c_iota").ap()
    iote_c = nc.inline_tensor(
        np.broadcast_to(np.arange(8, dtype=np.float32), (128, 8)).copy(),
        name="c_iote").ap()
    ecap_c = nc.inline_tensor(
        np.broadcast_to(np.arange(8, dtype=np.float32) * CAP - 1.0,
                        (128, 8)).copy(), name="c_ecap").ap()
    bigidx_c = nc.inline_tensor(
        np.full((128, 8 * NUNIT), 1 << 30, dtype=np.int32), name="c_big").ap()

    AR = nc.alloc_sbuf_tensor("AR", [128, 73728], BF16)
    WH = nc.alloc_sbuf_tensor("WH", [128, 8192], BF16)
    WF = nc.alloc_sbuf_tensor("WF", [128, 6144], F32)
    ASM = nc.alloc_sbuf_tensor("ASM", [128, nseq * 1024], BF16)
    RSM = nc.alloc_sbuf_tensor("RSM", [128, nseq * 8], F32)
    idbf = nc.alloc_sbuf_tensor("idbf", [128, 128], BF16)
    idf = nc.alloc_sbuf_tensor("idf", [128, 128], F32)
    iota128 = nc.alloc_sbuf_tensor("iota128", [128, 1], F32)
    iote = nc.alloc_sbuf_tensor("iote", [128, 8], F32)
    ecap = nc.alloc_sbuf_tensor("ecap", [128, 8], F32)
    T2B = nc.alloc_sbuf_tensor("T2B", [128, 72], F32)
    EQS = nc.alloc_sbuf_tensor("EQS", [128, 16 * NSUB], F32)
    META = nc.alloc_sbuf_tensor("META", [128, 16], F32)
    IDXS = nc.alloc_sbuf_tensor("IDXS", [128, 8 * CAP // 128], mybir.dt.int32)
    TKI = nc.alloc_sbuf_tensor("TKI", [128, 12], mybir.dt.int32)
    BASEE = nc.alloc_sbuf_tensor("BASEE", [8, 2], F32)
    gwsb = nc.alloc_sbuf_tensor("gwsb", [128, DC * NE], BF16)
    b1T = nc.alloc_sbuf_tensor("b1T", [128, JCN * NE], F32)
    bs1T = nc.alloc_sbuf_tensor("bs1T", [128, JCN], F32)
    rscol = nc.alloc_sbuf_tensor("rscol", [NE, 1], F32)
    b2all = nc.alloc_sbuf_tensor("b2all", [9, D], F32)
    GT9 = nc.alloc_sbuf_tensor("GT9", [9, 512], F32)
    STT = nc.alloc_sbuf_tensor("STT", [128, 24], F32)
    SMX = nc.alloc_sbuf_tensor("SMX", [128, 64], F32)
    ASB = nc.alloc_sbuf_tensor("ASB", [128, 1024], F32)

    A_W1, A_W2, A_WS1, A_WS2 = 0, 32768, 65536, 69632
    A_WQT, A_WKT, A_NAT1, A_NAT2 = 32768, 40960, 49152, 57344
    A_NATV, A_NATO, A_WVT, A_WOT = 0, 8192, 16384, 24576

    H_XNTM, H_XNF, H_Q, H_K = 0, 3072, 6144, 7168
    H2_XNF, H2_HSC, H2_XNG = 0, 4096, 6144
    H3_XNF, H3_V, H3_AT = 0, 4096, 6144

    F_X1, F_SC2, F_SCORES, F_GFM = 0, 3072, 4096, 4096
    F2_XG0, F2_Y, F2_YS, F2_XG1 = 0, 1024, 3072, 5120
    F2_B1N, F2_BS1N = 4096, 4608
    F3_X, F3_Y, F3_O = 0, 2048, 4096
    T2_RAW, T2_EQ1, T2_MSK, T2_EQ2 = 3072, 3200, 3328, 3456
    T2_GA, T2_GB, T2_GTM, T2_S = 3584, 3712, 3840, 3968

    pY0 = nc.alloc_psum_tensor("pY0", [128, 1024], F32)
    pY1 = nc.alloc_psum_tensor("pY1", [128, 1024], F32)
    pfa = nc.alloc_psum_tensor("pfa", [128, 512], F32)
    pfb = nc.alloc_psum_tensor("pfb", [128, 512], F32)
    pG = nc.alloc_psum_tensor("pG", [128, 512], F32)
    pbf = nc.alloc_psum_tensor("pbf", [128, 1024], BF16)

    em = Em(nc)
    op, grp = em.op, em.group
    rb_idx = nc.alloc_register(mybir.EngineType.Pool, "rb_idx")
    rb_tok = nc.alloc_register(mybir.EngineType.Pool, "rb_tok")
    rb_yab = nc.alloc_register(mybir.EngineType.Pool, "rb_yab")
    em.sched["gp"].append(lambda e: e.reg_mov(rb_idx, 8 * CAP - 1))
    em.sched["gp"].append(lambda e: e.reg_mov(rb_tok, NTOK - 1))
    em.sched["gp"].append(lambda e: e.reg_mov(rb_yab, 2 * NTOK - 1))

    em.dma_series("sy", "csml", [
        (lambda e: e.dma_start(idbf[:], ident_bf_c), (), ("idbf",)),
        (lambda e: e.dma_start(idf[:], ident_f_c), (), ("idf",)),
        (lambda e: e.dma_start(iota128[:], iota128_c), (), ("iota128",)),
        (lambda e: e.dma_start(iote[:], iote_c), (), ("iote",)),
        (lambda e: e.dma_start(ecap[:], ecap_c), (), ("ecap",)),
        (lambda e: e.dma_start(rscol[:], rs_in), (), ("rscol",)),
        (lambda e: e.dma_start(WF[0:NE, F2_B1N:F2_B1N + HF], b1), (),
         _wfg(F2_B1N, F2_B1N + HF)),
        (lambda e: e.dma_start(WF[0:1, F2_BS1N:F2_BS1N + HF],
                               bs1.rearrange("(o j) -> o j", o=1)), (),
         _wfg(F2_BS1N, F2_BS1N + HF)),
    ])
    op("gp", lambda e: e.dma_start(
        gwsb[:], gW.rearrange("(dc p) e -> p dc e", p=128)),
        writes=("gwsb",), ch="gwl")
    op("gp", lambda e: e.dma_start(IDXS[:, 0:8 * NUNIT], bigidx_c),
        writes=("idxbig",), ch="ibig")
    HALF_IDX = 4 * CAP
    for hh_ in range(2):
        op("gp", lambda e, hh_=hh_: e.dma_start(
            idx_scr[hh_ * HALF_IDX:(hh_ + 1) * HALF_IDX].rearrange(
                "(p k) o -> p k o", p=128),
            IDXS[:, 0:8 * NUNIT].rearrange("p (k o) -> p k o", o=1)),
            reads=("idxbig",), writes=("idxscr",), ch="idxp")
    op("ve", lambda e: e.memset(BASEE[:], 0.0), writes=("basee",))
    op("ve", lambda e: e.memset(META[:], 0.0), writes=("meta0", "meta1"))

    for jc in range(JCN):
        op("pe", lambda e, jc=jc: e.transpose(
            pG[0:128, 0:NE],
            WF[0:NE, F2_B1N + jc * 128:F2_B1N + (jc + 1) * 128],
            idf[0:NE, 0:NE]),
            reads=_wfg(F2_B1N, F2_B1N + HF) + ("idf",), writes=("pG:0",))
        op("ve", lambda e, jc=jc: e.tensor_copy(
            b1T[:, jc * 8:(jc + 1) * 8], pG[0:128, 0:NE]),
            reads=("pG:0",), writes=("b1T",))
    for jc in range(JCN):
        op("pe", lambda e, jc=jc: e.transpose(
            pG[0:128, 8:9],
            WF[0:1, F2_BS1N + jc * 128:F2_BS1N + (jc + 1) * 128],
            idf[0:1, 0:1]),
            reads=_wfg(F2_BS1N, F2_BS1N + HF) + ("idf",), writes=("pG:0",))
        op("ve", lambda e, jc=jc: e.tensor_copy(
            bs1T[:, jc:jc + 1], pG[0:128, 8:9]), reads=("pG:0",), writes=("bs1T",))

    def load_nat(dram, base, ch):
        op("gp", lambda e, dram=dram, base=base: e.dma_start(
            AR[:, base:base + 8192].rearrange("p (ic d) -> p ic d", d=D),
            dram.rearrange("(ic p) d -> p ic d", p=128)),
            writes=_arg(base, base + 8192), ch=ch)

    def transpose_weight(natbase, dstbase):
        for dc in range(DC):
            fns = []
            for ic in range(DC):
                fns.append(lambda e, dc=dc, ic=ic: e.transpose(
                    pbf[:, ic * 128:(ic + 1) * 128],
                    AR[:, natbase + ic * 1024 + dc * 128:
                       natbase + ic * 1024 + (dc + 1) * 128],
                    idbf[:]))
            grp("pe", fns, reads=_arg(natbase, natbase + 8192) + ("idbf",),
                writes=("pbf",))
            op("ac", lambda e, dc=dc: e.activation(
                AR[:, dstbase + dc * 1024:dstbase + (dc + 1) * 1024], pbf[:],
                AF.Copy), reads=("pbf",),
                writes=_arg(dstbase + dc * 1024, dstbase + (dc + 1) * 1024))

    op("ve", lambda e: e.memset(ASM[:], 0.0),
        writes=tuple(f"asm:{s}" for s in range(nseq)))
    load_nat(Wq, A_NAT1, "natq")
    load_nat(Wk, A_NAT2, "natk")
    transpose_weight(A_NAT1, A_WQT)
    transpose_weight(A_NAT2, A_WKT)

    w1items = []
    for e_ in range(NE):
        w1items.append((lambda e, e_=e_: e.dma_start(
            AR[:, A_W1 + e_ * 4096:A_W1 + (e_ + 1) * 4096].rearrange(
                "p (dc j) -> p dc j", j=HF),
            W1[e_].rearrange("(dc p) j -> p dc j", p=128)), (),
            _arg(A_W1 + e_ * 4096, A_W1 + (e_ + 1) * 4096)))
    w1items.append((lambda e: e.dma_start(
        AR[:, A_WS1:A_WS1 + 4096].rearrange("p (dc j) -> p dc j", j=HF),
        Ws1.rearrange("(dc p) j -> p dc j", p=128)), (),
        _arg(A_WS1, A_WS1 + 4096)))
    w1items.append((lambda e: e.dma_start(
        AR[:, A_WS2:A_WS2 + 4096].rearrange("p (jc d) -> p jc d", d=D),
        Ws2.rearrange("(jc p) d -> p jc d", p=128)), (),
        _arg(A_WS2, A_WS2 + 4096)))
    em.dma_series("gp", "w1s", w1items)

    if phases < 1:
        _finish(em)
        return _realize(nc, em)
    def stat(p3, k):
        return STT[:, p3 * 8 + k:p3 * 8 + k + 1]

    def lnpart(s, u):
        p3 = u % 3
        t0 = u * 128
        xg = _wfg(F_X1 + p3 * 1024, F_X1 + (p3 + 1) * 1024)
        xap = WF[:, F_X1 + p3 * 1024:F_X1 + (p3 + 1) * 1024]
        stn = f"st:{p3}"
        op("sy", lambda e: e.dma_start(
            xap, x_in[s, t0:t0 + 128, :]), writes=xg, ch=f"xl{p3}")
        op("ve", lambda e: e.reduce_sum(
            stat(p3, 0), xap, axis=AX), reads=xg, writes=(stn,))
        op("ac", lambda e: e.activation(
            WF[:, F_SC2:F_SC2 + 1024], xap, AF.Square,
            accum_out=stat(p3, 1)),
            reads=xg, writes=_wfg(F_SC2, F_SC2 + 1024) + (stn,))
        op("ve", lambda e: e.tensor_scalar(
            stat(p3, 2), stat(p3, 0), 1.0 / D, None, ALU.mult),
            reads=(stn,), writes=(stn,))
        op("ve", lambda e: e.tensor_scalar(
            stat(p3, 3), stat(p3, 1), 1.0 / D, None, ALU.mult),
            reads=(stn,), writes=(stn,))
        op("ve", lambda e: e.scalar_tensor_tensor(
            stat(p3, 4), stat(p3, 2), -1.0, stat(p3, 2),
            ALU.mult, ALU.mult), reads=(stn,), writes=(stn,))
        op("ve", lambda e: e.tensor_add(
            stat(p3, 3), stat(p3, 3), stat(p3, 4)),
            reads=(stn,), writes=(stn,))
        op("ac", lambda e: e.activation(
            stat(p3, 5), stat(p3, 3), AF.Ln, bias=np.float32(EPS)),
            reads=(stn,), writes=(stn,))
        op("ac", lambda e: e.activation(
            stat(p3, 6), stat(p3, 5), AF.Exp, scale=np.float32(-0.5)),
            reads=(stn,), writes=(stn,))
        op("ve", lambda e: e.tensor_scalar(
            WH[:, H_XNTM + p3 * 1024:H_XNTM + (p3 + 1) * 1024],
            xap, stat(p3, 2), stat(p3, 6), ALU.subtract, ALU.mult),
            reads=xg + (stn,),
            writes=_whg(H_XNTM + p3 * 1024, H_XNTM + (p3 + 1) * 1024))

    def txpart(s, u):
        p3 = u % 3
        t0 = u * 128
        fns = []
        for c in range(DC):
            fns.append(lambda e, c=c, p3=p3: e.transpose(
                pbf[:, c * 128:(c + 1) * 128],
                WH[:, H_XNTM + p3 * 1024 + c * 128:
                   H_XNTM + p3 * 1024 + (c + 1) * 128], idbf[:]))
        grp("pe", fns,
            reads=_whg(H_XNTM + p3 * 1024, H_XNTM + (p3 + 1) * 1024) + ("idbf",),
            writes=("pbf",))
        xnfg = _whg(H_XNF + p3 * 1024, H_XNF + (p3 + 1) * 1024)
        op("ac", lambda e: e.activation(
            WH[:, H_XNF + p3 * 1024:H_XNF + (p3 + 1) * 1024], pbf[:],
            AF.Copy), reads=("pbf",), writes=xnfg)
        op("gp", lambda e: e.dma_start(
            xn_scr[s, :, :, t0:t0 + 128].rearrange("c p t -> p c t"),
            WH[:, H_XNF + p3 * 1024:H_XNF + (p3 + 1) * 1024].rearrange(
                "p (c t) -> p c t", t=128)),
            reads=xnfg, writes=(f"xnscr:{s}:{u}",), ch=f"xnst{p3}")
        op("gp", lambda e: e.dma_start(
            xm_scr[s * S + t0:s * S + t0 + 128, 0:512],
            WH[:, H_XNTM + p3 * 1024:H_XNTM + (p3 + 1) * 1024].bitcast(F32)),
            reads=_whg(H_XNTM + p3 * 1024, H_XNTM + (p3 + 1) * 1024),
            writes=(f"xmscr:{s}:{u}",), ch=f"xmst{u % 2}")

    def pepart(s, u):
        p3 = u % 3
        t0 = u * 128
        xnfg = _whg(H_XNF + p3 * 1024, H_XNF + (p3 + 1) * 1024)
        xnf0 = H_XNF + p3 * 1024
        fns = []
        for h in range(2):
            for c in range(DC):
                fns.append(lambda e, h=h, c=c: e.matmul(
                    pY1[:, h * 512:(h + 1) * 512],
                    WH[:, xnf0 + c * 128:xnf0 + (c + 1) * 128],
                    AR[:, A_WQT + c * 1024 + h * 512:
                       A_WQT + c * 1024 + (h + 1) * 512],
                    start=(c == 0), stop=(c == DC - 1)))
        grp("pe", fns, reads=xnfg + _arg(A_WQT, A_WQT + 8192), writes=("pY1",))
        qg = _whg(H_Q, H_Q + 1024)
        op("ac", lambda e: e.activation(
            WH[:, H_Q:H_Q + 1024], pY1[:],
            AF.Copy, scale=np.float32(0.125)), reads=("pY1",), writes=qg)
        kg = _whg(H_K, H_K + 1024)
        for h, pp, pname in ((0, pfa, "pfa"), (1, pfb, "pfb")):
            fns = []
            for c in range(DC):
                fns.append(lambda e, h=h, c=c, pp=pp: e.matmul(
                    pp[:],
                    WH[:, xnf0 + c * 128:xnf0 + (c + 1) * 128],
                    AR[:, A_WKT + c * 1024 + h * 512:
                       A_WKT + c * 1024 + (h + 1) * 512],
                    start=(c == 0), stop=(c == DC - 1)))
            grp("pe", fns, reads=xnfg + _arg(A_WKT, A_WKT + 8192),
                writes=(pname,))
            op("ac", lambda e, h=h, pp=pp: e.activation(
                WH[:, H_K + h * 512:H_K + (h + 1) * 512], pp[:], AF.Copy),
                reads=(pname,), writes=kg)
    def pepart_a(s, u):
        qg = _whg(H_Q, H_Q + 1024)
        kg = _whg(H_K, H_K + 1024)
        fns = []
        for p in range(NPAIR):
            fns.append(lambda e, p=p: e.matmul(
                pY0[:, p * 128:(p + 1) * 128],
                WH[:, H_K + p * 128:H_K + (p + 1) * 128],
                WH[:, H_Q + p * 128:H_Q + (p + 1) * 128],
                start=True, stop=True, skip_group_check=True))
        grp("pe", fns, reads=kg + qg, writes=("pY0",))
        if u == 0:
            op("ve", lambda e: e.tensor_copy(ASB[:], pY0[:]),
                reads=("pY0",), writes=("asb",))
        else:
            op("ve", lambda e: e.tensor_add(ASB[:], ASB[:], pY0[:]),
                reads=("pY0", "asb"), writes=("asb",))
    def gatemm(s, u):
        t0 = u * 128
        p3 = u % 3
        xnfg = _whg(H_XNF + p3 * 1024, H_XNF + (p3 + 1) * 1024)
        xnf0 = H_XNF + p3 * 1024
        fns = []
        for c in range(DC):
            fns.append(lambda e, c=c: e.matmul(
                pG[0:NE, 0:128], gwsb[:, c * 8:(c + 1) * 8],
                WH[:, xnf0 + c * 128:xnf0 + (c + 1) * 128],
                start=(c == 0), stop=(c == DC - 1)))
        grp("pe", fns, reads=xnfg + ("gwsb",), writes=("pG:0",))
        op("ac", lambda e: e.activation(
            b2all[0:NE, 0:128], pG[0:NE, 0:128], AF.Exp, scale=np.float32(-1.0)),
            reads=("pG:0",), writes=("sig1",))
        op("ve", lambda e: e.tensor_scalar(
            b2all[0:NE, 128:256], b2all[0:NE, 0:128], 1.0, None, ALU.add),
            reads=("sig1",), writes=("sig2",))
        op("ve", lambda e: e.reciprocal(
            b2all[0:NE, 256:384], b2all[0:NE, 128:256]),
            reads=("sig2",), writes=("sig3",))
        op("ve", lambda e: e.tensor_scalar(
            WF[0:NE, F_SCORES + t0:F_SCORES + t0 + 128],
            b2all[0:NE, 256:384], rscol[:], None, ALU.mult),
            reads=("sig3", "rscol"),
            writes=_wfg(F_SCORES + t0, F_SCORES + t0 + 128))
    def gatepart(s, u):
        t0 = u * 128
        scg = _wfg(F_SCORES + t0, F_SCORES + t0 + 128)
        mpar = u % 2
        mslc = lambda a, b: META[:, mpar * 8 + a:mpar * 8 + b]
        tb = lambda a, b: T2B[:, a:b]
        op("pe", lambda e: e.transpose(
            pfa[:, 0:8], WF[0:NE, F_SCORES + t0:F_SCORES + t0 + 128],
            idf[0:NE, 0:NE]), reads=scg + ("idf",), writes=("pfa",))
        op("ve", lambda e: e.tensor_copy(tb(0, 8), pfa[:, 0:8]),
            reads=("pfa",), writes=("t2raw",))
        op("ve", lambda e: e.reduce_max(tb(32, 33), tb(0, 8), axis=AX),
            reads=("t2raw",), writes=("t2m",))
        op("ve", lambda e: e.tensor_scalar(
            tb(8, 16), tb(0, 8), tb(32, 33), None, ALU.is_equal),
            reads=("t2raw", "t2m"), writes=("t2eq1",))
        op("ve", lambda e: e.tensor_tensor_scan(
            tb(16, 24), tb(8, 16), tb(8, 16), 0.0, ALU.add, ALU.bypass),
            reads=("t2eq1",), writes=("t2msk",))
        op("ve", lambda e: e.tensor_scalar(
            tb(16, 24), tb(16, 24), 1.0, None, ALU.is_equal),
            reads=("t2msk",), writes=("t2msk",))
        op("ve", lambda e: e.tensor_mul(tb(8, 16), tb(8, 16), tb(16, 24)),
            reads=("t2eq1", "t2msk"), writes=("t2eq1",))
        op("ve", lambda e: e.scalar_tensor_tensor(
            tb(16, 24), tb(8, 16), -BIG, tb(0, 8), ALU.mult, ALU.add),
            reads=("t2eq1", "t2raw"), writes=("t2msk",))
        op("ve", lambda e: e.reduce_max(tb(33, 34), tb(16, 24), axis=AX),
            reads=("t2msk",), writes=("t2m",))
        op("ve", lambda e: e.tensor_scalar(
            tb(24, 32), tb(16, 24), tb(33, 34), None, ALU.is_equal),
            reads=("t2msk", "t2m"), writes=("t2eq2",))
        op("ve", lambda e: e.tensor_tensor_scan(
            tb(16, 24), tb(24, 32), tb(24, 32), 0.0, ALU.add, ALU.bypass),
            reads=("t2eq2",), writes=("t2msk",))
        op("ve", lambda e: e.tensor_scalar(
            tb(16, 24), tb(16, 24), 1.0, None, ALU.is_equal),
            reads=("t2msk",), writes=("t2msk",))
        op("ve", lambda e: e.tensor_mul(tb(24, 32), tb(24, 32), tb(16, 24)),
            reads=("t2eq2", "t2msk"), writes=("t2eq2",))
        op("ve", lambda e: e.tensor_add(tb(34, 35), tb(32, 33), tb(33, 34)),
            reads=("t2m",), writes=("t2den",))
        op("ve", lambda e: e.reciprocal(tb(35, 36), tb(34, 35)),
            reads=("t2den",), writes=("t2rde",))
        op("ve", lambda e: e.tensor_mul(mslc(0, 1), tb(32, 33), tb(35, 36)),
            reads=("t2m", "t2rde"), writes=(f"meta{mpar}",))
        op("ve", lambda e: e.tensor_mul(mslc(1, 2), tb(33, 34), tb(35, 36)),
            reads=("t2m", "t2rde"), writes=(f"meta{mpar}",))
        op("ve", lambda e: e.tensor_mul(tb(16, 24), tb(8, 16), iote[:]),
            reads=("t2eq1", "iote"), writes=("t2msk",))
        op("ve", lambda e: e.reduce_sum(mslc(2, 3), tb(16, 24), axis=AX),
            reads=("t2msk",), writes=(f"meta{mpar}",))
        op("ve", lambda e: e.tensor_mul(tb(16, 24), tb(24, 32), iote[:]),
            reads=("t2eq2", "iote"), writes=("t2msk",))
        op("ve", lambda e: e.reduce_sum(mslc(3, 4), tb(16, 24), axis=AX),
            reads=("t2msk",), writes=(f"meta{mpar}",))
        op("ve", lambda e: e.tensor_scalar(
            mslc(4, 5), iota128[:], float(s * S + t0), None, ALU.add),
            reads=("iota128",), writes=(f"meta{mpar}",))
        op("ve", lambda e: e.tensor_copy(EQS[:, u * 16:u * 16 + 8], tb(8, 16)),
            reads=("t2eq1",), writes=("eqs",))
        op("ve", lambda e: e.tensor_copy(EQS[:, u * 16 + 8:u * 16 + 16], tb(24, 32)),
            reads=("t2eq2",), writes=("eqs",))
        op("ve", lambda e: e.tensor_scalar(
            tb(0, 8), tb(8, 16), mslc(0, 1), None, ALU.mult),
            reads=("t2eq1", f"meta{mpar}"), writes=("t2raw",))
        op("ve", lambda e: e.tensor_scalar(
            tb(16, 24), tb(24, 32), mslc(1, 2), None, ALU.mult),
            reads=("t2eq2", f"meta{mpar}"), writes=("t2msk",))
        op("ve", lambda e: e.tensor_add(tb(0, 8), tb(0, 8), tb(16, 24)),
            reads=("t2raw", "t2msk"), writes=("t2raw",))
        op("gp", lambda e: e.dma_start(
            xm_scr[s * S + t0:s * S + t0 + 128, 512:520],
            META[:, mpar * 8:mpar * 8 + 8]),
            reads=(f"meta{mpar}",), writes=(f"xmscr2:{s}:{u}",),
            ch=f"xmm{mpar}")

    def gatepart_b(s, u):
        t0 = u * 128
        tb = lambda a, b: T2B[:, a:b]
        op("pe", lambda e: e.transpose(
            pG[0:NE, 128:256], tb(0, 8), idf[:]),
            reads=("t2raw", "idf", "pG:0"), writes=("pG:1",))
        gfg = _wfg(F_GFM + t0, F_GFM + t0 + 128)
        op("ac", lambda e: e.activation(
            WF[0:NE, F_GFM + t0:F_GFM + t0 + 128], pG[0:NE, 128:256],
            AF.Copy), reads=("pG:1", "pG:0"), writes=gfg)

    def idxpart(s, u):
        t0 = u * 128
        gfall = _wfg(F_GFM, F_GFM + S)
        mp = u % 2
        tki0 = 4 * (mp + 1)
        op("pe", lambda e, t0=t0: e.transpose(
            pY1[:, 0:8], WF[0:NE, F_GFM + t0:F_GFM + t0 + 128],
            idf[0:NE, 0:NE]), reads=gfall + ("idf",), writes=("pY1",))
        op("ac", lambda e: e.activation(T2B[:, 36:44], pY1[:, 0:8], AF.Copy),
            reads=("pY1",), writes=("eppos",))
        op("ve", lambda e: e.tensor_scalar(
            T2B[:, 44:52], T2B[:, 36:44], float(CAP) + 0.5, None, ALU.is_ge),
            reads=("eppos",), writes=("epge",))
        op("ve", lambda e: e.tensor_add(
            T2B[:, 52:60], T2B[:, 36:44], ecap[:]),
            reads=("eppos", "ecap"), writes=("epmsk",))
        op("ve", lambda e: e.scalar_tensor_tensor(
            T2B[:, 52:60], T2B[:, 44:52], float(1 << 30), T2B[:, 52:60],
            ALU.mult, ALU.add), reads=("epge", "epmsk"), writes=("epmsk",))
        op("ve", lambda e, s=s, t0=t0: e.tensor_scalar(
            T2B[:, 68:69], iota128[:], float(s * S + t0), None, ALU.add),
            reads=("iota128",), writes=("eptk",))
        op("ve", lambda e, tki0=tki0: e.tensor_copy(
            TKI[:, tki0:tki0 + 1], T2B[:, 68:69]),
            reads=("eptk",), writes=(f"eptok{mp}",))
        KTILES = 8 * CAP // 128
        for sl in range(2):
            eqap = EQS[:, u * 16 + sl * 8:u * 16 + sl * 8 + 8]
            op("ve", lambda e, eqap=eqap: e.tensor_mul(
                T2B[:, 60:68], eqap, T2B[:, 52:60]),
                reads=("eqs", "epmsk"), writes=("epeq",))
            op("ve", lambda e: e.reduce_sum(
                T2B[:, 69:70], T2B[:, 60:68], axis=AX),
                reads=("epeq",), writes=("epd",))
            op("ve", lambda e: e.tensor_scalar(
                T2B[:, 70:71], T2B[:, 69:70], 1.0 / 128, None, ALU.mult),
                reads=("epd",), writes=("epk",))
            op("ve", lambda e, tki0=tki0: e.tensor_copy(
                TKI[:, tki0 + 3:tki0 + 4], T2B[:, 70:71]),
                reads=("epk",), writes=(f"epki{mp}",))
            op("ve", lambda e, tki0=tki0: e.tensor_copy(
                T2B[:, 70:71], TKI[:, tki0 + 3:tki0 + 4]),
                reads=(f"epki{mp}",), writes=("epk",))
            op("ve", lambda e: e.scalar_tensor_tensor(
                T2B[:, 71:72], T2B[:, 70:71], -128.0, T2B[:, 69:70],
                ALU.mult, ALU.add), reads=("epk", "epd"), writes=("epr",))
            op("ve", lambda e: e.tensor_scalar(
                T2B[:, 68:69], T2B[:, 71:72], 0.0, None, ALU.is_lt),
                reads=("epr",), writes=("eptk",))
            op("ve", lambda e: e.tensor_sub(
                T2B[:, 70:71], T2B[:, 70:71], T2B[:, 68:69]),
                reads=("epk", "eptk"), writes=("epk",))
            op("ve", lambda e: e.scalar_tensor_tensor(
                T2B[:, 71:72], T2B[:, 68:69], 128.0, T2B[:, 71:72],
                ALU.mult, ALU.add), reads=("eptk", "epr"), writes=("epr",))
            op("ve", lambda e, KTILES=KTILES: e.scalar_tensor_tensor(
                T2B[:, 69:70], T2B[:, 71:72], float(KTILES), T2B[:, 70:71],
                ALU.mult, ALU.add), reads=("epr", "epk"), writes=("epd",))
            op("ve", lambda e, sl=sl, tki0=tki0: e.tensor_copy(
                TKI[:, tki0 + 1 + sl:tki0 + 2 + sl], T2B[:, 69:70]),
                reads=("epd",), writes=(f"epd{sl}{mp}",))
            op("gp", lambda e, sl=sl, tki0=tki0: e.indirect_dma_start(
                idx_scr, bass.IndirectOffsetOnAxis(
                    TKI[:, tki0 + 1 + sl:tki0 + 2 + sl], 0),
                TKI[:, tki0:tki0 + 1], None, bounds_check=rb_idx,
                oob_is_err=False),
                reads=(f"eptok{mp}", f"epd{sl}{mp}", "idxscr"),
                writes=("idxscr",),
                ch=f"ixs{sl}")

    for s in range(nseq):
        lnpart(s, 0)
        if NSUB > 1:
            lnpart(s, 1)
        txpart(s, 0)
        for u in range(NSUB):
            if u >= 4:
                gatepart_b(s, u - 4)
            if u >= 3:
                gatepart(s, u - 3)
            if u + 2 < NSUB:
                lnpart(s, u + 2)
            if u + 1 < NSUB:
                txpart(s, u + 1)
            if s >= 1 and u == 0:
                idxpart(s - 1, 0)
            pepart(s, u)
            gatemm(s, u)
            pepart_a(s, u)
            if s >= 1 and u + 1 < NSUB:
                idxpart(s - 1, u + 1)
        for u in range(max(0, NSUB - 4), NSUB):
            if u <= NSUB - 4:
                gatepart_b(s, u)
            else:
                gatepart(s, u)
                gatepart_b(s, u)
        op("ve", lambda e: e.reduce_max(
            SMX[:, 0:16], ASB[:].rearrange("p (a q) -> p a q", q=64), axis=AX),
            reads=("asb",), writes=("smx_m",))
        op("ve", lambda e: e.tensor_scalar(
            SMX[:, 16:32], SMX[:, 0:16], -1.0, None, ALU.mult),
            reads=("smx_m",), writes=("smx_n",))
        op("ve", lambda e: e.memset(SMX[:, 32:48], 1.0), writes=("smx_s",))
        for hh in range(NH):
            p, o = hh // 2, hh % 2
            r0 = 64 * o
            a = 2 * p + o
            op("ac", lambda e, s=s, p=p, r0=r0, a=a, o=o: e.activation(
                ASM[r0:r0 + 64, s * 1024 + p * 128 + o * 64:
                    s * 1024 + p * 128 + o * 64 + 64],
                ASB[r0:r0 + 64, p * 128 + o * 64:p * 128 + o * 64 + 64],
                AF.Exp, bias=SMX[r0:r0 + 64, 16 + a:16 + a + 1],
                accum_out=SMX[r0:r0 + 64, 32 + a:32 + a + 1]),
                reads=("asb", "smx_n"),
                writes=(f"asm:{s}", "smx_s"))
        op("ve", lambda e, s=s: e.reciprocal(
            RSM[0:64, s * 8:s * 8 + 8],
            SMX[0:64, 32:48].rearrange("p (a two) -> p a two", two=2)[:, :, 0]),
            reads=("smx_s",), writes=(f"rsm:{s}",))
        op("ve", lambda e, s=s: e.reciprocal(
            RSM[64:128, s * 8:s * 8 + 8],
            SMX[64:128, 32:48].rearrange("p (a two) -> p a two", two=2)[:, :, 1]),
            reads=("smx_s",), writes=(f"rsm:{s}",))

        gfall = _wfg(F_GFM, F_GFM + S)
        op("gp", lambda e, s=s: e.dma_start(
            gate_scr[s], WF[0:NE, F_GFM:F_GFM + S]),
            reads=gfall, writes=(f"gscr:{s}",), ch="gst")
        NHALF = (S + 1023) // 1024
        for hh in range(NHALF):
            hc = hh * 1024
            hw = min(1024, S - hc)
            op("ve", lambda e, hc=hc, hw=hw: e.tensor_scalar(
                WF[0:NE, F_SC2:F_SC2 + hw],
                WF[0:NE, F_GFM + hc:F_GFM + hc + hw], 0.0, None, ALU.is_gt),
                reads=gfall, writes=_wfg(F_SC2, F_SC2 + 1024))
            if hh == 0:
                ini = 0.0
            else:
                ini = WF[0:NE, F_GFM + hc - 1:F_GFM + hc]
            op("ve", lambda e, hc=hc, hw=hw, ini=ini: e.tensor_tensor_scan(
                WF[0:NE, F_GFM + hc:F_GFM + hc + hw],
                WF[0:NE, F_SC2:F_SC2 + hw], WF[0:NE, F_SC2:F_SC2 + hw],
                ini, ALU.add, ALU.bypass),
                reads=_wfg(F_SC2, F_SC2 + 1024) + gfall, writes=gfall)
        op("ve", lambda e: e.tensor_scalar(
            WF[0:NE, F_GFM:F_GFM + S], WF[0:NE, F_GFM:F_GFM + S],
            BASEE[:, 0:1], None, ALU.add),
            reads=gfall + ("basee",), writes=gfall)
        op("ve", lambda e: e.tensor_copy(
            BASEE[:, 0:1], WF[0:NE, F_GFM + S - 1:F_GFM + S]),
            reads=gfall, writes=("basee",))
    for u in range(NSUB):
        idxpart(nseq - 1, u)

    if phases < 2:
        _finish(em)
        return _realize(nc, em)
    for sl in range(2):
        ch = f"ixs{sl}"
        if ch in em.cnt:
            em.wtok[f"idxfin{sl}"] = (ch, em.cnt[ch])
    op("sy", lambda e: e.dma_start(
        IDXS[:], idx_scr.rearrange("(p k) o -> p (k o)", p=128)),
        reads=("idxscr", "idxfin0", "idxfin1", "idxbig"),
        writes=("idxall",), ch="ixall")
    for i_, ch in enumerate(("xmst0", "xmst1", "xmm0", "xmm1")):
        if ch in em.cnt:
            em.retag([k for k in em.wtok if k.startswith("xmscr")
                      and em.wtok[k][0] == ch], (ch, em.cnt[ch]))
            em.wtok[f"xmfin{i_}"] = (ch, em.cnt[ch])
    for ch in ("xnst0", "xnst1", "xnst2"):
        if ch in self_cnt(em):
            tokf = (ch, em.cnt[ch])
            em.retag([k for k in em.wtok if k.startswith("xnscr:")
                      and em.wtok[k][0] == ch], tokf)
    if "gst" in em.cnt:
        em.retag([k for k in em.wtok if k.startswith("gscr:")],
                 ("gst", em.cnt["gst"]))
    w2items = []
    for e_ in range(NE):
        w2items.append((lambda e, e_=e_: e.dma_start(
            AR[:, A_W2 + e_ * 4096:A_W2 + (e_ + 1) * 4096].rearrange(
                "p (jc d) -> p jc d", d=D),
            W2[e_].rearrange("(jc p) d -> p jc d", p=128)), (),
            _arg(A_W2 + e_ * 4096, A_W2 + (e_ + 1) * 4096)))
    em.dma_series("gp", "w2s", w2items)
    em.dma_series("sy", "b2l", [
        (lambda e: e.dma_start(b2all[0:NE, :], b2), (),
         ("b2sb", "sig1", "sig2", "sig3")),
        (lambda e: e.dma_start(b2all[8:9, :], bs2.rearrange("(o d) -> o d", o=1)),
         (), ("bs2sb",)),
    ])

    op("ve", lambda e: e.memset(GT9[0:9, :], 1.0),
        writes=("gt9ones", "gt9:0", "gt9:1"))

    OOBF = float(1 << 30)
    NGU = NE * NUNIT

    def xg_col(gidx):
        return (F2_XG0, 1024, 2048, F2_XG1)[gidx % 4]

    for slot_ in range(4):
        xgc_ = (F2_XG0, 1024, 2048, F2_XG1)[slot_]
        op("ve", lambda e, xgc_=xgc_: e.memset(WF[:, xgc_:xgc_ + 520], 0.0),
            writes=(f"xg:{slot_}", f"wf:{xgc_ // 1024}"))

    def emit_gather(gidx):
        xgc = xg_col(gidx)
        xgg = (f"xg:{gidx % 4}", f"wf:{xgc // 1024}")
        op("gp", lambda e, xgc=xgc, gidx=gidx: e.indirect_dma_start(
            WF[:, xgc:xgc + 520], None, xm_scr,
            bass.IndirectOffsetOnAxis(IDXS[:, gidx:gidx + 1], 0),
            bounds_check=rb_tok, oob_is_err=False),
            reads=("idxall", "xmfin0", "xmfin1", "xmfin2", "xmfin3"),
            writes=xgg, ch=f"xgl{gidx % 4}")

    def emit_tx(guni):
        xngb = H2_XNF + (guni % 2) * 2048
        xng = _whg(xngb, xngb + 2048)
        for sub in range(2):
            gidx = guni * 2 + sub
            xgc = xg_col(gidx)
            xgg = (f"xg:{gidx % 4}", f"wf:{xgc // 1024}")
            fns = []
            for c in range(DC):
                fns.append(lambda e, c=c, xgc=xgc: e.transpose(
                    pbf[:, c * 128:(c + 1) * 128],
                    WF[:, xgc:xgc + 512].bitcast(BF16)[:, c * 128:(c + 1) * 128],
                    idbf[:]))
            grp("pe", fns, reads=xgg + ("idbf",), writes=("pbf",))
            op("ac", lambda e, sub=sub, xngb=xngb: e.activation(
                WH[:, xngb:xngb + 2048].rearrange(
                    "p (c two t) -> p c two t", two=2, t=128)[:, :, sub, :],
                pbf[:].rearrange("p (c t) -> p c t", t=128), AF.Copy),
                reads=("pbf",), writes=xng)

    def emit_w1(guni):
        e_x = guni // NUNIT
        xngb = H2_XNF + (guni % 2) * 2048
        xng = _whg(xngb, xngb + 2048)
        upar = guni % 2
        hsg = _whg(H2_HSC + upar * 1024, H2_HSC + (upar + 1) * 1024)
        for half, pp, pname in ((0, pfa, "pfa"), (1, pfb, "pfb")):
            fns = []
            for jo in range(2):
                jc = half * 2 + jo
                for c in range(DC):
                    a0 = A_W1 + e_x * 4096 + c * 512 + jc * 128
                    fns.append(lambda e, jo=jo, c=c, a0=a0, pp=pp, xngb=xngb:
                        e.matmul(
                            pp[:, jo * 256:(jo + 1) * 256],
                            AR[:, a0:a0 + 128],
                            WH[:, xngb + c * 256:xngb + (c + 1) * 256],
                            start=(c == 0), stop=(c == DC - 1)))
            grp("pe", fns,
                reads=xng + _arg(A_W1 + e_x * 4096, A_W1 + (e_x + 1) * 4096),
                writes=(pname,))
            for jo in range(2):
                jc = half * 2 + jo
                op("ac", lambda e, jc=jc, jo=jo, pp=pp, upar=upar, e_x=e_x:
                    e.activation(
                        WH[:, H2_HSC + upar * 1024 + jc * 256:
                           H2_HSC + upar * 1024 + (jc + 1) * 256],
                        pp[:, jo * 256:(jo + 1) * 256], AF.Gelu,
                        bias=b1T[:, jc * 8 + e_x:jc * 8 + e_x + 1]),
                    reads=(pname, "b1T"), writes=hsg)

    def emit_w2(guni):
        e_x = guni // NUNIT
        upar = guni % 2
        hsg = _whg(H2_HSC + upar * 1024, H2_HSC + (upar + 1) * 1024)
        for sub in range(2):
            gidx = guni * 2 + sub
            xgc = xg_col(gidx)
            xgg = (f"xg:{gidx % 4}", f"wf:{xgc // 1024}")
            pY, pYn = (pY0, "pY0") if sub == 0 else (pY1, "pY1")
            fns = []
            for h in range(2):
                for jc in range(JCN):
                    a0 = A_W2 + e_x * 4096 + jc * 1024 + h * 512
                    fns.append(lambda e, jc=jc, h=h, a0=a0, pY=pY,
                               upar=upar, sub=sub: e.matmul(
                        pY[:, h * 512:(h + 1) * 512],
                        WH[:, H2_HSC + upar * 1024 + jc * 256 + sub * 128:
                           H2_HSC + upar * 1024 + jc * 256 + sub * 128 + 128],
                        AR[:, a0:a0 + 512],
                        start=(jc == 0), stop=(jc == JCN - 1)))
            grp("pe", fns,
                reads=hsg + _arg(A_W2 + e_x * 4096, A_W2 + (e_x + 1) * 4096),
                writes=(pYn,))
            op("ve", lambda e, xgc=xgc, e_x=e_x: e.tensor_scalar(
                T2B[:, 0:1], WF[:, xgc + 514:xgc + 515], float(e_x), None,
                ALU.is_equal), reads=xgg, writes=("t2raw",))
            op("ve", lambda e, xgc=xgc, e_x=e_x: e.tensor_scalar(
                T2B[:, 1:2], WF[:, xgc + 515:xgc + 516], float(e_x), None,
                ALU.is_equal), reads=xgg, writes=("t2raw",))
            op("ve", lambda e, xgc=xgc: e.tensor_mul(
                T2B[:, 2:3], WF[:, xgc + 512:xgc + 513], T2B[:, 0:1]),
                reads=xgg + ("t2raw",), writes=("t2m",))
            op("ve", lambda e, xgc=xgc: e.tensor_mul(
                T2B[:, 3:4], WF[:, xgc + 513:xgc + 514], T2B[:, 1:2]),
                reads=xgg + ("t2raw",), writes=("t2m",))
            op("ve", lambda e: e.tensor_add(
                T2B[:, 2:3], T2B[:, 2:3], T2B[:, 3:4]),
                reads=("t2m",), writes=("t2m",))
            op("ve", lambda e, xgc=xgc: e.scalar_tensor_tensor(
                T2B[:, 4:5], T2B[:, 1:2], float(NTOK),
                WF[:, xgc + 516:xgc + 517], ALU.mult, ALU.add),
                reads=("t2raw",) + xgg, writes=("t2d",))
            op("ve", lambda e, gidx=gidx: e.tensor_copy(
                T2B[:, 5:6], IDXS[:, gidx:gidx + 1]),
                reads=("idxall",), writes=("t2v",))
            op("ve", lambda e: e.tensor_scalar(
                T2B[:, 6:7], T2B[:, 5:6], float(NTOK), None, ALU.is_lt),
                reads=("t2v",), writes=("t2v",))
            op("ve", lambda e: e.tensor_scalar(
                T2B[:, 4:5], T2B[:, 4:5], float(2 * NTOK), None, ALU.subtract),
                reads=("t2d",), writes=("t2d",))
            op("ve", lambda e: e.tensor_mul(
                T2B[:, 4:5], T2B[:, 4:5], T2B[:, 6:7]),
                reads=("t2d", "t2v"), writes=("t2d",))
            op("ve", lambda e: e.tensor_scalar(
                T2B[:, 4:5], T2B[:, 4:5], float(2 * NTOK), None, ALU.add),
                reads=("t2d",), writes=("t2d",))
            op("ve", lambda e, gidx=gidx: e.tensor_copy(
                TKI[:, 2 + gidx % 2:3 + gidx % 2], T2B[:, 4:5]),
                reads=("t2d",), writes=(f"ysd{gidx % 2}",))
            ysc = F2_YS + (gidx % 2) * 1024
            ysg = (f"ys:{gidx % 2}", "wf:3" if gidx % 2 == 0 else "wf:4")
            op("ac", lambda e, pY=pY, ysc=ysc: e.activation(
                WF[:, ysc:ysc + 1024], pY[:], AF.Copy,
                scale=T2B[:, 2:3]),
                reads=(pYn, "t2m"), writes=ysg)
            op("gp", lambda e, ysc=ysc, gidx=gidx: e.indirect_dma_start(
                yab_scr, bass.IndirectOffsetOnAxis(
                    TKI[:, 2 + gidx % 2:3 + gidx % 2], 0),
                WF[:, ysc:ysc + 1024], None),
                reads=ysg + (f"ysd{gidx % 2}",), writes=(f"yab{gidx % 2}",),
                ch=f"ysc{gidx % 2}")

    for sub in range(4):
        emit_gather(sub)
    emit_tx(0)
    for guni in range(NGU):
        emit_w1(guni)
        if guni + 1 < NGU:
            emit_tx(guni + 1)
        emit_w2(guni)
        if guni + 2 < NGU:
            for sub in range(2):
                emit_gather((guni + 2) * 2 + sub)

    def sh_load(ti):
        si, c0 = ti // T2PS, (ti % T2PS) * 256
        par = ti % 2
        u0 = c0 // 128
        xn2g = _whg(H2_XNF + par * 2048, H2_XNF + (par + 1) * 2048)
        op("sy", lambda e, si=si, c0=c0, par=par: e.dma_start(
            WH[:, H2_XNF + par * 2048:H2_XNF + (par + 1) * 2048].rearrange(
                "p (c t) -> p c t", t=256),
            xn_scr[si, :, :, c0:c0 + 256].rearrange("c p t -> p c t")),
            reads=(f"xnscr:{si}:{u0}", f"xnscr:{si}:{u0 + 1}"), writes=xn2g,
            ch=f"xn2l{par}")
        op("sy", lambda e, si=si, c0=c0, par=par: e.dma_start(
            GT9[0:NE, par * 256:(par + 1) * 256],
            gate_scr[si, :, c0:c0 + 256]),
            reads=(f"gscr:{si}",), writes=(f"gt9:{par}",), ch=f"gtl{par}")

    sh_load(0)
    for ti in range(NT2):
        if ti + 1 < NT2:
            sh_load(ti + 1)
        si, c0 = ti // T2PS, (ti % T2PS) * 256
        par = ti % 2
        xn2g = _whg(H2_XNF + par * 2048, H2_XNF + (par + 1) * 2048)
        gtg = (f"gt9:{par}",)
        hsg = _whg(H2_HSC + par * 1024, H2_HSC + (par + 1) * 1024)
        for half, pp, pname in ((0, pfa, "pfa"), (1, pfb, "pfb")):
            fns = []
            for jo in range(2):
                jc = half * 2 + jo
                for c in range(DC):
                    a0 = A_WS1 + c * 512 + jc * 128
                    fns.append(lambda e, jo=jo, c=c, a0=a0, pp=pp, par=par:
                        e.matmul(
                            pp[:, jo * 256:(jo + 1) * 256],
                            AR[:, a0:a0 + 128],
                            WH[:, H2_XNF + par * 2048 + c * 256:
                               H2_XNF + par * 2048 + (c + 1) * 256],
                            start=(c == 0), stop=(c == DC - 1)))
            grp("pe", fns, reads=xn2g + _arg(A_WS1, A_WS1 + 4096),
                writes=(pname,))
            for jo in range(2):
                jc = half * 2 + jo
                op("ac", lambda e, jc=jc, jo=jo, pp=pp, par=par: e.activation(
                    WH[:, H2_HSC + par * 1024 + jc * 256:
                       H2_HSC + par * 1024 + (jc + 1) * 256],
                    pp[:, jo * 256:(jo + 1) * 256], AF.Gelu,
                    bias=bs1T[:, jc:jc + 1]),
                    reads=(pname, "bs1T"), writes=hsg)
        for sub, pY, pYn in ((0, pY0, "pY0"), (1, pY1, "pY1")):
            fns = []
            for h in range(2):
                for jc in range(JCN):
                    a0 = A_WS2 + jc * 1024 + h * 512
                    fns.append(lambda e, jc=jc, h=h, a0=a0, pY=pY, par=par,
                               sub=sub: e.matmul(
                        pY[:, h * 512:(h + 1) * 512],
                        WH[:, H2_HSC + par * 1024 + jc * 256 + sub * 128:
                           H2_HSC + par * 1024 + jc * 256 + sub * 128 + 128],
                        AR[:, a0:a0 + 512], start=(jc == 0), stop=False))
                fns.append(lambda e, h=h, pY=pY, par=par, sub=sub: e.matmul(
                    pY[:, h * 512:(h + 1) * 512],
                    GT9[0:9, par * 256 + sub * 128:par * 256 + sub * 128 + 128],
                    b2all[0:9, h * 512:(h + 1) * 512], start=False, stop=True))
            grp("pe", fns,
                reads=hsg + _arg(A_WS2, A_WS2 + 4096) + gtg +
                ("b2sb", "bs2sb", "gt9ones"), writes=(pYn,))
            yc = F2_Y + sub * 1024
            yg = _wfg(yc, yc + 1024)
            op("ac", lambda e, pY=pY, yc=yc: e.activation(
                WF[:, yc:yc + 1024], pY[:], AF.Copy), reads=(pYn,), writes=yg)
            gsub = (si * S + c0) // 128 + sub
            op("sy", lambda e, yc=yc, gsub=gsub: e.dma_start(
                y_scr[gsub], WF[:, yc:yc + 1024]),
                reads=yg, writes=(f"yscr:{gsub}",), ch=f"yst{sub}")

    if phases < 3:
        _finish(em)
        return _realize(nc, em)
    for sub_ in range(2):
        ch = f"yst{sub_}"
        if ch in em.cnt:
            em.retag([k for k in em.wtok if k.startswith("yscr:")
                      and em.wtok[k][0] == ch], (ch, em.cnt[ch]))
    load_nat(Wv, A_NATV, "natq")
    load_nat(Wo, A_NATO, "natk")
    transpose_weight(A_NATV, A_WVT)
    transpose_weight(A_NATO, A_WOT)

    def p3load_xn(ti):
        si, c0 = ti // T3PS, (ti % T3PS) * 256
        par = ti % 2
        u0 = c0 // 128
        xn3g = _whg(H3_XNF + par * 2048, H3_XNF + (par + 1) * 2048)
        op("sy", lambda e, si=si, c0=c0, par=par: e.dma_start(
            WH[:, H3_XNF + par * 2048:H3_XNF + (par + 1) * 2048].rearrange(
                "p (c t) -> p c t", t=256),
            xn_scr[si, :, :, c0:c0 + 256].rearrange("c p t -> p c t")),
            reads=(f"xnscr:{si}:{u0}", f"xnscr:{si}:{u0 + 1}"), writes=xn3g,
            ch=f"xn2l{par}")

    def p3load_sub(ti, sub):
        si, c0 = ti // T3PS, (ti % T3PS) * 256
        gsub = (si * S + c0) // 128 + sub
        spar = gsub % 2
        xg = _wfg(F3_X + spar * 1024, F3_X + (spar + 1) * 1024)
        ygd = _wfg(F3_Y + spar * 1024, F3_Y + (spar + 1) * 1024)
        op("sy", lambda e, gsub=gsub, spar=spar: e.dma_start(
            WF[:, F3_Y + spar * 1024:F3_Y + (spar + 1) * 1024],
            y_scr[gsub]), reads=(f"yscr:{gsub}",), writes=ygd,
            ch=f"yl{spar}")
        op("gp", lambda e, gsub=gsub, spar=spar: e.dma_start(
            WF[:, F3_Y + spar * 1024:F3_Y + (spar + 1) * 1024],
            yab_scr[gsub * 128:gsub * 128 + 128, :],
            accum_op=ALU.add),
            reads=("yab0", "yab1"), writes=ygd, ch=f"yaa{spar}")
        op("gp", lambda e, gsub=gsub, spar=spar: e.dma_start(
            WF[:, F3_Y + spar * 1024:F3_Y + (spar + 1) * 1024],
            yab_scr[NTOK + gsub * 128:NTOK + gsub * 128 + 128, :],
            accum_op=ALU.add),
            reads=("yab0", "yab1"), writes=ygd, ch=f"yab{spar}")
        op("sy", lambda e, si=si, c0=c0, sub=sub, spar=spar: e.dma_start(
            WF[:, F3_X + spar * 1024:F3_X + (spar + 1) * 1024],
            x_in[si, c0 + sub * 128:c0 + sub * 128 + 128, :]), writes=xg,
            ch=f"xl{spar}")

    p3load_xn(0)
    p3load_sub(0, 0)
    p3load_sub(0, 1)
    for ti in range(NT3):
        si, c0 = ti // T3PS, (ti % T3PS) * 256
        par = ti % 2
        xn3g = _whg(H3_XNF + par * 2048, H3_XNF + (par + 1) * 2048)
        if p3cut < 2:
            continue
        vg = _whg(H3_V, H3_V + 2048)
        for ic in range(DC):
            pp, pname = (pfa, "pfa") if ic % 2 == 0 else (pfb, "pfb")
            fns = []
            for c in range(DC):
                fns.append(lambda e, ic=ic, c=c, par=par, pp=pp: e.matmul(
                    pp[:, 0:256],
                    AR[:, A_WVT + c * 1024 + ic * 128:
                       A_WVT + c * 1024 + (ic + 1) * 128],
                    WH[:, H3_XNF + par * 2048 + c * 256:
                       H3_XNF + par * 2048 + (c + 1) * 256],
                    start=(c == 0), stop=(c == DC - 1)))
            grp("pe", fns, reads=xn3g + _arg(A_WVT, A_WVT + 8192),
                writes=(pname,))
            op("ac", lambda e, ic=ic, pp=pp, pname=pname, si=si: e.activation(
                WH[:, H3_V + ic * 256:H3_V + (ic + 1) * 256], pp[:, 0:256],
                AF.Copy, scale=RSM[:, si * 8 + ic:si * 8 + ic + 1]),
                reads=(pname, f"rsm:{si}"), writes=vg)
        if p3cut < 3:
            continue
        atg = _whg(H3_AT, H3_AT + 2048)
        for p in range(NPAIR):
            pp, pname = (pfa, "pfa") if p % 2 == 0 else (pfb, "pfb")
            fns = [lambda e, p=p, si=si, pp=pp: e.matmul(
                pp[:, 0:256],
                ASM[:, si * 1024 + p * 128:si * 1024 + (p + 1) * 128],
                WH[:, H3_V + p * 256:H3_V + (p + 1) * 256],
                start=True, stop=True)]
            grp("pe", fns, reads=(f"asm:{si}",) + vg, writes=(pname,))
            if p3cut == 30:
                continue
            op("ac", lambda e, p=p, pp=pp, pname=pname: e.activation(
                WH[:, H3_AT + p * 256:H3_AT + (p + 1) * 256],
                pp[:, 0:256], AF.Copy),
                reads=(pname,), writes=atg)
        if p3cut < 4 or p3cut == 30:
            continue
        for sub in range(2):
            gsub = (si * S + c0) // 128 + sub
            spar = gsub % 2
            pY, pYn = (pY0, "pY0") if sub == 0 else (pY1, "pY1")
            xg = _wfg(F3_X + spar * 1024, F3_X + (spar + 1) * 1024)
            ygd = _wfg(F3_Y + spar * 1024, F3_Y + (spar + 1) * 1024)
            og = _wfg(F3_O + spar * 1024, F3_O + (spar + 1) * 1024)
            fns = []
            for p in range(NPAIR):
                for h in range(2):
                    fns.append(lambda e, p=p, h=h, sub=sub, pY=pY: e.matmul(
                        pY[:, h * 512:(h + 1) * 512],
                        WH[:, H3_AT + p * 256 + sub * 128:
                           H3_AT + p * 256 + sub * 128 + 128],
                        AR[:, A_WOT + p * 1024 + h * 512:
                           A_WOT + p * 1024 + (h + 1) * 512],
                        start=(p == 0), stop=(p == NPAIR - 1)))
            grp("pe", fns, reads=atg + _arg(A_WOT, A_WOT + 8192), writes=(pYn,))
            if p3cut < 5:
                continue
            op("ac", lambda e, spar=spar: e.activation(
                WF[:, F3_O + spar * 1024:F3_O + (spar + 1) * 1024],
                WF[:, F3_Y + spar * 1024:F3_Y + (spar + 1) * 1024], AF.Silu),
                reads=ygd, writes=og)
            if p3cut < 6:
                continue
            op("ve", lambda e, spar=spar, pY=pY: e.tensor_tensor(
                WF[:, F3_O + spar * 1024:F3_O + (spar + 1) * 1024],
                WF[:, F3_O + spar * 1024:F3_O + (spar + 1) * 1024],
                pY[:], ALU.mult), reads=og + (pYn,), writes=og)
            op("ve", lambda e, spar=spar: e.tensor_tensor(
                WF[:, F3_O + spar * 1024:F3_O + (spar + 1) * 1024],
                WF[:, F3_O + spar * 1024:F3_O + (spar + 1) * 1024],
                WF[:, F3_X + spar * 1024:F3_X + (spar + 1) * 1024], ALU.add),
                reads=og + xg, writes=og)
            op("gp", lambda e, si=si, c0=c0, sub=sub, spar=spar: e.dma_start(
                out[si, c0 + sub * 128:c0 + sub * 128 + 128, :],
                WF[:, F3_O + spar * 1024:F3_O + (spar + 1) * 1024]),
                reads=og, ch=f"outst{spar}")
            if ti + 1 < NT3:
                if sub == 0:
                    p3load_xn(ti + 1)
                p3load_sub(ti + 1, sub)

    if dbg:
        _dbg_dump(nc, em, op, locals())
    _finish(em)
    return _realize(nc, em)


def _dbg_dump(nc, em, op, L):
    xm_scr, idx_scr, yab_scr, y_scr, gate_scr = (
        L["xm_scr"], L["idx_scr"], L["yab_scr"], L["y_scr"], L["gate_scr"])
    for name, ap, shp, dt in (
            ("d_xm", xm_scr, list(xm_scr.shape), F32),
            ("d_idx", idx_scr, list(idx_scr.shape), mybir.dt.int32),
            ("d_yab", yab_scr, list(yab_scr.shape), F32),
            ("d_ysc", y_scr, list(y_scr.shape), F32),
            ("d_gate", gate_scr, list(gate_scr.shape), F32)):
        o = nc.dram_tensor(name, shp, dt, kind="ExternalOutput").ap()
        op("gp", lambda e, o=o, ap=ap: e.dma_start(o, ap),
            reads=(), writes=(), ch="dbgd")


def _finish(em):
    for ch, v in sorted(em.cnt.items()):
        if ch in ("pe", "ve", "ac"):
            continue
        em._wait("gp", (ch, v))
        em._wait("sy", (ch, v))


def _realize(nc, em):
    with nc.Block() as block:
        block.sync(lambda e: [f(e) for f in em.sched["sy"]])
        block.gpsimd(lambda e: [f(e) for f in em.sched["gp"]])
        block.tensor(lambda e: [f(e) for f in em.sched["pe"]])
        block.vector(lambda e: [f(e) for f in em.sched["ve"]])
        block.scalar(lambda e: [f(e) for f in em.sched["ac"]])
    return nc


def kernel(**inputs):
    f = np.float32
    x = np.ascontiguousarray(np.asarray(inputs["L0_attn_x"], dtype=f))
    rs = np.asarray(inputs["L0_ffn_route_scale"], dtype=f).reshape(1, 1)
    rs_col = np.ascontiguousarray(np.broadcast_to(rs, (NE, 1)))
    shared = {}
    for k in ("L0_attn_Wq", "L0_attn_Wk", "L0_attn_Wv", "L0_attn_Wo",
              "L0_ffn_gate_W", "L0_ffn_W1", "L0_ffn_b1", "L0_ffn_W2",
              "L0_ffn_b2", "L0_ffn_Ws1", "L0_ffn_bs1", "L0_ffn_Ws2",
              "L0_ffn_bs2"):
        shared[k] = np.ascontiguousarray(np.asarray(inputs[k], dtype=f))
    nseq = x.shape[0] // NCORES
    S = x.shape[1]
    nc = build_nc(nseq, S)
    in_maps = []
    for c in range(NCORES):
        m = dict(shared)
        m["L0_attn_x"] = np.ascontiguousarray(x[c * nseq:(c + 1) * nseq])
        m["route_scale_col"] = rs_col
        in_maps.append(m)
    res = run_bass_kernel_spmd(nc, in_maps, core_ids=list(range(NCORES)))
    outs = [np.asarray(r["out"]) for r in res.results]
    return np.concatenate(outs, axis=0).astype(np.float32)


# version 15
# speedup vs baseline: 1.2261x; 1.1728x over previous
import numpy as np
import ml_dtypes
import concourse.bass as bass
import concourse.mybir as mybir
from concourse.bass_utils import run_bass_kernel_spmd

F32 = mybir.dt.float32
BF16 = mybir.dt.bfloat16
AF = mybir.ActivationFunctionType
ALU = mybir.AluOpType
AX = mybir.AxisListType.X

NCORES = 8
D = 1024
DC = 8
NH = 16
NPAIR = 8
NE = 8
HF = 512
JCN = 4
EPS = 1e-5
BIG = 1.0e30

ENGS = ("sy", "gp", "pe", "ve", "ac")
DMA_ENGS = ("sy", "gp")


class Em:

    def __init__(self, nc):
        self.nc = nc
        self.sched = {k: [] for k in ENGS}
        self.cnt = {}
        self.sems = {}
        for k in ("pe", "ve", "ac"):
            self.sems[k] = nc.alloc_semaphore(f"sem_{k}")
            self.cnt[k] = 0
        self.waited = {k: {} for k in ENGS}
        self.wtok = {}
        self.rtok = {}

    def _sem(self, key):
        if key not in self.sems:
            self.sems[key] = self.nc.alloc_semaphore(f"sem_{key}")
            self.cnt[key] = 0
        return self.sems[key]

    def _wait(self, eng, tok):
        if tok is None:
            return
        src, v = tok
        if self.waited[eng].get(src, 0) >= v:
            return
        self.waited[eng][src] = v
        sem = self.sems[src]
        self.sched[eng].append(lambda e, sem=sem, v=v: e.wait_ge(sem, v))

    def _pre(self, eng, reads, writes):
        for r in reads:
            self._wait(eng, self.wtok.get(r))
        for w in writes:
            self._wait(eng, self.wtok.get(w))
            for src, v in self.rtok.get(w, {}).items():
                self._wait(eng, (src, v))

    def _post(self, tok, eng, reads, writes):
        for w in writes:
            self.wtok[w] = tok
            self.rtok[w] = {}
        for r in reads:
            d = self.rtok.setdefault(r, {})
            if d.get(tok[0], 0) < tok[1]:
                d[tok[0]] = tok[1]

    def op(self, eng, fn, reads=(), writes=(), ch=None):
        assert (eng in DMA_ENGS) == (ch is not None)
        self._pre(eng, reads, writes)
        key = ch if ch is not None else eng
        sem = self._sem(key) if ch is not None else self.sems[key]
        step = 16 if ch is not None else 1
        self.cnt[key] = self.cnt.get(key, 0) + step
        v = self.cnt[key]
        self.sched[eng].append(
            lambda e, fn=fn, sem=sem, step=step: fn(e).then_inc(sem, step))
        tok = (key, v)
        self._post(tok, eng, reads, writes)
        return tok

    def group(self, eng, fns, reads=(), writes=()):
        assert eng not in DMA_ENGS
        self._pre(eng, reads, writes)
        for fn in fns[:-1]:
            self.sched[eng].append(fn)
        self.cnt[eng] += 1
        v = self.cnt[eng]
        sem = self.sems[eng]
        fn = fns[-1]
        self.sched[eng].append(
            lambda e, fn=fn, sem=sem: fn(e).then_inc(sem, 1))
        tok = (eng, v)
        self._post(tok, eng, reads, writes)
        return tok

    def dma_series(self, eng, ch, items):
        allr, allw = [], []
        for fn, rd, wr in items:
            allr += list(rd)
            allw += list(wr)
        self._pre(eng, tuple(allr), tuple(allw))
        sem = self._sem(ch)
        tok = None
        for fn, rd, wr in items:
            self.cnt[ch] += 16
            v = self.cnt[ch]
            self.sched[eng].append(
                lambda e, fn=fn, sem=sem: fn(e).then_inc(sem, 16))
            tok = (ch, v)
        self._post(tok, eng, tuple(allr), tuple(allw))
        return tok

    def retag(self, bufs, tok):
        for b in bufs:
            self.wtok[b] = tok


def self_cnt(em):
    return em.cnt


def _wfg(c0, c1):
    return tuple(f"wf:{g}" for g in range(c0 // 1024, (c1 + 1023) // 1024))


def _whg(c0, c1):
    return tuple(f"wh:{g}" for g in range(c0 // 1024, (c1 + 1023) // 1024))


def _arg(c0, c1):
    return tuple(f"ar:{g}" for g in range(c0 // 1024, (c1 + 1023) // 1024))


def build_nc(nseq, S, phases=3, p3cut=9, dbg=False):
    nc = bass.Bass("TRN2", target_bir_lowering=False, detect_race_conditions=False)
    NSUB = S // 128
    NTOK = nseq * S
    CAP = min(((NTOK // 4 + 512) + 255) // 256 * 256,
              ((NTOK + 255) // 256) * 256)
    NUNIT = CAP // 256
    T2PS = S // 256
    NT2 = nseq * T2PS
    T3PS = S // 256
    NT3 = nseq * T3PS

    x_in = nc.dram_tensor("L0_attn_x", [nseq, S, D], F32, kind="ExternalInput").ap()
    Wq = nc.dram_tensor("L0_attn_Wq", [D, D], F32, kind="ExternalInput").ap()
    Wk = nc.dram_tensor("L0_attn_Wk", [D, D], F32, kind="ExternalInput").ap()
    Wv = nc.dram_tensor("L0_attn_Wv", [D, D], F32, kind="ExternalInput").ap()
    Wo = nc.dram_tensor("L0_attn_Wo", [D, D], F32, kind="ExternalInput").ap()
    gW = nc.dram_tensor("L0_ffn_gate_W", [D, NE], F32, kind="ExternalInput").ap()
    W1 = nc.dram_tensor("L0_ffn_W1", [NE, D, HF], F32, kind="ExternalInput").ap()
    b1 = nc.dram_tensor("L0_ffn_b1", [NE, HF], F32, kind="ExternalInput").ap()
    W2 = nc.dram_tensor("L0_ffn_W2", [NE, HF, D], F32, kind="ExternalInput").ap()
    b2 = nc.dram_tensor("L0_ffn_b2", [NE, D], F32, kind="ExternalInput").ap()
    Ws1 = nc.dram_tensor("L0_ffn_Ws1", [D, HF], F32, kind="ExternalInput").ap()
    bs1 = nc.dram_tensor("L0_ffn_bs1", [HF], F32, kind="ExternalInput").ap()
    Ws2 = nc.dram_tensor("L0_ffn_Ws2", [HF, D], F32, kind="ExternalInput").ap()
    bs2 = nc.dram_tensor("L0_ffn_bs2", [D], F32, kind="ExternalInput").ap()
    rs_in = nc.dram_tensor("route_scale_col", [NE, 1], F32, kind="ExternalInput").ap()
    out = nc.dram_tensor("out", [nseq, S, D], F32, kind="ExternalOutput").ap()

    xn_scr = nc.dram_tensor("xn_scr", [nseq, DC, 128, S], BF16).ap()
    xm_scr = nc.dram_tensor("xm_scr", [NTOK, 520], F32).ap()
    idx_scr = nc.dram_tensor("idx_scr", [8 * CAP, 1], mybir.dt.int32).ap()
    yab_scr = nc.dram_tensor("yab_scr", [2 * NTOK + 128, D], F32).ap()
    y_scr = nc.dram_tensor("y_scr", [nseq * S // 128, 128, D], F32).ap()
    gate_scr = nc.dram_tensor("gate_scr", [nseq, NE, S], F32).ap()

    ident_bf_c = nc.inline_tensor(np.eye(128, dtype=ml_dtypes.bfloat16), name="c_idbf").ap()
    ident_f_c = nc.inline_tensor(np.eye(128, dtype=np.float32), name="c_idf").ap()
    iota128_c = nc.inline_tensor(
        np.arange(128, dtype=np.float32).reshape(128, 1), name="c_iota").ap()
    iote_c = nc.inline_tensor(
        np.broadcast_to(np.arange(8, dtype=np.float32), (128, 8)).copy(),
        name="c_iote").ap()
    ecap_c = nc.inline_tensor(
        np.broadcast_to(np.arange(8, dtype=np.float32) * CAP - 1.0,
                        (128, 8)).copy(), name="c_ecap").ap()
    bigidx_c = nc.inline_tensor(
        np.full((128, 8 * NUNIT), 1 << 30, dtype=np.int32), name="c_big").ap()

    AR = nc.alloc_sbuf_tensor("AR", [128, 73728], BF16)
    WH = nc.alloc_sbuf_tensor("WH", [128, 8192], BF16)
    WF = nc.alloc_sbuf_tensor("WF", [128, 6144], F32)
    ASM = nc.alloc_sbuf_tensor("ASM", [128, nseq * 1024], BF16)
    RSM = nc.alloc_sbuf_tensor("RSM", [128, nseq * 8], F32)
    idbf = nc.alloc_sbuf_tensor("idbf", [128, 128], BF16)
    idf = nc.alloc_sbuf_tensor("idf", [128, 128], F32)
    iota128 = nc.alloc_sbuf_tensor("iota128", [128, 1], F32)
    iote = nc.alloc_sbuf_tensor("iote", [128, 8], F32)
    ecap = nc.alloc_sbuf_tensor("ecap", [128, 8], F32)
    T2B = nc.alloc_sbuf_tensor("T2B", [128, 72], F32)
    EQS = nc.alloc_sbuf_tensor("EQS", [128, 16 * NSUB], F32)
    META = nc.alloc_sbuf_tensor("META", [128, 16], F32)
    IDXS = nc.alloc_sbuf_tensor("IDXS", [128, 8 * CAP // 128], mybir.dt.int32)
    TKI = nc.alloc_sbuf_tensor("TKI", [128, 12], mybir.dt.int32)
    BASEE = nc.alloc_sbuf_tensor("BASEE", [8, 2], F32)
    gwsb = nc.alloc_sbuf_tensor("gwsb", [128, DC * NE], BF16)
    b1T = nc.alloc_sbuf_tensor("b1T", [128, JCN * NE], F32)
    bs1T = nc.alloc_sbuf_tensor("bs1T", [128, JCN], F32)
    rscol = nc.alloc_sbuf_tensor("rscol", [NE, 1], F32)
    b2all = nc.alloc_sbuf_tensor("b2all", [9, D], F32)
    GT9 = nc.alloc_sbuf_tensor("GT9", [9, 512], F32)
    STT = nc.alloc_sbuf_tensor("STT", [128, 24], F32)
    SMX = nc.alloc_sbuf_tensor("SMX", [128, 64], F32)
    ASB = nc.alloc_sbuf_tensor("ASB", [128, 1024], F32)

    A_W1, A_W2, A_WS1, A_WS2 = 0, 32768, 65536, 69632
    A_WQT, A_WKT, A_NAT1, A_NAT2 = 32768, 40960, 49152, 57344
    A_NATV, A_NATO, A_WVT, A_WOT = 0, 8192, 16384, 24576

    H_XNTM, H_XNF, H_Q, H_K = 0, 3072, 6144, 7168
    H2_XNF, H2_HSC, H2_XNG = 0, 4096, 6144
    H3_XNF, H3_V, H3_AT = 0, 4096, 6144

    F_X1, F_SC2, F_SCORES, F_GFM = 0, 3072, 4096, 4096
    F2_XG0, F2_Y, F2_YS, F2_XG1 = 0, 1024, 3072, 5120
    F2_B1N, F2_BS1N = 4096, 4608
    F3_X, F3_Y, F3_O = 0, 2048, 4096
    T2_RAW, T2_EQ1, T2_MSK, T2_EQ2 = 3072, 3200, 3328, 3456
    T2_GA, T2_GB, T2_GTM, T2_S = 3584, 3712, 3840, 3968

    pY0 = nc.alloc_psum_tensor("pY0", [128, 1024], F32)
    pY1 = nc.alloc_psum_tensor("pY1", [128, 1024], F32)
    pfa = nc.alloc_psum_tensor("pfa", [128, 512], F32)
    pfb = nc.alloc_psum_tensor("pfb", [128, 512], F32)
    pG = nc.alloc_psum_tensor("pG", [128, 512], F32)
    pbf = nc.alloc_psum_tensor("pbf", [128, 1024], BF16)

    em = Em(nc)
    op, grp = em.op, em.group
    rb_idx = nc.alloc_register(mybir.EngineType.Pool, "rb_idx")
    rb_tok = nc.alloc_register(mybir.EngineType.Pool, "rb_tok")
    rb_yab = nc.alloc_register(mybir.EngineType.Pool, "rb_yab")
    em.sched["gp"].append(lambda e: e.reg_mov(rb_idx, 8 * CAP - 1))
    em.sched["gp"].append(lambda e: e.reg_mov(rb_tok, NTOK - 1))
    em.sched["gp"].append(lambda e: e.reg_mov(rb_yab, 2 * NTOK - 1))

    em.dma_series("sy", "csml", [
        (lambda e: e.dma_start(idbf[:], ident_bf_c), (), ("idbf",)),
        (lambda e: e.dma_start(idf[:], ident_f_c), (), ("idf",)),
        (lambda e: e.dma_start(iota128[:], iota128_c), (), ("iota128",)),
        (lambda e: e.dma_start(iote[:], iote_c), (), ("iote",)),
        (lambda e: e.dma_start(ecap[:], ecap_c), (), ("ecap",)),
        (lambda e: e.dma_start(rscol[:], rs_in), (), ("rscol",)),
        (lambda e: e.dma_start(WF[0:NE, F2_B1N:F2_B1N + HF], b1), (),
         _wfg(F2_B1N, F2_B1N + HF)),
        (lambda e: e.dma_start(WF[0:1, F2_BS1N:F2_BS1N + HF],
                               bs1.rearrange("(o j) -> o j", o=1)), (),
         _wfg(F2_BS1N, F2_BS1N + HF)),
    ])
    op("gp", lambda e: e.dma_start(
        gwsb[:], gW.rearrange("(dc p) e -> p dc e", p=128)),
        writes=("gwsb",), ch="gwl")
    op("gp", lambda e: e.dma_start(IDXS[:, 0:8 * NUNIT], bigidx_c),
        writes=("idxbig",), ch="ibig")
    HALF_IDX = 4 * CAP
    for hh_ in range(2):
        op("gp", lambda e, hh_=hh_: e.dma_start(
            idx_scr[hh_ * HALF_IDX:(hh_ + 1) * HALF_IDX].rearrange(
                "(p k) o -> p k o", p=128),
            IDXS[:, 0:8 * NUNIT].rearrange("p (k o) -> p k o", o=1)),
            reads=("idxbig",), writes=("idxscr",), ch="idxp")
    op("ve", lambda e: e.memset(BASEE[:], 0.0), writes=("basee",))
    op("ve", lambda e: e.memset(META[:], 0.0), writes=("meta0", "meta1"))

    for jc in range(JCN):
        op("pe", lambda e, jc=jc: e.transpose(
            pG[0:128, 0:NE],
            WF[0:NE, F2_B1N + jc * 128:F2_B1N + (jc + 1) * 128],
            idf[0:NE, 0:NE]),
            reads=_wfg(F2_B1N, F2_B1N + HF) + ("idf",), writes=("pG:0",))
        op("ve", lambda e, jc=jc: e.tensor_copy(
            b1T[:, jc * 8:(jc + 1) * 8], pG[0:128, 0:NE]),
            reads=("pG:0",), writes=("b1T",))
    for jc in range(JCN):
        op("pe", lambda e, jc=jc: e.transpose(
            pG[0:128, 8:9],
            WF[0:1, F2_BS1N + jc * 128:F2_BS1N + (jc + 1) * 128],
            idf[0:1, 0:1]),
            reads=_wfg(F2_BS1N, F2_BS1N + HF) + ("idf",), writes=("pG:0",))
        op("ve", lambda e, jc=jc: e.tensor_copy(
            bs1T[:, jc:jc + 1], pG[0:128, 8:9]), reads=("pG:0",), writes=("bs1T",))

    def load_nat(dram, base, ch):
        op("gp", lambda e, dram=dram, base=base: e.dma_start(
            AR[:, base:base + 8192].rearrange("p (ic d) -> p ic d", d=D),
            dram.rearrange("(ic p) d -> p ic d", p=128)),
            writes=_arg(base, base + 8192), ch=ch)

    def transpose_weight(natbase, dstbase):
        for dc in range(DC):
            fns = []
            for ic in range(DC):
                fns.append(lambda e, dc=dc, ic=ic: e.transpose(
                    pbf[:, ic * 128:(ic + 1) * 128],
                    AR[:, natbase + ic * 1024 + dc * 128:
                       natbase + ic * 1024 + (dc + 1) * 128],
                    idbf[:]))
            grp("pe", fns, reads=_arg(natbase, natbase + 8192) + ("idbf",),
                writes=("pbf",))
            op("ac", lambda e, dc=dc: e.activation(
                AR[:, dstbase + dc * 1024:dstbase + (dc + 1) * 1024], pbf[:],
                AF.Copy), reads=("pbf",),
                writes=_arg(dstbase + dc * 1024, dstbase + (dc + 1) * 1024))

    op("ve", lambda e: e.memset(ASM[:], 0.0),
        writes=tuple(f"asm:{s}" for s in range(nseq)))
    load_nat(Wq, A_NAT1, "natq")
    load_nat(Wk, A_NAT2, "natk")
    transpose_weight(A_NAT1, A_WQT)
    transpose_weight(A_NAT2, A_WKT)

    w1items = []
    for e_ in range(NE):
        w1items.append((lambda e, e_=e_: e.dma_start(
            AR[:, A_W1 + e_ * 4096:A_W1 + (e_ + 1) * 4096].rearrange(
                "p (dc j) -> p dc j", j=HF),
            W1[e_].rearrange("(dc p) j -> p dc j", p=128)), (),
            _arg(A_W1 + e_ * 4096, A_W1 + (e_ + 1) * 4096)))
    w1items.append((lambda e: e.dma_start(
        AR[:, A_WS1:A_WS1 + 4096].rearrange("p (dc j) -> p dc j", j=HF),
        Ws1.rearrange("(dc p) j -> p dc j", p=128)), (),
        _arg(A_WS1, A_WS1 + 4096)))
    w1items.append((lambda e: e.dma_start(
        AR[:, A_WS2:A_WS2 + 4096].rearrange("p (jc d) -> p jc d", d=D),
        Ws2.rearrange("(jc p) d -> p jc d", p=128)), (),
        _arg(A_WS2, A_WS2 + 4096)))
    em.dma_series("gp", "w1s", w1items)

    if phases < 1:
        _finish(em)
        return _realize(nc, em)
    def stat(p3, k):
        return STT[:, p3 * 8 + k:p3 * 8 + k + 1]

    def lnpart(s, u):
        p3 = u % 3
        t0 = u * 128
        xg = _wfg(F_X1 + p3 * 1024, F_X1 + (p3 + 1) * 1024)
        xap = WF[:, F_X1 + p3 * 1024:F_X1 + (p3 + 1) * 1024]
        stn = f"st:{p3}"
        op("sy", lambda e: e.dma_start(
            xap, x_in[s, t0:t0 + 128, :]), writes=xg, ch=f"xl{p3}")
        op("ve", lambda e: e.reduce_sum(
            stat(p3, 0), xap, axis=AX), reads=xg, writes=(stn,))
        op("ac", lambda e: e.activation(
            WF[:, F_SC2:F_SC2 + 1024], xap, AF.Square,
            accum_out=stat(p3, 1)),
            reads=xg, writes=_wfg(F_SC2, F_SC2 + 1024) + (stn,))
        op("ve", lambda e: e.tensor_scalar(
            stat(p3, 2), stat(p3, 0), 1.0 / D, None, ALU.mult),
            reads=(stn,), writes=(stn,))
        op("ve", lambda e: e.tensor_scalar(
            stat(p3, 3), stat(p3, 1), 1.0 / D, None, ALU.mult),
            reads=(stn,), writes=(stn,))
        op("ve", lambda e: e.scalar_tensor_tensor(
            stat(p3, 4), stat(p3, 2), -1.0, stat(p3, 2),
            ALU.mult, ALU.mult), reads=(stn,), writes=(stn,))
        op("ve", lambda e: e.tensor_add(
            stat(p3, 3), stat(p3, 3), stat(p3, 4)),
            reads=(stn,), writes=(stn,))
        op("ac", lambda e: e.activation(
            stat(p3, 5), stat(p3, 3), AF.Ln, bias=np.float32(EPS)),
            reads=(stn,), writes=(stn,))
        op("ac", lambda e: e.activation(
            stat(p3, 6), stat(p3, 5), AF.Exp, scale=np.float32(-0.5)),
            reads=(stn,), writes=(stn,))
        op("ve", lambda e: e.tensor_scalar(
            WH[:, H_XNTM + p3 * 1024:H_XNTM + (p3 + 1) * 1024],
            xap, stat(p3, 2), stat(p3, 6), ALU.subtract, ALU.mult),
            reads=xg + (stn,),
            writes=_whg(H_XNTM + p3 * 1024, H_XNTM + (p3 + 1) * 1024))

    def txpart(s, u):
        p3 = u % 3
        t0 = u * 128
        fns = []
        for c in range(DC):
            fns.append(lambda e, c=c, p3=p3: e.transpose(
                pbf[:, c * 128:(c + 1) * 128],
                WH[:, H_XNTM + p3 * 1024 + c * 128:
                   H_XNTM + p3 * 1024 + (c + 1) * 128], idbf[:]))
        grp("pe", fns,
            reads=_whg(H_XNTM + p3 * 1024, H_XNTM + (p3 + 1) * 1024) + ("idbf",),
            writes=("pbf",))
        xnfg = _whg(H_XNF + p3 * 1024, H_XNF + (p3 + 1) * 1024)
        op("ac", lambda e: e.activation(
            WH[:, H_XNF + p3 * 1024:H_XNF + (p3 + 1) * 1024], pbf[:],
            AF.Copy), reads=("pbf",), writes=xnfg)
        op("gp", lambda e: e.dma_start(
            xn_scr[s, :, :, t0:t0 + 128].rearrange("c p t -> p c t"),
            WH[:, H_XNF + p3 * 1024:H_XNF + (p3 + 1) * 1024].rearrange(
                "p (c t) -> p c t", t=128)),
            reads=xnfg, writes=(f"xnscr:{s}:{u}",), ch=f"xnst{p3}")
        op("gp", lambda e: e.dma_start(
            xm_scr[s * S + t0:s * S + t0 + 128, 0:512],
            WH[:, H_XNTM + p3 * 1024:H_XNTM + (p3 + 1) * 1024].bitcast(F32)),
            reads=_whg(H_XNTM + p3 * 1024, H_XNTM + (p3 + 1) * 1024),
            writes=(f"xmscr:{s}:{u}",), ch=f"xmst{u % 2}")

    def pepart(s, u):
        p3 = u % 3
        t0 = u * 128
        xnfg = _whg(H_XNF + p3 * 1024, H_XNF + (p3 + 1) * 1024)
        xnf0 = H_XNF + p3 * 1024
        fns = []
        for h in range(2):
            for c in range(DC):
                fns.append(lambda e, h=h, c=c: e.matmul(
                    pY1[:, h * 512:(h + 1) * 512],
                    WH[:, xnf0 + c * 128:xnf0 + (c + 1) * 128],
                    AR[:, A_WQT + c * 1024 + h * 512:
                       A_WQT + c * 1024 + (h + 1) * 512],
                    start=(c == 0), stop=(c == DC - 1)))
        grp("pe", fns, reads=xnfg + _arg(A_WQT, A_WQT + 8192), writes=("pY1",))
        qg = _whg(H_Q, H_Q + 1024)
        op("ac", lambda e: e.activation(
            WH[:, H_Q:H_Q + 1024], pY1[:],
            AF.Copy, scale=np.float32(0.125)), reads=("pY1",), writes=qg)
        kg = _whg(H_K, H_K + 1024)
        for h, pp, pname in ((0, pfa, "pfa"), (1, pfb, "pfb")):
            fns = []
            for c in range(DC):
                fns.append(lambda e, h=h, c=c, pp=pp: e.matmul(
                    pp[:],
                    WH[:, xnf0 + c * 128:xnf0 + (c + 1) * 128],
                    AR[:, A_WKT + c * 1024 + h * 512:
                       A_WKT + c * 1024 + (h + 1) * 512],
                    start=(c == 0), stop=(c == DC - 1)))
            grp("pe", fns, reads=xnfg + _arg(A_WKT, A_WKT + 8192),
                writes=(pname,))
            op("ac", lambda e, h=h, pp=pp: e.activation(
                WH[:, H_K + h * 512:H_K + (h + 1) * 512], pp[:], AF.Copy),
                reads=(pname,), writes=kg)
    def pepart_a(s, u):
        qg = _whg(H_Q, H_Q + 1024)
        kg = _whg(H_K, H_K + 1024)
        fns = []
        for p in range(NPAIR):
            fns.append(lambda e, p=p: e.matmul(
                pY0[:, p * 128:(p + 1) * 128],
                WH[:, H_K + p * 128:H_K + (p + 1) * 128],
                WH[:, H_Q + p * 128:H_Q + (p + 1) * 128],
                start=True, stop=True, skip_group_check=True))
        grp("pe", fns, reads=kg + qg, writes=("pY0",))
        if u == 0:
            op("ve", lambda e: e.tensor_copy(ASB[:], pY0[:]),
                reads=("pY0",), writes=("asb",))
        else:
            op("ve", lambda e: e.tensor_add(ASB[:], ASB[:], pY0[:]),
                reads=("pY0", "asb"), writes=("asb",))
    def gatemm(s, u):
        t0 = u * 128
        p3 = u % 3
        xnfg = _whg(H_XNF + p3 * 1024, H_XNF + (p3 + 1) * 1024)
        xnf0 = H_XNF + p3 * 1024
        fns = []
        for c in range(DC):
            fns.append(lambda e, c=c: e.matmul(
                pG[0:NE, 0:128], gwsb[:, c * 8:(c + 1) * 8],
                WH[:, xnf0 + c * 128:xnf0 + (c + 1) * 128],
                start=(c == 0), stop=(c == DC - 1)))
        grp("pe", fns, reads=xnfg + ("gwsb",), writes=("pG:0",))
        op("ac", lambda e: e.activation(
            b2all[0:NE, 0:128], pG[0:NE, 0:128], AF.Exp, scale=np.float32(-1.0)),
            reads=("pG:0",), writes=("sig1",))
        op("ve", lambda e: e.tensor_scalar(
            b2all[0:NE, 128:256], b2all[0:NE, 0:128], 1.0, None, ALU.add),
            reads=("sig1",), writes=("sig2",))
        op("ve", lambda e: e.reciprocal(
            b2all[0:NE, 256:384], b2all[0:NE, 128:256]),
            reads=("sig2",), writes=("sig3",))
        op("ve", lambda e: e.tensor_scalar(
            WF[0:NE, F_SCORES + t0:F_SCORES + t0 + 128],
            b2all[0:NE, 256:384], rscol[:], None, ALU.mult),
            reads=("sig3", "rscol"),
            writes=_wfg(F_SCORES + t0, F_SCORES + t0 + 128))
    def gatepart(s, u):
        t0 = u * 128
        scg = _wfg(F_SCORES + t0, F_SCORES + t0 + 128)
        mpar = u % 2
        mslc = lambda a, b: META[:, mpar * 8 + a:mpar * 8 + b]
        tb = lambda a, b: T2B[:, a:b]
        op("pe", lambda e: e.transpose(
            pfa[:, 0:8], WF[0:NE, F_SCORES + t0:F_SCORES + t0 + 128],
            idf[0:NE, 0:NE]), reads=scg + ("idf",), writes=("pfa",))
        op("ve", lambda e: e.tensor_copy(tb(0, 8), pfa[:, 0:8]),
            reads=("pfa",), writes=("t2raw",))
        op("ve", lambda e: e.reduce_max(tb(32, 33), tb(0, 8), axis=AX),
            reads=("t2raw",), writes=("t2m",))
        op("ve", lambda e: e.tensor_scalar(
            tb(8, 16), tb(0, 8), tb(32, 33), None, ALU.is_equal),
            reads=("t2raw", "t2m"), writes=("t2eq1",))
        op("ve", lambda e: e.tensor_tensor_scan(
            tb(16, 24), tb(8, 16), tb(8, 16), 0.0, ALU.add, ALU.bypass),
            reads=("t2eq1",), writes=("t2msk",))
        op("ve", lambda e: e.tensor_scalar(
            tb(16, 24), tb(16, 24), 1.0, None, ALU.is_equal),
            reads=("t2msk",), writes=("t2msk",))
        op("ve", lambda e: e.tensor_mul(tb(8, 16), tb(8, 16), tb(16, 24)),
            reads=("t2eq1", "t2msk"), writes=("t2eq1",))
        op("ve", lambda e: e.scalar_tensor_tensor(
            tb(16, 24), tb(8, 16), -BIG, tb(0, 8), ALU.mult, ALU.add),
            reads=("t2eq1", "t2raw"), writes=("t2msk",))
        op("ve", lambda e: e.reduce_max(tb(33, 34), tb(16, 24), axis=AX),
            reads=("t2msk",), writes=("t2m",))
        op("ve", lambda e: e.tensor_scalar(
            tb(24, 32), tb(16, 24), tb(33, 34), None, ALU.is_equal),
            reads=("t2msk", "t2m"), writes=("t2eq2",))
        op("ve", lambda e: e.tensor_tensor_scan(
            tb(16, 24), tb(24, 32), tb(24, 32), 0.0, ALU.add, ALU.bypass),
            reads=("t2eq2",), writes=("t2msk",))
        op("ve", lambda e: e.tensor_scalar(
            tb(16, 24), tb(16, 24), 1.0, None, ALU.is_equal),
            reads=("t2msk",), writes=("t2msk",))
        op("ve", lambda e: e.tensor_mul(tb(24, 32), tb(24, 32), tb(16, 24)),
            reads=("t2eq2", "t2msk"), writes=("t2eq2",))
        op("ve", lambda e: e.tensor_add(tb(34, 35), tb(32, 33), tb(33, 34)),
            reads=("t2m",), writes=("t2den",))
        op("ve", lambda e: e.reciprocal(tb(35, 36), tb(34, 35)),
            reads=("t2den",), writes=("t2rde",))
        op("ve", lambda e: e.tensor_mul(mslc(0, 1), tb(32, 33), tb(35, 36)),
            reads=("t2m", "t2rde"), writes=(f"meta{mpar}",))
        op("ve", lambda e: e.tensor_mul(mslc(1, 2), tb(33, 34), tb(35, 36)),
            reads=("t2m", "t2rde"), writes=(f"meta{mpar}",))
        op("ve", lambda e: e.tensor_mul(tb(16, 24), tb(8, 16), iote[:]),
            reads=("t2eq1", "iote"), writes=("t2msk",))
        op("ve", lambda e: e.reduce_sum(mslc(2, 3), tb(16, 24), axis=AX),
            reads=("t2msk",), writes=(f"meta{mpar}",))
        op("ve", lambda e: e.tensor_mul(tb(16, 24), tb(24, 32), iote[:]),
            reads=("t2eq2", "iote"), writes=("t2msk",))
        op("ve", lambda e: e.reduce_sum(mslc(3, 4), tb(16, 24), axis=AX),
            reads=("t2msk",), writes=(f"meta{mpar}",))
        op("ve", lambda e: e.tensor_scalar(
            mslc(4, 5), iota128[:], float(s * S + t0), None, ALU.add),
            reads=("iota128",), writes=(f"meta{mpar}",))
        op("ve", lambda e: e.tensor_copy(EQS[:, u * 16:u * 16 + 8], tb(8, 16)),
            reads=("t2eq1",), writes=("eqs",))
        op("ve", lambda e: e.tensor_copy(EQS[:, u * 16 + 8:u * 16 + 16], tb(24, 32)),
            reads=("t2eq2",), writes=("eqs",))
        op("ve", lambda e: e.tensor_scalar(
            tb(0, 8), tb(8, 16), mslc(0, 1), None, ALU.mult),
            reads=("t2eq1", f"meta{mpar}"), writes=("t2raw",))
        op("ve", lambda e: e.tensor_scalar(
            tb(16, 24), tb(24, 32), mslc(1, 2), None, ALU.mult),
            reads=("t2eq2", f"meta{mpar}"), writes=("t2msk",))
        op("ve", lambda e: e.tensor_add(tb(0, 8), tb(0, 8), tb(16, 24)),
            reads=("t2raw", "t2msk"), writes=("t2raw",))
        op("gp", lambda e: e.dma_start(
            xm_scr[s * S + t0:s * S + t0 + 128, 512:520],
            META[:, mpar * 8:mpar * 8 + 8]),
            reads=(f"meta{mpar}",), writes=(f"xmscr2:{s}:{u}",),
            ch=f"xmm{mpar}")

    def gatepart_b(s, u):
        t0 = u * 128
        tb = lambda a, b: T2B[:, a:b]
        op("pe", lambda e: e.transpose(
            pG[0:NE, 128:256], tb(0, 8), idf[:]),
            reads=("t2raw", "idf", "pG:0"), writes=("pG:1",))
        gfg = _wfg(F_GFM + t0, F_GFM + t0 + 128)
        op("ac", lambda e: e.activation(
            WF[0:NE, F_GFM + t0:F_GFM + t0 + 128], pG[0:NE, 128:256],
            AF.Copy), reads=("pG:1", "pG:0"), writes=gfg)

    def idxpart(s, u):
        t0 = u * 128
        gfall = _wfg(F_GFM, F_GFM + S)
        mp = u % 2
        tki0 = 4 * (mp + 1)
        op("pe", lambda e, t0=t0: e.transpose(
            pY1[:, 0:8], WF[0:NE, F_GFM + t0:F_GFM + t0 + 128],
            idf[0:NE, 0:NE]), reads=gfall + ("idf",), writes=("pY1",))
        op("ac", lambda e: e.activation(T2B[:, 36:44], pY1[:, 0:8], AF.Copy),
            reads=("pY1",), writes=("eppos",))
        op("ve", lambda e: e.tensor_scalar(
            T2B[:, 44:52], T2B[:, 36:44], float(CAP) + 0.5, None, ALU.is_ge),
            reads=("eppos",), writes=("epge",))
        op("ve", lambda e: e.tensor_add(
            T2B[:, 52:60], T2B[:, 36:44], ecap[:]),
            reads=("eppos", "ecap"), writes=("epmsk",))
        op("ve", lambda e: e.scalar_tensor_tensor(
            T2B[:, 52:60], T2B[:, 44:52], float(1 << 30), T2B[:, 52:60],
            ALU.mult, ALU.add), reads=("epge", "epmsk"), writes=("epmsk",))
        op("ve", lambda e, s=s, t0=t0: e.tensor_scalar(
            T2B[:, 68:69], iota128[:], float(s * S + t0), None, ALU.add),
            reads=("iota128",), writes=("eptk",))
        op("ve", lambda e, tki0=tki0: e.tensor_copy(
            TKI[:, tki0:tki0 + 1], T2B[:, 68:69]),
            reads=("eptk",), writes=(f"eptok{mp}",))
        KTILES = 8 * CAP // 128
        for sl in range(2):
            eqap = EQS[:, u * 16 + sl * 8:u * 16 + sl * 8 + 8]
            op("ve", lambda e, eqap=eqap: e.tensor_mul(
                T2B[:, 60:68], eqap, T2B[:, 52:60]),
                reads=("eqs", "epmsk"), writes=("epeq",))
            op("ve", lambda e: e.reduce_sum(
                T2B[:, 69:70], T2B[:, 60:68], axis=AX),
                reads=("epeq",), writes=("epd",))
            op("ve", lambda e: e.tensor_scalar(
                T2B[:, 70:71], T2B[:, 69:70], 1.0 / 128, None, ALU.mult),
                reads=("epd",), writes=("epk",))
            op("ve", lambda e, tki0=tki0: e.tensor_copy(
                TKI[:, tki0 + 3:tki0 + 4], T2B[:, 70:71]),
                reads=("epk",), writes=(f"epki{mp}",))
            op("ve", lambda e, tki0=tki0: e.tensor_copy(
                T2B[:, 70:71], TKI[:, tki0 + 3:tki0 + 4]),
                reads=(f"epki{mp}",), writes=("epk",))
            op("ve", lambda e: e.scalar_tensor_tensor(
                T2B[:, 71:72], T2B[:, 70:71], -128.0, T2B[:, 69:70],
                ALU.mult, ALU.add), reads=("epk", "epd"), writes=("epr",))
            op("ve", lambda e: e.tensor_scalar(
                T2B[:, 68:69], T2B[:, 71:72], 0.0, None, ALU.is_lt),
                reads=("epr",), writes=("eptk",))
            op("ve", lambda e: e.tensor_sub(
                T2B[:, 70:71], T2B[:, 70:71], T2B[:, 68:69]),
                reads=("epk", "eptk"), writes=("epk",))
            op("ve", lambda e: e.scalar_tensor_tensor(
                T2B[:, 71:72], T2B[:, 68:69], 128.0, T2B[:, 71:72],
                ALU.mult, ALU.add), reads=("eptk", "epr"), writes=("epr",))
            op("ve", lambda e, KTILES=KTILES: e.scalar_tensor_tensor(
                T2B[:, 69:70], T2B[:, 71:72], float(KTILES), T2B[:, 70:71],
                ALU.mult, ALU.add), reads=("epr", "epk"), writes=("epd",))
            op("ve", lambda e, sl=sl, tki0=tki0: e.tensor_copy(
                TKI[:, tki0 + 1 + sl:tki0 + 2 + sl], T2B[:, 69:70]),
                reads=("epd",), writes=(f"epd{sl}{mp}",))
            op("gp", lambda e, sl=sl, tki0=tki0: e.indirect_dma_start(
                idx_scr, bass.IndirectOffsetOnAxis(
                    TKI[:, tki0 + 1 + sl:tki0 + 2 + sl], 0),
                TKI[:, tki0:tki0 + 1], None, bounds_check=rb_idx,
                oob_is_err=False),
                reads=(f"eptok{mp}", f"epd{sl}{mp}", "idxscr"),
                writes=("idxscr",),
                ch=f"ixs{sl}")

    for s in range(nseq):
        lnpart(s, 0)
        if NSUB > 1:
            lnpart(s, 1)
        txpart(s, 0)
        for u in range(NSUB):
            if u >= 4:
                gatepart_b(s, u - 4)
            if u + 2 < NSUB:
                lnpart(s, u + 2)
            if u >= 3:
                gatepart(s, u - 3)
            if u + 1 < NSUB:
                txpart(s, u + 1)
            if s >= 1 and u == 0:
                idxpart(s - 1, 0)
            pepart(s, u)
            gatemm(s, u)
            pepart_a(s, u)
            if s >= 1 and u + 1 < NSUB:
                idxpart(s - 1, u + 1)
        for u in range(max(0, NSUB - 4), NSUB):
            if u <= NSUB - 4:
                gatepart_b(s, u)
            else:
                gatepart(s, u)
                gatepart_b(s, u)
        op("ve", lambda e: e.reduce_max(
            SMX[:, 0:16], ASB[:].rearrange("p (a q) -> p a q", q=64), axis=AX),
            reads=("asb",), writes=("smx_m",))
        op("ve", lambda e: e.tensor_scalar(
            SMX[:, 16:32], SMX[:, 0:16], -1.0, None, ALU.mult),
            reads=("smx_m",), writes=("smx_n",))
        op("ve", lambda e: e.memset(SMX[:, 32:48], 1.0), writes=("smx_s",))
        for hh in range(NH):
            p, o = hh // 2, hh % 2
            r0 = 64 * o
            a = 2 * p + o
            op("ac", lambda e, s=s, p=p, r0=r0, a=a, o=o: e.activation(
                ASM[r0:r0 + 64, s * 1024 + p * 128 + o * 64:
                    s * 1024 + p * 128 + o * 64 + 64],
                ASB[r0:r0 + 64, p * 128 + o * 64:p * 128 + o * 64 + 64],
                AF.Exp, bias=SMX[r0:r0 + 64, 16 + a:16 + a + 1],
                accum_out=SMX[r0:r0 + 64, 32 + a:32 + a + 1]),
                reads=("asb", "smx_n"),
                writes=(f"asm:{s}", "smx_s"))
        op("ve", lambda e, s=s: e.reciprocal(
            RSM[0:64, s * 8:s * 8 + 8],
            SMX[0:64, 32:48].rearrange("p (a two) -> p a two", two=2)[:, :, 0]),
            reads=("smx_s",), writes=(f"rsm:{s}",))
        op("ve", lambda e, s=s: e.reciprocal(
            RSM[64:128, s * 8:s * 8 + 8],
            SMX[64:128, 32:48].rearrange("p (a two) -> p a two", two=2)[:, :, 1]),
            reads=("smx_s",), writes=(f"rsm:{s}",))

        gfall = _wfg(F_GFM, F_GFM + S)
        op("gp", lambda e, s=s: e.dma_start(
            gate_scr[s], WF[0:NE, F_GFM:F_GFM + S]),
            reads=gfall, writes=(f"gscr:{s}",), ch="gst")
        NHALF = (S + 1023) // 1024
        for hh in range(NHALF):
            hc = hh * 1024
            hw = min(1024, S - hc)
            op("ve", lambda e, hc=hc, hw=hw: e.tensor_scalar(
                WF[0:NE, F_SC2:F_SC2 + hw],
                WF[0:NE, F_GFM + hc:F_GFM + hc + hw], 0.0, None, ALU.is_gt),
                reads=gfall, writes=_wfg(F_SC2, F_SC2 + 1024))
            if hh == 0:
                ini = 0.0
            else:
                ini = WF[0:NE, F_GFM + hc - 1:F_GFM + hc]
            op("ve", lambda e, hc=hc, hw=hw, ini=ini: e.tensor_tensor_scan(
                WF[0:NE, F_GFM + hc:F_GFM + hc + hw],
                WF[0:NE, F_SC2:F_SC2 + hw], WF[0:NE, F_SC2:F_SC2 + hw],
                ini, ALU.add, ALU.bypass),
                reads=_wfg(F_SC2, F_SC2 + 1024) + gfall, writes=gfall)
        op("ve", lambda e: e.tensor_scalar(
            WF[0:NE, F_GFM:F_GFM + S], WF[0:NE, F_GFM:F_GFM + S],
            BASEE[:, 0:1], None, ALU.add),
            reads=gfall + ("basee",), writes=gfall)
        op("ve", lambda e: e.tensor_copy(
            BASEE[:, 0:1], WF[0:NE, F_GFM + S - 1:F_GFM + S]),
            reads=gfall, writes=("basee",))
    for u in range(NSUB):
        idxpart(nseq - 1, u)

    if phases < 2:
        _finish(em)
        return _realize(nc, em)
    for sl in range(2):
        ch = f"ixs{sl}"
        if ch in em.cnt:
            em.wtok[f"idxfin{sl}"] = (ch, em.cnt[ch])
    op("sy", lambda e: e.dma_start(
        IDXS[:], idx_scr.rearrange("(p k) o -> p (k o)", p=128)),
        reads=("idxscr", "idxfin0", "idxfin1", "idxbig"),
        writes=("idxall",), ch="ixall")
    for i_, ch in enumerate(("xmst0", "xmst1", "xmm0", "xmm1")):
        if ch in em.cnt:
            em.retag([k for k in em.wtok if k.startswith("xmscr")
                      and em.wtok[k][0] == ch], (ch, em.cnt[ch]))
            em.wtok[f"xmfin{i_}"] = (ch, em.cnt[ch])
    for ch in ("xnst0", "xnst1", "xnst2"):
        if ch in self_cnt(em):
            tokf = (ch, em.cnt[ch])
            em.retag([k for k in em.wtok if k.startswith("xnscr:")
                      and em.wtok[k][0] == ch], tokf)
    if "gst" in em.cnt:
        em.retag([k for k in em.wtok if k.startswith("gscr:")],
                 ("gst", em.cnt["gst"]))
    w2items = []
    for e_ in range(NE):
        w2items.append((lambda e, e_=e_: e.dma_start(
            AR[:, A_W2 + e_ * 4096:A_W2 + (e_ + 1) * 4096].rearrange(
                "p (jc d) -> p jc d", d=D),
            W2[e_].rearrange("(jc p) d -> p jc d", p=128)), (),
            _arg(A_W2 + e_ * 4096, A_W2 + (e_ + 1) * 4096)))
    em.dma_series("gp", "w2s", w2items)
    em.dma_series("sy", "b2l", [
        (lambda e: e.dma_start(b2all[0:NE, :], b2), (),
         ("b2sb", "sig1", "sig2", "sig3")),
        (lambda e: e.dma_start(b2all[8:9, :], bs2.rearrange("(o d) -> o d", o=1)),
         (), ("bs2sb",)),
    ])

    op("ve", lambda e: e.memset(GT9[0:9, :], 1.0),
        writes=("gt9ones", "gt9:0", "gt9:1"))

    OOBF = float(1 << 30)
    NGU = NE * NUNIT

    def xg_col(gidx):
        return (F2_XG0, 1024, 2048, F2_XG1)[gidx % 4]

    for slot_ in range(4):
        xgc_ = (F2_XG0, 1024, 2048, F2_XG1)[slot_]
        op("ve", lambda e, xgc_=xgc_: e.memset(WF[:, xgc_:xgc_ + 520], 0.0),
            writes=(f"xg:{slot_}", f"wf:{xgc_ // 1024}"))

    def emit_gather(gidx):
        xgc = xg_col(gidx)
        xgg = (f"xg:{gidx % 4}", f"wf:{xgc // 1024}")
        op("gp", lambda e, xgc=xgc, gidx=gidx: e.indirect_dma_start(
            WF[:, xgc:xgc + 520], None, xm_scr,
            bass.IndirectOffsetOnAxis(IDXS[:, gidx:gidx + 1], 0),
            bounds_check=rb_tok, oob_is_err=False),
            reads=("idxall", "xmfin0", "xmfin1", "xmfin2", "xmfin3"),
            writes=xgg, ch=f"xgl{gidx % 4}")

    def emit_tx(guni):
        xngb = H2_XNF + (guni % 2) * 2048
        xng = _whg(xngb, xngb + 2048)
        for sub in range(2):
            gidx = guni * 2 + sub
            xgc = xg_col(gidx)
            xgg = (f"xg:{gidx % 4}", f"wf:{xgc // 1024}")
            fns = []
            for c in range(DC):
                fns.append(lambda e, c=c, xgc=xgc: e.transpose(
                    pbf[:, c * 128:(c + 1) * 128],
                    WF[:, xgc:xgc + 512].bitcast(BF16)[:, c * 128:(c + 1) * 128],
                    idbf[:]))
            grp("pe", fns, reads=xgg + ("idbf",), writes=("pbf",))
            op("ac", lambda e, sub=sub, xngb=xngb: e.activation(
                WH[:, xngb:xngb + 2048].rearrange(
                    "p (c two t) -> p c two t", two=2, t=128)[:, :, sub, :],
                pbf[:].rearrange("p (c t) -> p c t", t=128), AF.Copy),
                reads=("pbf",), writes=xng)

    def emit_w1(guni):
        e_x = guni // NUNIT
        xngb = H2_XNF + (guni % 2) * 2048
        xng = _whg(xngb, xngb + 2048)
        upar = guni % 2
        hsg = _whg(H2_HSC + upar * 1024, H2_HSC + (upar + 1) * 1024)
        for half, pp, pname in ((0, pfa, "pfa"), (1, pfb, "pfb")):
            fns = []
            for jo in range(2):
                jc = half * 2 + jo
                for c in range(DC):
                    a0 = A_W1 + e_x * 4096 + c * 512 + jc * 128
                    fns.append(lambda e, jo=jo, c=c, a0=a0, pp=pp, xngb=xngb:
                        e.matmul(
                            pp[:, jo * 256:(jo + 1) * 256],
                            AR[:, a0:a0 + 128],
                            WH[:, xngb + c * 256:xngb + (c + 1) * 256],
                            start=(c == 0), stop=(c == DC - 1)))
            grp("pe", fns,
                reads=xng + _arg(A_W1 + e_x * 4096, A_W1 + (e_x + 1) * 4096),
                writes=(pname,))
            for jo in range(2):
                jc = half * 2 + jo
                op("ac", lambda e, jc=jc, jo=jo, pp=pp, upar=upar, e_x=e_x:
                    e.activation(
                        WH[:, H2_HSC + upar * 1024 + jc * 256:
                           H2_HSC + upar * 1024 + (jc + 1) * 256],
                        pp[:, jo * 256:(jo + 1) * 256], AF.Gelu,
                        bias=b1T[:, jc * 8 + e_x:jc * 8 + e_x + 1]),
                    reads=(pname, "b1T"), writes=hsg)

    def emit_meta(guni):
        e_x = guni // NUNIT
        for sub in range(2):
            gidx = guni * 2 + sub
            xgc = xg_col(gidx)
            xgg = (f"xg:{gidx % 4}", f"wf:{xgc // 1024}")
            b = 8 * sub
            nm = f"spm{sub}"
            op("ve", lambda e, xgc=xgc, e_x=e_x, b=b: e.tensor_scalar(
                T2B[:, b:b + 1], WF[:, xgc + 514:xgc + 515], float(e_x), None,
                ALU.is_equal), reads=xgg, writes=(nm,))
            op("ve", lambda e, xgc=xgc, e_x=e_x, b=b: e.tensor_scalar(
                T2B[:, b + 1:b + 2], WF[:, xgc + 515:xgc + 516], float(e_x),
                None, ALU.is_equal), reads=xgg + (nm,), writes=(nm,))
            op("ve", lambda e, xgc=xgc, b=b: e.tensor_mul(
                T2B[:, b + 2:b + 3], WF[:, xgc + 512:xgc + 513],
                T2B[:, b:b + 1]), reads=xgg + (nm,), writes=(nm,))
            op("ve", lambda e, xgc=xgc, b=b: e.tensor_mul(
                T2B[:, b + 3:b + 4], WF[:, xgc + 513:xgc + 514],
                T2B[:, b + 1:b + 2]), reads=xgg + (nm,), writes=(nm,))
            op("ve", lambda e, b=b: e.tensor_add(
                T2B[:, b + 2:b + 3], T2B[:, b + 2:b + 3], T2B[:, b + 3:b + 4]),
                reads=(nm,), writes=(nm,))
            op("ve", lambda e, xgc=xgc, b=b: e.scalar_tensor_tensor(
                T2B[:, b + 4:b + 5], T2B[:, b + 1:b + 2], float(NTOK),
                WF[:, xgc + 516:xgc + 517], ALU.mult, ALU.add),
                reads=(nm,) + xgg, writes=(nm,))
            op("ve", lambda e, gidx=gidx, b=b: e.tensor_copy(
                T2B[:, b + 5:b + 6], IDXS[:, gidx:gidx + 1]),
                reads=("idxall", nm), writes=(nm,))
            op("ve", lambda e, b=b: e.tensor_scalar(
                T2B[:, b + 6:b + 7], T2B[:, b + 5:b + 6], float(NTOK), None,
                ALU.is_lt), reads=(nm,), writes=(nm,))
            op("ve", lambda e, b=b: e.tensor_scalar(
                T2B[:, b + 4:b + 5], T2B[:, b + 4:b + 5], float(2 * NTOK),
                None, ALU.subtract), reads=(nm,), writes=(nm,))
            op("ve", lambda e, b=b: e.tensor_mul(
                T2B[:, b + 4:b + 5], T2B[:, b + 4:b + 5], T2B[:, b + 6:b + 7]),
                reads=(nm,), writes=(nm,))
            op("ve", lambda e, b=b: e.tensor_scalar(
                T2B[:, b + 4:b + 5], T2B[:, b + 4:b + 5], float(2 * NTOK),
                None, ALU.add), reads=(nm,), writes=(nm,))
            op("ve", lambda e, gidx=gidx, b=b: e.tensor_copy(
                TKI[:, 2 + gidx % 2:3 + gidx % 2], T2B[:, b + 4:b + 5]),
                reads=(nm,), writes=(f"ysd{gidx % 2}",))

    def emit_w2(guni):
        e_x = guni // NUNIT
        upar = guni % 2
        hsg = _whg(H2_HSC + upar * 1024, H2_HSC + (upar + 1) * 1024)
        for sub in range(2):
            gidx = guni * 2 + sub
            pY, pYn = (pY0, "pY0") if sub == 0 else (pY1, "pY1")
            fns = []
            for h in range(2):
                for jc in range(JCN):
                    a0 = A_W2 + e_x * 4096 + jc * 1024 + h * 512
                    fns.append(lambda e, jc=jc, h=h, a0=a0, pY=pY,
                               upar=upar, sub=sub: e.matmul(
                        pY[:, h * 512:(h + 1) * 512],
                        WH[:, H2_HSC + upar * 1024 + jc * 256 + sub * 128:
                           H2_HSC + upar * 1024 + jc * 256 + sub * 128 + 128],
                        AR[:, a0:a0 + 512],
                        start=(jc == 0), stop=(jc == JCN - 1)))
            grp("pe", fns,
                reads=hsg + _arg(A_W2 + e_x * 4096, A_W2 + (e_x + 1) * 4096),
                writes=(pYn,))
            ysc = F2_YS + (gidx % 2) * 1024
            ysg = (f"ys:{gidx % 2}", "wf:3" if gidx % 2 == 0 else "wf:4")
            op("ac", lambda e, pY=pY, ysc=ysc, sub=sub: e.activation(
                WF[:, ysc:ysc + 1024], pY[:], AF.Copy,
                scale=T2B[:, 8 * sub + 2:8 * sub + 3]),
                reads=(pYn, f"spm{sub}"), writes=ysg)
            op("gp", lambda e, ysc=ysc, gidx=gidx: e.indirect_dma_start(
                yab_scr, bass.IndirectOffsetOnAxis(
                    TKI[:, 2 + gidx % 2:3 + gidx % 2], 0),
                WF[:, ysc:ysc + 1024], None),
                reads=ysg + (f"ysd{gidx % 2}",), writes=(f"yab{gidx % 2}",),
                ch=f"ysc{gidx % 2}")

    for sub in range(4):
        emit_gather(sub)
    emit_tx(0)
    for guni in range(NGU):
        emit_w1(guni)
        if guni + 1 < NGU:
            emit_tx(guni + 1)
        emit_meta(guni)
        if guni + 2 < NGU:
            for sub in range(2):
                emit_gather((guni + 2) * 2 + sub)
        emit_w2(guni)

    def sh_load(ti):
        si, c0 = ti // T2PS, (ti % T2PS) * 256
        par = ti % 2
        u0 = c0 // 128
        xn2g = _whg(H2_XNF + par * 2048, H2_XNF + (par + 1) * 2048)
        op("sy", lambda e, si=si, c0=c0, par=par: e.dma_start(
            WH[:, H2_XNF + par * 2048:H2_XNF + (par + 1) * 2048].rearrange(
                "p (c t) -> p c t", t=256),
            xn_scr[si, :, :, c0:c0 + 256].rearrange("c p t -> p c t")),
            reads=(f"xnscr:{si}:{u0}", f"xnscr:{si}:{u0 + 1}"), writes=xn2g,
            ch=f"xn2l{par}")
        op("sy", lambda e, si=si, c0=c0, par=par: e.dma_start(
            GT9[0:NE, par * 256:(par + 1) * 256],
            gate_scr[si, :, c0:c0 + 256]),
            reads=(f"gscr:{si}",), writes=(f"gt9:{par}",), ch=f"gtl{par}")

    sh_load(0)
    for ti in range(NT2):
        if ti + 1 < NT2:
            sh_load(ti + 1)
        si, c0 = ti // T2PS, (ti % T2PS) * 256
        par = ti % 2
        xn2g = _whg(H2_XNF + par * 2048, H2_XNF + (par + 1) * 2048)
        gtg = (f"gt9:{par}",)
        hsg = _whg(H2_HSC + par * 1024, H2_HSC + (par + 1) * 1024)
        for half, pp, pname in ((0, pfa, "pfa"), (1, pfb, "pfb")):
            fns = []
            for jo in range(2):
                jc = half * 2 + jo
                for c in range(DC):
                    a0 = A_WS1 + c * 512 + jc * 128
                    fns.append(lambda e, jo=jo, c=c, a0=a0, pp=pp, par=par:
                        e.matmul(
                            pp[:, jo * 256:(jo + 1) * 256],
                            AR[:, a0:a0 + 128],
                            WH[:, H2_XNF + par * 2048 + c * 256:
                               H2_XNF + par * 2048 + (c + 1) * 256],
                            start=(c == 0), stop=(c == DC - 1)))
            grp("pe", fns, reads=xn2g + _arg(A_WS1, A_WS1 + 4096),
                writes=(pname,))
            for jo in range(2):
                jc = half * 2 + jo
                op("ac", lambda e, jc=jc, jo=jo, pp=pp, par=par: e.activation(
                    WH[:, H2_HSC + par * 1024 + jc * 256:
                       H2_HSC + par * 1024 + (jc + 1) * 256],
                    pp[:, jo * 256:(jo + 1) * 256], AF.Gelu,
                    bias=bs1T[:, jc:jc + 1]),
                    reads=(pname, "bs1T"), writes=hsg)
        for sub, pY, pYn in ((0, pY0, "pY0"), (1, pY1, "pY1")):
            fns = []
            for h in range(2):
                for jc in range(JCN):
                    a0 = A_WS2 + jc * 1024 + h * 512
                    fns.append(lambda e, jc=jc, h=h, a0=a0, pY=pY, par=par,
                               sub=sub: e.matmul(
                        pY[:, h * 512:(h + 1) * 512],
                        WH[:, H2_HSC + par * 1024 + jc * 256 + sub * 128:
                           H2_HSC + par * 1024 + jc * 256 + sub * 128 + 128],
                        AR[:, a0:a0 + 512], start=(jc == 0), stop=False))
                fns.append(lambda e, h=h, pY=pY, par=par, sub=sub: e.matmul(
                    pY[:, h * 512:(h + 1) * 512],
                    GT9[0:9, par * 256 + sub * 128:par * 256 + sub * 128 + 128],
                    b2all[0:9, h * 512:(h + 1) * 512], start=False, stop=True))
            grp("pe", fns,
                reads=hsg + _arg(A_WS2, A_WS2 + 4096) + gtg +
                ("b2sb", "bs2sb", "gt9ones"), writes=(pYn,))
            yc = F2_Y + sub * 1024
            yg = _wfg(yc, yc + 1024)
            op("ac", lambda e, pY=pY, yc=yc: e.activation(
                WF[:, yc:yc + 1024], pY[:], AF.Copy), reads=(pYn,), writes=yg)
            gsub = (si * S + c0) // 128 + sub
            op("sy", lambda e, yc=yc, gsub=gsub: e.dma_start(
                y_scr[gsub], WF[:, yc:yc + 1024]),
                reads=yg, writes=(f"yscr:{gsub}",), ch=f"yst{sub}")

    if phases < 3:
        _finish(em)
        return _realize(nc, em)
    for sub_ in range(2):
        ch = f"yst{sub_}"
        if ch in em.cnt:
            em.retag([k for k in em.wtok if k.startswith("yscr:")
                      and em.wtok[k][0] == ch], (ch, em.cnt[ch]))
    load_nat(Wv, A_NATV, "natq")
    load_nat(Wo, A_NATO, "natk")
    transpose_weight(A_NATV, A_WVT)
    transpose_weight(A_NATO, A_WOT)

    def p3load_xn(ti):
        si, c0 = ti // T3PS, (ti % T3PS) * 256
        par = ti % 2
        u0 = c0 // 128
        xn3g = _whg(H3_XNF + par * 2048, H3_XNF + (par + 1) * 2048)
        op("sy", lambda e, si=si, c0=c0, par=par: e.dma_start(
            WH[:, H3_XNF + par * 2048:H3_XNF + (par + 1) * 2048].rearrange(
                "p (c t) -> p c t", t=256),
            xn_scr[si, :, :, c0:c0 + 256].rearrange("c p t -> p c t")),
            reads=(f"xnscr:{si}:{u0}", f"xnscr:{si}:{u0 + 1}"), writes=xn3g,
            ch=f"xn2l{par}")

    def p3load_sub(ti, sub):
        si, c0 = ti // T3PS, (ti % T3PS) * 256
        gsub = (si * S + c0) // 128 + sub
        spar = gsub % 2
        xg = _wfg(F3_X + spar * 1024, F3_X + (spar + 1) * 1024)
        ygd = _wfg(F3_Y + spar * 1024, F3_Y + (spar + 1) * 1024)
        op("sy", lambda e, gsub=gsub, spar=spar: e.dma_start(
            WF[:, F3_Y + spar * 1024:F3_Y + (spar + 1) * 1024],
            y_scr[gsub]), reads=(f"yscr:{gsub}",), writes=ygd,
            ch=f"yl{spar}")
        op("gp", lambda e, gsub=gsub, spar=spar: e.dma_start(
            WF[:, F3_Y + spar * 1024:F3_Y + (spar + 1) * 1024],
            yab_scr[gsub * 128:gsub * 128 + 128, :],
            accum_op=ALU.add),
            reads=("yab0", "yab1"), writes=ygd, ch=f"yaa{spar}")
        op("gp", lambda e, gsub=gsub, spar=spar: e.dma_start(
            WF[:, F3_Y + spar * 1024:F3_Y + (spar + 1) * 1024],
            yab_scr[NTOK + gsub * 128:NTOK + gsub * 128 + 128, :],
            accum_op=ALU.add),
            reads=("yab0", "yab1"), writes=ygd, ch=f"yab{spar}")
        op("sy", lambda e, si=si, c0=c0, sub=sub, spar=spar: e.dma_start(
            WF[:, F3_X + spar * 1024:F3_X + (spar + 1) * 1024],
            x_in[si, c0 + sub * 128:c0 + sub * 128 + 128, :]), writes=xg,
            ch=f"xl{spar}")

    p3load_xn(0)
    p3load_sub(0, 0)
    p3load_sub(0, 1)
    for ti in range(NT3):
        si, c0 = ti // T3PS, (ti % T3PS) * 256
        par = ti % 2
        xn3g = _whg(H3_XNF + par * 2048, H3_XNF + (par + 1) * 2048)
        if p3cut < 2:
            continue
        vg = _whg(H3_V, H3_V + 2048)
        for ic in range(DC):
            pp, pname = (pfa, "pfa") if ic % 2 == 0 else (pfb, "pfb")
            fns = []
            for c in range(DC):
                fns.append(lambda e, ic=ic, c=c, par=par, pp=pp: e.matmul(
                    pp[:, 0:256],
                    AR[:, A_WVT + c * 1024 + ic * 128:
                       A_WVT + c * 1024 + (ic + 1) * 128],
                    WH[:, H3_XNF + par * 2048 + c * 256:
                       H3_XNF + par * 2048 + (c + 1) * 256],
                    start=(c == 0), stop=(c == DC - 1)))
            grp("pe", fns, reads=xn3g + _arg(A_WVT, A_WVT + 8192),
                writes=(pname,))
            op("ac", lambda e, ic=ic, pp=pp, pname=pname, si=si: e.activation(
                WH[:, H3_V + ic * 256:H3_V + (ic + 1) * 256], pp[:, 0:256],
                AF.Copy, scale=RSM[:, si * 8 + ic:si * 8 + ic + 1]),
                reads=(pname, f"rsm:{si}"), writes=vg)
        if p3cut < 3:
            continue
        atg = _whg(H3_AT, H3_AT + 2048)
        for p in range(NPAIR):
            pp, pname = (pfa, "pfa") if p % 2 == 0 else (pfb, "pfb")
            fns = [lambda e, p=p, si=si, pp=pp: e.matmul(
                pp[:, 0:256],
                ASM[:, si * 1024 + p * 128:si * 1024 + (p + 1) * 128],
                WH[:, H3_V + p * 256:H3_V + (p + 1) * 256],
                start=True, stop=True)]
            grp("pe", fns, reads=(f"asm:{si}",) + vg, writes=(pname,))
            if p3cut == 30:
                continue
            op("ac", lambda e, p=p, pp=pp, pname=pname: e.activation(
                WH[:, H3_AT + p * 256:H3_AT + (p + 1) * 256],
                pp[:, 0:256], AF.Copy),
                reads=(pname,), writes=atg)
        if p3cut < 4 or p3cut == 30:
            continue
        for sub in range(2):
            gsub = (si * S + c0) // 128 + sub
            spar = gsub % 2
            pY, pYn = (pY0, "pY0") if sub == 0 else (pY1, "pY1")
            xg = _wfg(F3_X + spar * 1024, F3_X + (spar + 1) * 1024)
            ygd = _wfg(F3_Y + spar * 1024, F3_Y + (spar + 1) * 1024)
            og = _wfg(F3_O + spar * 1024, F3_O + (spar + 1) * 1024)
            fns = []
            for p in range(NPAIR):
                for h in range(2):
                    fns.append(lambda e, p=p, h=h, sub=sub, pY=pY: e.matmul(
                        pY[:, h * 512:(h + 1) * 512],
                        WH[:, H3_AT + p * 256 + sub * 128:
                           H3_AT + p * 256 + sub * 128 + 128],
                        AR[:, A_WOT + p * 1024 + h * 512:
                           A_WOT + p * 1024 + (h + 1) * 512],
                        start=(p == 0), stop=(p == NPAIR - 1)))
            grp("pe", fns, reads=atg + _arg(A_WOT, A_WOT + 8192), writes=(pYn,))
            if p3cut < 5:
                continue
            op("ac", lambda e, spar=spar: e.activation(
                WF[:, F3_O + spar * 1024:F3_O + (spar + 1) * 1024],
                WF[:, F3_Y + spar * 1024:F3_Y + (spar + 1) * 1024], AF.Silu),
                reads=ygd, writes=og)
            if p3cut < 6:
                continue
            op("ve", lambda e, spar=spar, pY=pY: e.tensor_tensor(
                WF[:, F3_O + spar * 1024:F3_O + (spar + 1) * 1024],
                WF[:, F3_O + spar * 1024:F3_O + (spar + 1) * 1024],
                pY[:], ALU.mult), reads=og + (pYn,), writes=og)
            op("ve", lambda e, spar=spar: e.tensor_tensor(
                WF[:, F3_O + spar * 1024:F3_O + (spar + 1) * 1024],
                WF[:, F3_O + spar * 1024:F3_O + (spar + 1) * 1024],
                WF[:, F3_X + spar * 1024:F3_X + (spar + 1) * 1024], ALU.add),
                reads=og + xg, writes=og)
            op("gp", lambda e, si=si, c0=c0, sub=sub, spar=spar: e.dma_start(
                out[si, c0 + sub * 128:c0 + sub * 128 + 128, :],
                WF[:, F3_O + spar * 1024:F3_O + (spar + 1) * 1024]),
                reads=og, ch=f"outst{spar}")
            if ti + 1 < NT3:
                if sub == 0:
                    p3load_xn(ti + 1)
                p3load_sub(ti + 1, sub)

    if dbg:
        _dbg_dump(nc, em, op, locals())
    _finish(em)
    return _realize(nc, em)


def _dbg_dump(nc, em, op, L):
    xm_scr, idx_scr, yab_scr, y_scr, gate_scr = (
        L["xm_scr"], L["idx_scr"], L["yab_scr"], L["y_scr"], L["gate_scr"])
    for name, ap, shp, dt in (
            ("d_xm", xm_scr, list(xm_scr.shape), F32),
            ("d_idx", idx_scr, list(idx_scr.shape), mybir.dt.int32),
            ("d_yab", yab_scr, list(yab_scr.shape), F32),
            ("d_ysc", y_scr, list(y_scr.shape), F32),
            ("d_gate", gate_scr, list(gate_scr.shape), F32)):
        o = nc.dram_tensor(name, shp, dt, kind="ExternalOutput").ap()
        op("gp", lambda e, o=o, ap=ap: e.dma_start(o, ap),
            reads=(), writes=(), ch="dbgd")


def _finish(em):
    for ch, v in sorted(em.cnt.items()):
        if ch in ("pe", "ve", "ac"):
            continue
        em._wait("gp", (ch, v))
        em._wait("sy", (ch, v))


def _realize(nc, em):
    with nc.Block() as block:
        block.sync(lambda e: [f(e) for f in em.sched["sy"]])
        block.gpsimd(lambda e: [f(e) for f in em.sched["gp"]])
        block.tensor(lambda e: [f(e) for f in em.sched["pe"]])
        block.vector(lambda e: [f(e) for f in em.sched["ve"]])
        block.scalar(lambda e: [f(e) for f in em.sched["ac"]])
    return nc


def kernel(**inputs):
    f = np.float32
    x = np.ascontiguousarray(np.asarray(inputs["L0_attn_x"], dtype=f))
    rs = np.asarray(inputs["L0_ffn_route_scale"], dtype=f).reshape(1, 1)
    rs_col = np.ascontiguousarray(np.broadcast_to(rs, (NE, 1)))
    shared = {}
    for k in ("L0_attn_Wq", "L0_attn_Wk", "L0_attn_Wv", "L0_attn_Wo",
              "L0_ffn_gate_W", "L0_ffn_W1", "L0_ffn_b1", "L0_ffn_W2",
              "L0_ffn_b2", "L0_ffn_Ws1", "L0_ffn_bs1", "L0_ffn_Ws2",
              "L0_ffn_bs2"):
        shared[k] = np.ascontiguousarray(np.asarray(inputs[k], dtype=f))
    nseq = x.shape[0] // NCORES
    S = x.shape[1]
    nc = build_nc(nseq, S)
    in_maps = []
    for c in range(NCORES):
        m = dict(shared)
        m["L0_attn_x"] = np.ascontiguousarray(x[c * nseq:(c + 1) * nseq])
        m["route_scale_col"] = rs_col
        in_maps.append(m)
    res = run_bass_kernel_spmd(nc, in_maps, core_ids=list(range(NCORES)))
    outs = [np.asarray(r["out"]) for r in res.results]
    return np.concatenate(outs, axis=0).astype(np.float32)


# version 16 (final)
# speedup vs baseline: 1.2384x; 1.0101x over previous
import numpy as np
import ml_dtypes
import concourse.bass as bass
import concourse.mybir as mybir
from concourse.bass_utils import run_bass_kernel_spmd

F32 = mybir.dt.float32
BF16 = mybir.dt.bfloat16
AF = mybir.ActivationFunctionType
ALU = mybir.AluOpType
AX = mybir.AxisListType.X

NCORES = 8
D = 1024
DC = 8
NH = 16
NPAIR = 8
NE = 8
HF = 512
JCN = 4
EPS = 1e-5
BIG = 1.0e30

ENGS = ("sy", "gp", "pe", "ve", "ac")
DMA_ENGS = ("sy", "gp")


class Em:

    def __init__(self, nc):
        self.nc = nc
        self.sched = {k: [] for k in ENGS}
        self.cnt = {}
        self.sems = {}
        for k in ("pe", "ve", "ac"):
            self.sems[k] = nc.alloc_semaphore(f"sem_{k}")
            self.cnt[k] = 0
        self.waited = {k: {} for k in ENGS}
        self.wtok = {}
        self.rtok = {}

    def _sem(self, key):
        if key not in self.sems:
            self.sems[key] = self.nc.alloc_semaphore(f"sem_{key}")
            self.cnt[key] = 0
        return self.sems[key]

    def _wait(self, eng, tok):
        if tok is None:
            return
        src, v = tok
        if self.waited[eng].get(src, 0) >= v:
            return
        self.waited[eng][src] = v
        sem = self.sems[src]
        self.sched[eng].append(lambda e, sem=sem, v=v: e.wait_ge(sem, v))

    def _pre(self, eng, reads, writes):
        for r in reads:
            self._wait(eng, self.wtok.get(r))
        for w in writes:
            self._wait(eng, self.wtok.get(w))
            for src, v in self.rtok.get(w, {}).items():
                self._wait(eng, (src, v))

    def _post(self, tok, eng, reads, writes):
        for w in writes:
            self.wtok[w] = tok
            self.rtok[w] = {}
        for r in reads:
            d = self.rtok.setdefault(r, {})
            if d.get(tok[0], 0) < tok[1]:
                d[tok[0]] = tok[1]

    def op(self, eng, fn, reads=(), writes=(), ch=None):
        assert (eng in DMA_ENGS) == (ch is not None)
        self._pre(eng, reads, writes)
        key = ch if ch is not None else eng
        sem = self._sem(key) if ch is not None else self.sems[key]
        step = 16 if ch is not None else 1
        self.cnt[key] = self.cnt.get(key, 0) + step
        v = self.cnt[key]
        self.sched[eng].append(
            lambda e, fn=fn, sem=sem, step=step: fn(e).then_inc(sem, step))
        tok = (key, v)
        self._post(tok, eng, reads, writes)
        return tok

    def group(self, eng, fns, reads=(), writes=()):
        assert eng not in DMA_ENGS
        self._pre(eng, reads, writes)
        for fn in fns[:-1]:
            self.sched[eng].append(fn)
        self.cnt[eng] += 1
        v = self.cnt[eng]
        sem = self.sems[eng]
        fn = fns[-1]
        self.sched[eng].append(
            lambda e, fn=fn, sem=sem: fn(e).then_inc(sem, 1))
        tok = (eng, v)
        self._post(tok, eng, reads, writes)
        return tok

    def dma_series(self, eng, ch, items):
        allr, allw = [], []
        for fn, rd, wr in items:
            allr += list(rd)
            allw += list(wr)
        self._pre(eng, tuple(allr), tuple(allw))
        sem = self._sem(ch)
        tok = None
        for fn, rd, wr in items:
            self.cnt[ch] += 16
            v = self.cnt[ch]
            self.sched[eng].append(
                lambda e, fn=fn, sem=sem: fn(e).then_inc(sem, 16))
            tok = (ch, v)
        self._post(tok, eng, tuple(allr), tuple(allw))
        return tok

    def retag(self, bufs, tok):
        for b in bufs:
            self.wtok[b] = tok


def self_cnt(em):
    return em.cnt


def _wfg(c0, c1):
    return tuple(f"wf:{g}" for g in range(c0 // 1024, (c1 + 1023) // 1024))


def _whg(c0, c1):
    return tuple(f"wh:{g}" for g in range(c0 // 1024, (c1 + 1023) // 1024))


def _arg(c0, c1):
    return tuple(f"ar:{g}" for g in range(c0 // 1024, (c1 + 1023) // 1024))


def build_nc(nseq, S, phases=3, p3cut=9, dbg=False):
    nc = bass.Bass("TRN2", target_bir_lowering=False, detect_race_conditions=False)
    NSUB = S // 128
    NTOK = nseq * S
    CAP = min(((NTOK // 4 + 512) + 255) // 256 * 256,
              ((NTOK + 255) // 256) * 256)
    NUNIT = CAP // 256
    T2PS = S // 256
    NT2 = nseq * T2PS
    T3PS = S // 256
    NT3 = nseq * T3PS

    x_in = nc.dram_tensor("L0_attn_x", [nseq, S, D], F32, kind="ExternalInput").ap()
    Wq = nc.dram_tensor("L0_attn_Wq", [D, D], F32, kind="ExternalInput").ap()
    Wk = nc.dram_tensor("L0_attn_Wk", [D, D], F32, kind="ExternalInput").ap()
    Wv = nc.dram_tensor("L0_attn_Wv", [D, D], F32, kind="ExternalInput").ap()
    Wo = nc.dram_tensor("L0_attn_Wo", [D, D], F32, kind="ExternalInput").ap()
    gW = nc.dram_tensor("L0_ffn_gate_W", [D, NE], F32, kind="ExternalInput").ap()
    W1 = nc.dram_tensor("L0_ffn_W1", [NE, D, HF], F32, kind="ExternalInput").ap()
    b1 = nc.dram_tensor("L0_ffn_b1", [NE, HF], F32, kind="ExternalInput").ap()
    W2 = nc.dram_tensor("L0_ffn_W2", [NE, HF, D], F32, kind="ExternalInput").ap()
    b2 = nc.dram_tensor("L0_ffn_b2", [NE, D], F32, kind="ExternalInput").ap()
    Ws1 = nc.dram_tensor("L0_ffn_Ws1", [D, HF], F32, kind="ExternalInput").ap()
    bs1 = nc.dram_tensor("L0_ffn_bs1", [HF], F32, kind="ExternalInput").ap()
    Ws2 = nc.dram_tensor("L0_ffn_Ws2", [HF, D], F32, kind="ExternalInput").ap()
    bs2 = nc.dram_tensor("L0_ffn_bs2", [D], F32, kind="ExternalInput").ap()
    rs_in = nc.dram_tensor("route_scale_col", [NE, 1], F32, kind="ExternalInput").ap()
    out = nc.dram_tensor("out", [nseq, S, D], F32, kind="ExternalOutput").ap()

    xn_scr = nc.dram_tensor("xn_scr", [nseq, DC, 128, S], BF16).ap()
    xm_scr = nc.dram_tensor("xm_scr", [NTOK, 520], F32).ap()
    idx_scr = nc.dram_tensor("idx_scr", [8 * CAP, 1], mybir.dt.int32).ap()
    yab_scr = nc.dram_tensor("yab_scr", [2 * NTOK + 128, D], F32).ap()
    y_scr = nc.dram_tensor("y_scr", [nseq * S // 128, 128, D], F32).ap()
    gate_scr = nc.dram_tensor("gate_scr", [nseq, NE, S], F32).ap()

    ident_bf_c = nc.inline_tensor(np.eye(128, dtype=ml_dtypes.bfloat16), name="c_idbf").ap()
    ident_f_c = nc.inline_tensor(np.eye(128, dtype=np.float32), name="c_idf").ap()
    iota128_c = nc.inline_tensor(
        np.arange(128, dtype=np.float32).reshape(128, 1), name="c_iota").ap()
    iote_c = nc.inline_tensor(
        np.broadcast_to(np.arange(8, dtype=np.float32), (128, 8)).copy(),
        name="c_iote").ap()
    ecap_c = nc.inline_tensor(
        np.broadcast_to(np.arange(8, dtype=np.float32) * CAP - 1.0,
                        (128, 8)).copy(), name="c_ecap").ap()
    bigidx_c = nc.inline_tensor(
        np.full((128, 8 * NUNIT), 1 << 30, dtype=np.int32), name="c_big").ap()

    AR = nc.alloc_sbuf_tensor("AR", [128, 73728], BF16)
    WH = nc.alloc_sbuf_tensor("WH", [128, 8192], BF16)
    WF = nc.alloc_sbuf_tensor("WF", [128, 6144], F32)
    ASM = nc.alloc_sbuf_tensor("ASM", [128, nseq * 1024], BF16)
    RSM = nc.alloc_sbuf_tensor("RSM", [128, nseq * 8], F32)
    idbf = nc.alloc_sbuf_tensor("idbf", [128, 128], BF16)
    idf = nc.alloc_sbuf_tensor("idf", [128, 128], F32)
    iota128 = nc.alloc_sbuf_tensor("iota128", [128, 1], F32)
    iote = nc.alloc_sbuf_tensor("iote", [128, 8], F32)
    ecap = nc.alloc_sbuf_tensor("ecap", [128, 8], F32)
    T2B = nc.alloc_sbuf_tensor("T2B", [128, 72], F32)
    EQS = nc.alloc_sbuf_tensor("EQS", [128, 16 * NSUB], F32)
    META = nc.alloc_sbuf_tensor("META", [128, 16], F32)
    IDXS = nc.alloc_sbuf_tensor("IDXS", [128, 8 * CAP // 128], mybir.dt.int32)
    TKI = nc.alloc_sbuf_tensor("TKI", [128, 12], mybir.dt.int32)
    BASEE = nc.alloc_sbuf_tensor("BASEE", [8, 2], F32)
    gwsb = nc.alloc_sbuf_tensor("gwsb", [128, DC * NE], BF16)
    b1T = nc.alloc_sbuf_tensor("b1T", [128, JCN * NE], F32)
    bs1T = nc.alloc_sbuf_tensor("bs1T", [128, JCN], F32)
    rscol = nc.alloc_sbuf_tensor("rscol", [NE, 1], F32)
    b2all = nc.alloc_sbuf_tensor("b2all", [9, D], F32)
    GT9 = nc.alloc_sbuf_tensor("GT9", [9, 512], F32)
    STT = nc.alloc_sbuf_tensor("STT", [128, 24], F32)
    SMX = nc.alloc_sbuf_tensor("SMX", [128, 64], F32)
    ASB = nc.alloc_sbuf_tensor("ASB", [128, 1024], F32)

    A_W1, A_W2, A_WS1, A_WS2 = 0, 32768, 65536, 69632
    A_WQT, A_WKT, A_NAT1, A_NAT2 = 32768, 40960, 49152, 57344
    A_NATV, A_NATO, A_WVT, A_WOT = 0, 8192, 16384, 24576

    H_XNTM, H_XNF, H_Q, H_K = 0, 3072, 6144, 7168
    H2_XNF, H2_HSC, H2_XNG = 0, 4096, 6144
    H3_XNF, H3_V, H3_AT = 0, 4096, 6144

    F_X1, F_SC2, F_SCORES, F_GFM = 0, 3072, 4096, 4096
    F2_XG0, F2_Y, F2_YS, F2_XG1 = 0, 1024, 3072, 5120
    F2_B1N, F2_BS1N = 4096, 4608
    F3_X, F3_Y, F3_O = 0, 2048, 4096
    T2_RAW, T2_EQ1, T2_MSK, T2_EQ2 = 3072, 3200, 3328, 3456
    T2_GA, T2_GB, T2_GTM, T2_S = 3584, 3712, 3840, 3968

    pY0 = nc.alloc_psum_tensor("pY0", [128, 1024], F32)
    pY1 = nc.alloc_psum_tensor("pY1", [128, 1024], F32)
    pfa = nc.alloc_psum_tensor("pfa", [128, 512], F32)
    pfb = nc.alloc_psum_tensor("pfb", [128, 512], F32)
    pG = nc.alloc_psum_tensor("pG", [128, 512], F32)
    pbf = nc.alloc_psum_tensor("pbf", [128, 1024], BF16)

    em = Em(nc)
    op, grp = em.op, em.group
    rb_idx = nc.alloc_register(mybir.EngineType.Pool, "rb_idx")
    rb_tok = nc.alloc_register(mybir.EngineType.Pool, "rb_tok")
    rb_yab = nc.alloc_register(mybir.EngineType.Pool, "rb_yab")
    em.sched["gp"].append(lambda e: e.reg_mov(rb_idx, 8 * CAP - 1))
    em.sched["gp"].append(lambda e: e.reg_mov(rb_tok, NTOK - 1))
    em.sched["gp"].append(lambda e: e.reg_mov(rb_yab, 2 * NTOK - 1))

    em.dma_series("sy", "csml", [
        (lambda e: e.dma_start(idbf[:], ident_bf_c), (), ("idbf",)),
        (lambda e: e.dma_start(idf[:], ident_f_c), (), ("idf",)),
        (lambda e: e.dma_start(iota128[:], iota128_c), (), ("iota128",)),
        (lambda e: e.dma_start(iote[:], iote_c), (), ("iote",)),
        (lambda e: e.dma_start(ecap[:], ecap_c), (), ("ecap",)),
        (lambda e: e.dma_start(rscol[:], rs_in), (), ("rscol",)),
        (lambda e: e.dma_start(WF[0:NE, F2_B1N:F2_B1N + HF], b1), (),
         _wfg(F2_B1N, F2_B1N + HF)),
        (lambda e: e.dma_start(WF[0:1, F2_BS1N:F2_BS1N + HF],
                               bs1.rearrange("(o j) -> o j", o=1)), (),
         _wfg(F2_BS1N, F2_BS1N + HF)),
    ])
    op("gp", lambda e: e.dma_start(
        gwsb[:], gW.rearrange("(dc p) e -> p dc e", p=128)),
        writes=("gwsb",), ch="gwl")
    op("gp", lambda e: e.dma_start(IDXS[:, 0:8 * NUNIT], bigidx_c),
        writes=("idxbig",), ch="ibig")
    HALF_IDX = 4 * CAP
    for hh_ in range(2):
        op("gp", lambda e, hh_=hh_: e.dma_start(
            idx_scr[hh_ * HALF_IDX:(hh_ + 1) * HALF_IDX].rearrange(
                "(p k) o -> p k o", p=128),
            IDXS[:, 0:8 * NUNIT].rearrange("p (k o) -> p k o", o=1)),
            reads=("idxbig",), writes=("idxscr",), ch="idxp")
    op("ve", lambda e: e.memset(BASEE[:], 0.0), writes=("basee",))
    op("ve", lambda e: e.memset(META[:], 0.0), writes=("meta0", "meta1"))

    for jc in range(JCN):
        op("pe", lambda e, jc=jc: e.transpose(
            pG[0:128, 0:NE],
            WF[0:NE, F2_B1N + jc * 128:F2_B1N + (jc + 1) * 128],
            idf[0:NE, 0:NE]),
            reads=_wfg(F2_B1N, F2_B1N + HF) + ("idf",), writes=("pG:0",))
        op("ve", lambda e, jc=jc: e.tensor_copy(
            b1T[:, jc * 8:(jc + 1) * 8], pG[0:128, 0:NE]),
            reads=("pG:0",), writes=("b1T",))
    for jc in range(JCN):
        op("pe", lambda e, jc=jc: e.transpose(
            pG[0:128, 8:9],
            WF[0:1, F2_BS1N + jc * 128:F2_BS1N + (jc + 1) * 128],
            idf[0:1, 0:1]),
            reads=_wfg(F2_BS1N, F2_BS1N + HF) + ("idf",), writes=("pG:0",))
        op("ve", lambda e, jc=jc: e.tensor_copy(
            bs1T[:, jc:jc + 1], pG[0:128, 8:9]), reads=("pG:0",), writes=("bs1T",))

    def load_nat(dram, base, ch):
        op("gp", lambda e, dram=dram, base=base: e.dma_start(
            AR[:, base:base + 8192].rearrange("p (ic d) -> p ic d", d=D),
            dram.rearrange("(ic p) d -> p ic d", p=128)),
            writes=_arg(base, base + 8192), ch=ch)

    def transpose_weight(natbase, dstbase):
        for dc in range(DC):
            fns = []
            for ic in range(DC):
                fns.append(lambda e, dc=dc, ic=ic: e.transpose(
                    pbf[:, ic * 128:(ic + 1) * 128],
                    AR[:, natbase + ic * 1024 + dc * 128:
                       natbase + ic * 1024 + (dc + 1) * 128],
                    idbf[:]))
            grp("pe", fns, reads=_arg(natbase, natbase + 8192) + ("idbf",),
                writes=("pbf",))
            op("ac", lambda e, dc=dc: e.activation(
                AR[:, dstbase + dc * 1024:dstbase + (dc + 1) * 1024], pbf[:],
                AF.Copy), reads=("pbf",),
                writes=_arg(dstbase + dc * 1024, dstbase + (dc + 1) * 1024))

    op("ve", lambda e: e.memset(ASM[:], 0.0),
        writes=tuple(f"asm:{s}" for s in range(nseq)))
    load_nat(Wq, A_NAT1, "natq")
    load_nat(Wk, A_NAT2, "natk")
    transpose_weight(A_NAT1, A_WQT)
    transpose_weight(A_NAT2, A_WKT)

    w1items = []
    for e_ in range(NE):
        w1items.append((lambda e, e_=e_: e.dma_start(
            AR[:, A_W1 + e_ * 4096:A_W1 + (e_ + 1) * 4096].rearrange(
                "p (dc j) -> p dc j", j=HF),
            W1[e_].rearrange("(dc p) j -> p dc j", p=128)), (),
            _arg(A_W1 + e_ * 4096, A_W1 + (e_ + 1) * 4096)))
    w1items.append((lambda e: e.dma_start(
        AR[:, A_WS1:A_WS1 + 4096].rearrange("p (dc j) -> p dc j", j=HF),
        Ws1.rearrange("(dc p) j -> p dc j", p=128)), (),
        _arg(A_WS1, A_WS1 + 4096)))
    w1items.append((lambda e: e.dma_start(
        AR[:, A_WS2:A_WS2 + 4096].rearrange("p (jc d) -> p jc d", d=D),
        Ws2.rearrange("(jc p) d -> p jc d", p=128)), (),
        _arg(A_WS2, A_WS2 + 4096)))
    em.dma_series("gp", "w1s", w1items)

    if phases < 1:
        _finish(em)
        return _realize(nc, em)
    def stat(p3, k):
        return STT[:, p3 * 8 + k:p3 * 8 + k + 1]

    def xload(s, u):
        p3 = u % 3
        t0 = u * 128
        xg = _wfg(F_X1 + p3 * 1024, F_X1 + (p3 + 1) * 1024)
        xap = WF[:, F_X1 + p3 * 1024:F_X1 + (p3 + 1) * 1024]
        op("sy", lambda e: e.dma_start(
            xap, x_in[s, t0:t0 + 128, :]), writes=xg, ch=f"xl{p3}")

    def lnpart(s, u):
        p3 = u % 3
        t0 = u * 128
        xg = _wfg(F_X1 + p3 * 1024, F_X1 + (p3 + 1) * 1024)
        xap = WF[:, F_X1 + p3 * 1024:F_X1 + (p3 + 1) * 1024]
        stn = f"st:{p3}"
        op("ve", lambda e: e.reduce_sum(
            stat(p3, 0), xap, axis=AX), reads=xg, writes=(stn,))
        op("ac", lambda e: e.activation(
            WF[:, F_SC2:F_SC2 + 1024], xap, AF.Square,
            accum_out=stat(p3, 1)),
            reads=xg, writes=_wfg(F_SC2, F_SC2 + 1024) + (stn,))
        op("ve", lambda e: e.tensor_scalar(
            stat(p3, 2), stat(p3, 0), 1.0 / D, None, ALU.mult),
            reads=(stn,), writes=(stn,))
        op("ve", lambda e: e.tensor_scalar(
            stat(p3, 3), stat(p3, 1), 1.0 / D, None, ALU.mult),
            reads=(stn,), writes=(stn,))
        op("ve", lambda e: e.scalar_tensor_tensor(
            stat(p3, 4), stat(p3, 2), -1.0, stat(p3, 2),
            ALU.mult, ALU.mult), reads=(stn,), writes=(stn,))
        op("ve", lambda e: e.tensor_add(
            stat(p3, 3), stat(p3, 3), stat(p3, 4)),
            reads=(stn,), writes=(stn,))
        op("ac", lambda e: e.activation(
            stat(p3, 5), stat(p3, 3), AF.Ln, bias=np.float32(EPS)),
            reads=(stn,), writes=(stn,))
        op("ac", lambda e: e.activation(
            stat(p3, 6), stat(p3, 5), AF.Exp, scale=np.float32(-0.5)),
            reads=(stn,), writes=(stn,))
        op("ve", lambda e: e.tensor_scalar(
            WH[:, H_XNTM + p3 * 1024:H_XNTM + (p3 + 1) * 1024],
            xap, stat(p3, 2), stat(p3, 6), ALU.subtract, ALU.mult),
            reads=xg + (stn,),
            writes=_whg(H_XNTM + p3 * 1024, H_XNTM + (p3 + 1) * 1024))

    def txpart(s, u):
        p3 = u % 3
        t0 = u * 128
        fns = []
        for c in range(DC):
            fns.append(lambda e, c=c, p3=p3: e.transpose(
                pbf[:, c * 128:(c + 1) * 128],
                WH[:, H_XNTM + p3 * 1024 + c * 128:
                   H_XNTM + p3 * 1024 + (c + 1) * 128], idbf[:]))
        grp("pe", fns,
            reads=_whg(H_XNTM + p3 * 1024, H_XNTM + (p3 + 1) * 1024) + ("idbf",),
            writes=("pbf",))
        xnfg = _whg(H_XNF + p3 * 1024, H_XNF + (p3 + 1) * 1024)
        op("ac", lambda e: e.activation(
            WH[:, H_XNF + p3 * 1024:H_XNF + (p3 + 1) * 1024], pbf[:],
            AF.Copy), reads=("pbf",), writes=xnfg)
        op("gp", lambda e: e.dma_start(
            xn_scr[s, :, :, t0:t0 + 128].rearrange("c p t -> p c t"),
            WH[:, H_XNF + p3 * 1024:H_XNF + (p3 + 1) * 1024].rearrange(
                "p (c t) -> p c t", t=128)),
            reads=xnfg, writes=(f"xnscr:{s}:{u}",), ch=f"xnst{p3}")
        op("gp", lambda e: e.dma_start(
            xm_scr[s * S + t0:s * S + t0 + 128, 0:512],
            WH[:, H_XNTM + p3 * 1024:H_XNTM + (p3 + 1) * 1024].bitcast(F32)),
            reads=_whg(H_XNTM + p3 * 1024, H_XNTM + (p3 + 1) * 1024),
            writes=(f"xmscr:{s}:{u}",), ch=f"xmst{u % 2}")

    def pepart(s, u):
        p3 = u % 3
        t0 = u * 128
        xnfg = _whg(H_XNF + p3 * 1024, H_XNF + (p3 + 1) * 1024)
        xnf0 = H_XNF + p3 * 1024
        fns = []
        for h in range(2):
            for c in range(DC):
                fns.append(lambda e, h=h, c=c: e.matmul(
                    pY1[:, h * 512:(h + 1) * 512],
                    WH[:, xnf0 + c * 128:xnf0 + (c + 1) * 128],
                    AR[:, A_WQT + c * 1024 + h * 512:
                       A_WQT + c * 1024 + (h + 1) * 512],
                    start=(c == 0), stop=(c == DC - 1)))
        grp("pe", fns, reads=xnfg + _arg(A_WQT, A_WQT + 8192), writes=("pY1",))
        qg = _whg(H_Q, H_Q + 1024)
        op("ac", lambda e: e.activation(
            WH[:, H_Q:H_Q + 1024], pY1[:],
            AF.Copy, scale=np.float32(0.125)), reads=("pY1",), writes=qg)
        kg = _whg(H_K, H_K + 1024)
        for h, pp, pname in ((0, pfa, "pfa"), (1, pfb, "pfb")):
            fns = []
            for c in range(DC):
                fns.append(lambda e, h=h, c=c, pp=pp: e.matmul(
                    pp[:],
                    WH[:, xnf0 + c * 128:xnf0 + (c + 1) * 128],
                    AR[:, A_WKT + c * 1024 + h * 512:
                       A_WKT + c * 1024 + (h + 1) * 512],
                    start=(c == 0), stop=(c == DC - 1)))
            grp("pe", fns, reads=xnfg + _arg(A_WKT, A_WKT + 8192),
                writes=(pname,))
            op("ac", lambda e, h=h, pp=pp: e.activation(
                WH[:, H_K + h * 512:H_K + (h + 1) * 512], pp[:], AF.Copy),
                reads=(pname,), writes=kg)
    def pepart_a(s, u):
        qg = _whg(H_Q, H_Q + 1024)
        kg = _whg(H_K, H_K + 1024)
        fns = []
        for p in range(NPAIR):
            fns.append(lambda e, p=p: e.matmul(
                pY0[:, p * 128:(p + 1) * 128],
                WH[:, H_K + p * 128:H_K + (p + 1) * 128],
                WH[:, H_Q + p * 128:H_Q + (p + 1) * 128],
                start=True, stop=True, skip_group_check=True))
        grp("pe", fns, reads=kg + qg, writes=("pY0",))
        if u == 0:
            op("ve", lambda e: e.tensor_copy(ASB[:], pY0[:]),
                reads=("pY0",), writes=("asb",))
        else:
            op("ve", lambda e: e.tensor_add(ASB[:], ASB[:], pY0[:]),
                reads=("pY0", "asb"), writes=("asb",))
    def gatemm(s, u):
        t0 = u * 128
        p3 = u % 3
        xnfg = _whg(H_XNF + p3 * 1024, H_XNF + (p3 + 1) * 1024)
        xnf0 = H_XNF + p3 * 1024
        fns = []
        for c in range(DC):
            fns.append(lambda e, c=c: e.matmul(
                pG[0:NE, 0:128], gwsb[:, c * 8:(c + 1) * 8],
                WH[:, xnf0 + c * 128:xnf0 + (c + 1) * 128],
                start=(c == 0), stop=(c == DC - 1)))
        grp("pe", fns, reads=xnfg + ("gwsb",), writes=("pG:0",))
        op("ac", lambda e: e.activation(
            b2all[0:NE, 0:128], pG[0:NE, 0:128], AF.Exp, scale=np.float32(-1.0)),
            reads=("pG:0",), writes=("sig1",))
        op("ve", lambda e: e.tensor_scalar(
            b2all[0:NE, 128:256], b2all[0:NE, 0:128], 1.0, None, ALU.add),
            reads=("sig1",), writes=("sig2",))
        op("ve", lambda e: e.reciprocal(
            b2all[0:NE, 256:384], b2all[0:NE, 128:256]),
            reads=("sig2",), writes=("sig3",))
        op("ve", lambda e: e.tensor_scalar(
            WF[0:NE, F_SCORES + t0:F_SCORES + t0 + 128],
            b2all[0:NE, 256:384], rscol[:], None, ALU.mult),
            reads=("sig3", "rscol"),
            writes=_wfg(F_SCORES + t0, F_SCORES + t0 + 128))
    def gatepart(s, u):
        t0 = u * 128
        scg = _wfg(F_SCORES + t0, F_SCORES + t0 + 128)
        mpar = u % 2
        mslc = lambda a, b: META[:, mpar * 8 + a:mpar * 8 + b]
        tb = lambda a, b: T2B[:, a:b]
        op("pe", lambda e: e.transpose(
            pfa[:, 0:8], WF[0:NE, F_SCORES + t0:F_SCORES + t0 + 128],
            idf[0:NE, 0:NE]), reads=scg + ("idf",), writes=("pfa",))
        op("ac", lambda e: e.activation(tb(0, 8), pfa[:, 0:8], AF.Copy),
            reads=("pfa",), writes=("t2raw",))
        op("ve", lambda e: e.reduce_max(tb(32, 33), tb(0, 8), axis=AX),
            reads=("t2raw",), writes=("t2m",))
        op("ve", lambda e: e.tensor_scalar(
            tb(8, 16), tb(0, 8), tb(32, 33), None, ALU.is_equal),
            reads=("t2raw", "t2m"), writes=("t2eq1",))
        op("ve", lambda e: e.tensor_tensor_scan(
            tb(16, 24), tb(8, 16), tb(8, 16), 0.0, ALU.add, ALU.bypass),
            reads=("t2eq1",), writes=("t2msk",))
        op("ve", lambda e: e.tensor_scalar(
            tb(16, 24), tb(16, 24), 1.0, None, ALU.is_equal),
            reads=("t2msk",), writes=("t2msk",))
        op("ve", lambda e: e.tensor_mul(tb(8, 16), tb(8, 16), tb(16, 24)),
            reads=("t2eq1", "t2msk"), writes=("t2eq1",))
        op("ve", lambda e: e.scalar_tensor_tensor(
            tb(16, 24), tb(8, 16), -BIG, tb(0, 8), ALU.mult, ALU.add),
            reads=("t2eq1", "t2raw"), writes=("t2msk",))
        op("ve", lambda e: e.reduce_max(tb(33, 34), tb(16, 24), axis=AX),
            reads=("t2msk",), writes=("t2m",))
        op("ve", lambda e: e.tensor_scalar(
            tb(24, 32), tb(16, 24), tb(33, 34), None, ALU.is_equal),
            reads=("t2msk", "t2m"), writes=("t2eq2",))
        op("ve", lambda e: e.tensor_tensor_scan(
            tb(16, 24), tb(24, 32), tb(24, 32), 0.0, ALU.add, ALU.bypass),
            reads=("t2eq2",), writes=("t2msk",))
        op("ve", lambda e: e.tensor_scalar(
            tb(16, 24), tb(16, 24), 1.0, None, ALU.is_equal),
            reads=("t2msk",), writes=("t2msk",))
        op("ve", lambda e: e.tensor_mul(tb(24, 32), tb(24, 32), tb(16, 24)),
            reads=("t2eq2", "t2msk"), writes=("t2eq2",))
        op("ve", lambda e: e.tensor_add(tb(34, 35), tb(32, 33), tb(33, 34)),
            reads=("t2m",), writes=("t2den",))
        op("ve", lambda e: e.reciprocal(tb(35, 36), tb(34, 35)),
            reads=("t2den",), writes=("t2rde",))
        op("ve", lambda e: e.tensor_mul(mslc(0, 1), tb(32, 33), tb(35, 36)),
            reads=("t2m", "t2rde"), writes=(f"meta{mpar}",))
        op("ve", lambda e: e.tensor_mul(mslc(1, 2), tb(33, 34), tb(35, 36)),
            reads=("t2m", "t2rde"), writes=(f"meta{mpar}",))
        op("ve", lambda e: e.tensor_mul(tb(16, 24), tb(8, 16), iote[:]),
            reads=("t2eq1", "iote"), writes=("t2msk",))
        op("ve", lambda e: e.reduce_sum(mslc(2, 3), tb(16, 24), axis=AX),
            reads=("t2msk",), writes=(f"meta{mpar}",))
        op("ve", lambda e: e.tensor_mul(tb(16, 24), tb(24, 32), iote[:]),
            reads=("t2eq2", "iote"), writes=("t2msk",))
        op("ve", lambda e: e.reduce_sum(mslc(3, 4), tb(16, 24), axis=AX),
            reads=("t2msk",), writes=(f"meta{mpar}",))
        op("ve", lambda e: e.tensor_scalar(
            mslc(4, 5), iota128[:], float(s * S + t0), None, ALU.add),
            reads=("iota128",), writes=(f"meta{mpar}",))
        op("ve", lambda e: e.tensor_copy(EQS[:, u * 16:u * 16 + 8], tb(8, 16)),
            reads=("t2eq1",), writes=("eqs",))
        op("ve", lambda e: e.tensor_copy(EQS[:, u * 16 + 8:u * 16 + 16], tb(24, 32)),
            reads=("t2eq2",), writes=("eqs",))
        op("ve", lambda e: e.tensor_scalar(
            tb(0, 8), tb(8, 16), mslc(0, 1), None, ALU.mult),
            reads=("t2eq1", f"meta{mpar}"), writes=("t2raw",))
        op("ve", lambda e: e.tensor_scalar(
            tb(16, 24), tb(24, 32), mslc(1, 2), None, ALU.mult),
            reads=("t2eq2", f"meta{mpar}"), writes=("t2msk",))
        op("ve", lambda e: e.tensor_add(tb(0, 8), tb(0, 8), tb(16, 24)),
            reads=("t2raw", "t2msk"), writes=("t2raw",))
        op("gp", lambda e: e.dma_start(
            xm_scr[s * S + t0:s * S + t0 + 128, 512:520],
            META[:, mpar * 8:mpar * 8 + 8]),
            reads=(f"meta{mpar}",), writes=(f"xmscr2:{s}:{u}",),
            ch=f"xmm{mpar}")

    def gatepart_b(s, u):
        t0 = u * 128
        tb = lambda a, b: T2B[:, a:b]
        op("pe", lambda e: e.transpose(
            pG[0:NE, 128:256], tb(0, 8), idf[:]),
            reads=("t2raw", "idf", "pG:0"), writes=("pG:1",))
        gfg = _wfg(F_GFM + t0, F_GFM + t0 + 128)
        op("ac", lambda e: e.activation(
            WF[0:NE, F_GFM + t0:F_GFM + t0 + 128], pG[0:NE, 128:256],
            AF.Copy), reads=("pG:1", "pG:0"), writes=gfg)

    def idxpart(s, u):
        t0 = u * 128
        gfall = _wfg(F_GFM, F_GFM + S)
        mp = u % 2
        tki0 = 4 * (mp + 1)
        op("pe", lambda e, t0=t0: e.transpose(
            pY1[:, 0:8], WF[0:NE, F_GFM + t0:F_GFM + t0 + 128],
            idf[0:NE, 0:NE]), reads=gfall + ("idf",), writes=("pY1",))
        op("ac", lambda e: e.activation(T2B[:, 36:44], pY1[:, 0:8], AF.Copy),
            reads=("pY1",), writes=("eppos",))
        op("ve", lambda e: e.tensor_scalar(
            T2B[:, 44:52], T2B[:, 36:44], float(CAP) + 0.5, None, ALU.is_ge),
            reads=("eppos",), writes=("epge",))
        op("ve", lambda e: e.tensor_add(
            T2B[:, 52:60], T2B[:, 36:44], ecap[:]),
            reads=("eppos", "ecap"), writes=("epmsk",))
        op("ve", lambda e: e.scalar_tensor_tensor(
            T2B[:, 52:60], T2B[:, 44:52], float(1 << 30), T2B[:, 52:60],
            ALU.mult, ALU.add), reads=("epge", "epmsk"), writes=("epmsk",))
        op("ve", lambda e, s=s, t0=t0: e.tensor_scalar(
            T2B[:, 68:69], iota128[:], float(s * S + t0), None, ALU.add),
            reads=("iota128",), writes=("eptk",))
        op("ve", lambda e, tki0=tki0: e.tensor_copy(
            TKI[:, tki0:tki0 + 1], T2B[:, 68:69]),
            reads=("eptk",), writes=(f"eptok{mp}",))
        KTILES = 8 * CAP // 128
        for sl in range(2):
            eqap = EQS[:, u * 16 + sl * 8:u * 16 + sl * 8 + 8]
            op("ve", lambda e, eqap=eqap: e.tensor_mul(
                T2B[:, 60:68], eqap, T2B[:, 52:60]),
                reads=("eqs", "epmsk"), writes=("epeq",))
            op("ve", lambda e: e.reduce_sum(
                T2B[:, 69:70], T2B[:, 60:68], axis=AX),
                reads=("epeq",), writes=("epd",))
            op("ve", lambda e: e.tensor_scalar(
                T2B[:, 70:71], T2B[:, 69:70], 1.0 / 128, None, ALU.mult),
                reads=("epd",), writes=("epk",))
            op("ve", lambda e, tki0=tki0: e.tensor_copy(
                TKI[:, tki0 + 3:tki0 + 4], T2B[:, 70:71]),
                reads=("epk",), writes=(f"epki{mp}",))
            op("ve", lambda e, tki0=tki0: e.tensor_copy(
                T2B[:, 70:71], TKI[:, tki0 + 3:tki0 + 4]),
                reads=(f"epki{mp}",), writes=("epk",))
            op("ve", lambda e: e.scalar_tensor_tensor(
                T2B[:, 71:72], T2B[:, 70:71], -128.0, T2B[:, 69:70],
                ALU.mult, ALU.add), reads=("epk", "epd"), writes=("epr",))
            op("ve", lambda e: e.tensor_scalar(
                T2B[:, 68:69], T2B[:, 71:72], 0.0, None, ALU.is_lt),
                reads=("epr",), writes=("eptk",))
            op("ve", lambda e: e.tensor_sub(
                T2B[:, 70:71], T2B[:, 70:71], T2B[:, 68:69]),
                reads=("epk", "eptk"), writes=("epk",))
            op("ve", lambda e: e.scalar_tensor_tensor(
                T2B[:, 71:72], T2B[:, 68:69], 128.0, T2B[:, 71:72],
                ALU.mult, ALU.add), reads=("eptk", "epr"), writes=("epr",))
            op("ve", lambda e, KTILES=KTILES: e.scalar_tensor_tensor(
                T2B[:, 69:70], T2B[:, 71:72], float(KTILES), T2B[:, 70:71],
                ALU.mult, ALU.add), reads=("epr", "epk"), writes=("epd",))
            op("ve", lambda e, sl=sl, tki0=tki0: e.tensor_copy(
                TKI[:, tki0 + 1 + sl:tki0 + 2 + sl], T2B[:, 69:70]),
                reads=("epd",), writes=(f"epd{sl}{mp}",))
            op("gp", lambda e, sl=sl, tki0=tki0: e.indirect_dma_start(
                idx_scr, bass.IndirectOffsetOnAxis(
                    TKI[:, tki0 + 1 + sl:tki0 + 2 + sl], 0),
                TKI[:, tki0:tki0 + 1], None, bounds_check=rb_idx,
                oob_is_err=False),
                reads=(f"eptok{mp}", f"epd{sl}{mp}", "idxscr"),
                writes=("idxscr",),
                ch=f"ixs{sl}")

    for s in range(nseq):
        for uu in range(min(3, NSUB)):
            xload(s, uu)
        lnpart(s, 0)
        if NSUB > 1:
            lnpart(s, 1)
        txpart(s, 0)
        for u in range(NSUB):
            if u >= 4:
                gatepart_b(s, u - 4)
            if u + 3 < NSUB:
                xload(s, u + 3)
            if u + 2 < NSUB:
                lnpart(s, u + 2)
            if u >= 3:
                gatepart(s, u - 3)
            if u + 1 < NSUB:
                txpart(s, u + 1)
            if s >= 1 and u == 0:
                idxpart(s - 1, 0)
            pepart(s, u)
            gatemm(s, u)
            pepart_a(s, u)
            if s >= 1 and u + 1 < NSUB:
                idxpart(s - 1, u + 1)
        for u in range(max(0, NSUB - 4), NSUB):
            if u <= NSUB - 4:
                gatepart_b(s, u)
            else:
                gatepart(s, u)
                gatepart_b(s, u)
        op("ve", lambda e: e.reduce_max(
            SMX[:, 0:16], ASB[:].rearrange("p (a q) -> p a q", q=64), axis=AX),
            reads=("asb",), writes=("smx_m",))
        op("ve", lambda e: e.tensor_scalar(
            SMX[:, 16:32], SMX[:, 0:16], -1.0, None, ALU.mult),
            reads=("smx_m",), writes=("smx_n",))
        op("ve", lambda e: e.memset(SMX[:, 32:48], 1.0), writes=("smx_s",))
        for hh in range(NH):
            p, o = hh // 2, hh % 2
            r0 = 64 * o
            a = 2 * p + o
            op("ac", lambda e, s=s, p=p, r0=r0, a=a, o=o: e.activation(
                ASM[r0:r0 + 64, s * 1024 + p * 128 + o * 64:
                    s * 1024 + p * 128 + o * 64 + 64],
                ASB[r0:r0 + 64, p * 128 + o * 64:p * 128 + o * 64 + 64],
                AF.Exp, bias=SMX[r0:r0 + 64, 16 + a:16 + a + 1],
                accum_out=SMX[r0:r0 + 64, 32 + a:32 + a + 1]),
                reads=("asb", "smx_n"),
                writes=(f"asm:{s}", "smx_s"))
        op("ve", lambda e, s=s: e.reciprocal(
            RSM[0:64, s * 8:s * 8 + 8],
            SMX[0:64, 32:48].rearrange("p (a two) -> p a two", two=2)[:, :, 0]),
            reads=("smx_s",), writes=(f"rsm:{s}",))
        op("ve", lambda e, s=s: e.reciprocal(
            RSM[64:128, s * 8:s * 8 + 8],
            SMX[64:128, 32:48].rearrange("p (a two) -> p a two", two=2)[:, :, 1]),
            reads=("smx_s",), writes=(f"rsm:{s}",))

        gfall = _wfg(F_GFM, F_GFM + S)
        op("gp", lambda e, s=s: e.dma_start(
            gate_scr[s], WF[0:NE, F_GFM:F_GFM + S]),
            reads=gfall, writes=(f"gscr:{s}",), ch="gst")
        NHALF = (S + 1023) // 1024
        for hh in range(NHALF):
            hc = hh * 1024
            hw = min(1024, S - hc)
            op("ve", lambda e, hc=hc, hw=hw: e.tensor_scalar(
                WF[0:NE, F_SC2:F_SC2 + hw],
                WF[0:NE, F_GFM + hc:F_GFM + hc + hw], 0.0, None, ALU.is_gt),
                reads=gfall, writes=_wfg(F_SC2, F_SC2 + 1024))
            if hh == 0:
                ini = 0.0
            else:
                ini = WF[0:NE, F_GFM + hc - 1:F_GFM + hc]
            op("ve", lambda e, hc=hc, hw=hw, ini=ini: e.tensor_tensor_scan(
                WF[0:NE, F_GFM + hc:F_GFM + hc + hw],
                WF[0:NE, F_SC2:F_SC2 + hw], WF[0:NE, F_SC2:F_SC2 + hw],
                ini, ALU.add, ALU.bypass),
                reads=_wfg(F_SC2, F_SC2 + 1024) + gfall, writes=gfall)
        op("ve", lambda e: e.tensor_scalar(
            WF[0:NE, F_GFM:F_GFM + S], WF[0:NE, F_GFM:F_GFM + S],
            BASEE[:, 0:1], None, ALU.add),
            reads=gfall + ("basee",), writes=gfall)
        op("ve", lambda e: e.tensor_copy(
            BASEE[:, 0:1], WF[0:NE, F_GFM + S - 1:F_GFM + S]),
            reads=gfall, writes=("basee",))
    for u in range(NSUB):
        idxpart(nseq - 1, u)

    if phases < 2:
        _finish(em)
        return _realize(nc, em)
    for sl in range(2):
        ch = f"ixs{sl}"
        if ch in em.cnt:
            em.wtok[f"idxfin{sl}"] = (ch, em.cnt[ch])
    op("sy", lambda e: e.dma_start(
        IDXS[:], idx_scr.rearrange("(p k) o -> p (k o)", p=128)),
        reads=("idxscr", "idxfin0", "idxfin1", "idxbig"),
        writes=("idxall",), ch="ixall")
    for i_, ch in enumerate(("xmst0", "xmst1", "xmm0", "xmm1")):
        if ch in em.cnt:
            em.retag([k for k in em.wtok if k.startswith("xmscr")
                      and em.wtok[k][0] == ch], (ch, em.cnt[ch]))
            em.wtok[f"xmfin{i_}"] = (ch, em.cnt[ch])
    for ch in ("xnst0", "xnst1", "xnst2"):
        if ch in self_cnt(em):
            tokf = (ch, em.cnt[ch])
            em.retag([k for k in em.wtok if k.startswith("xnscr:")
                      and em.wtok[k][0] == ch], tokf)
    if "gst" in em.cnt:
        em.retag([k for k in em.wtok if k.startswith("gscr:")],
                 ("gst", em.cnt["gst"]))
    w2items = []
    for e_ in range(NE):
        w2items.append((lambda e, e_=e_: e.dma_start(
            AR[:, A_W2 + e_ * 4096:A_W2 + (e_ + 1) * 4096].rearrange(
                "p (jc d) -> p jc d", d=D),
            W2[e_].rearrange("(jc p) d -> p jc d", p=128)), (),
            _arg(A_W2 + e_ * 4096, A_W2 + (e_ + 1) * 4096)))
    em.dma_series("gp", "w2s", w2items)
    em.dma_series("sy", "b2l", [
        (lambda e: e.dma_start(b2all[0:NE, :], b2), (),
         ("b2sb", "sig1", "sig2", "sig3")),
        (lambda e: e.dma_start(b2all[8:9, :], bs2.rearrange("(o d) -> o d", o=1)),
         (), ("bs2sb",)),
    ])

    op("ve", lambda e: e.memset(GT9[0:9, :], 1.0),
        writes=("gt9ones", "gt9:0", "gt9:1"))

    OOBF = float(1 << 30)
    NGU = NE * NUNIT

    def xg_col(gidx):
        return (F2_XG0, 1024, 2048, F2_XG1)[gidx % 4]

    for slot_ in range(4):
        xgc_ = (F2_XG0, 1024, 2048, F2_XG1)[slot_]
        op("ve", lambda e, xgc_=xgc_: e.memset(WF[:, xgc_:xgc_ + 520], 0.0),
            writes=(f"xg:{slot_}", f"wf:{xgc_ // 1024}"))

    def emit_gather(gidx):
        xgc = xg_col(gidx)
        xgg = (f"xg:{gidx % 4}", f"wf:{xgc // 1024}")
        op("gp", lambda e, xgc=xgc, gidx=gidx: e.indirect_dma_start(
            WF[:, xgc:xgc + 520], None, xm_scr,
            bass.IndirectOffsetOnAxis(IDXS[:, gidx:gidx + 1], 0),
            bounds_check=rb_tok, oob_is_err=False),
            reads=("idxall", "xmfin0", "xmfin1", "xmfin2", "xmfin3"),
            writes=xgg, ch=f"xgl{gidx % 4}")

    def emit_tx(guni):
        xngb = H2_XNF + (guni % 2) * 2048
        xng = _whg(xngb, xngb + 2048)
        for sub in range(2):
            gidx = guni * 2 + sub
            xgc = xg_col(gidx)
            xgg = (f"xg:{gidx % 4}", f"wf:{xgc // 1024}")
            fns = []
            for c in range(DC):
                fns.append(lambda e, c=c, xgc=xgc: e.transpose(
                    pbf[:, c * 128:(c + 1) * 128],
                    WF[:, xgc:xgc + 512].bitcast(BF16)[:, c * 128:(c + 1) * 128],
                    idbf[:]))
            grp("pe", fns, reads=xgg + ("idbf",), writes=("pbf",))
            op("ac", lambda e, sub=sub, xngb=xngb: e.activation(
                WH[:, xngb:xngb + 2048].rearrange(
                    "p (c two t) -> p c two t", two=2, t=128)[:, :, sub, :],
                pbf[:].rearrange("p (c t) -> p c t", t=128), AF.Copy),
                reads=("pbf",), writes=xng)

    def emit_w1(guni):
        e_x = guni // NUNIT
        xngb = H2_XNF + (guni % 2) * 2048
        xng = _whg(xngb, xngb + 2048)
        upar = guni % 2
        hsg = _whg(H2_HSC + upar * 1024, H2_HSC + (upar + 1) * 1024)
        for half, pp, pname in ((0, pfa, "pfa"), (1, pfb, "pfb")):
            fns = []
            for jo in range(2):
                jc = half * 2 + jo
                for c in range(DC):
                    a0 = A_W1 + e_x * 4096 + c * 512 + jc * 128
                    fns.append(lambda e, jo=jo, c=c, a0=a0, pp=pp, xngb=xngb:
                        e.matmul(
                            pp[:, jo * 256:(jo + 1) * 256],
                            AR[:, a0:a0 + 128],
                            WH[:, xngb + c * 256:xngb + (c + 1) * 256],
                            start=(c == 0), stop=(c == DC - 1)))
            grp("pe", fns,
                reads=xng + _arg(A_W1 + e_x * 4096, A_W1 + (e_x + 1) * 4096),
                writes=(pname,))
            for jo in range(2):
                jc = half * 2 + jo
                op("ac", lambda e, jc=jc, jo=jo, pp=pp, upar=upar, e_x=e_x:
                    e.activation(
                        WH[:, H2_HSC + upar * 1024 + jc * 256:
                           H2_HSC + upar * 1024 + (jc + 1) * 256],
                        pp[:, jo * 256:(jo + 1) * 256], AF.Gelu,
                        bias=b1T[:, jc * 8 + e_x:jc * 8 + e_x + 1]),
                    reads=(pname, "b1T"), writes=hsg)

    def emit_meta(guni):
        e_x = guni // NUNIT
        for sub in range(2):
            gidx = guni * 2 + sub
            xgc = xg_col(gidx)
            xgg = (f"xg:{gidx % 4}", f"wf:{xgc // 1024}")
            b = 8 * sub
            nm = f"spm{sub}"
            op("ve", lambda e, xgc=xgc, e_x=e_x, b=b: e.tensor_scalar(
                T2B[:, b:b + 1], WF[:, xgc + 514:xgc + 515], float(e_x), None,
                ALU.is_equal), reads=xgg, writes=(nm,))
            op("ve", lambda e, xgc=xgc, e_x=e_x, b=b: e.tensor_scalar(
                T2B[:, b + 1:b + 2], WF[:, xgc + 515:xgc + 516], float(e_x),
                None, ALU.is_equal), reads=xgg + (nm,), writes=(nm,))
            op("ve", lambda e, xgc=xgc, b=b: e.tensor_mul(
                T2B[:, b + 2:b + 3], WF[:, xgc + 512:xgc + 513],
                T2B[:, b:b + 1]), reads=xgg + (nm,), writes=(nm,))
            op("ve", lambda e, xgc=xgc, b=b: e.tensor_mul(
                T2B[:, b + 3:b + 4], WF[:, xgc + 513:xgc + 514],
                T2B[:, b + 1:b + 2]), reads=xgg + (nm,), writes=(nm,))
            op("ve", lambda e, b=b: e.tensor_add(
                T2B[:, b + 2:b + 3], T2B[:, b + 2:b + 3], T2B[:, b + 3:b + 4]),
                reads=(nm,), writes=(nm,))
            op("ve", lambda e, xgc=xgc, b=b: e.scalar_tensor_tensor(
                T2B[:, b + 4:b + 5], T2B[:, b + 1:b + 2], float(NTOK),
                WF[:, xgc + 516:xgc + 517], ALU.mult, ALU.add),
                reads=(nm,) + xgg, writes=(nm,))
            op("ve", lambda e, gidx=gidx, b=b: e.tensor_copy(
                T2B[:, b + 5:b + 6], IDXS[:, gidx:gidx + 1]),
                reads=("idxall", nm), writes=(nm,))
            op("ve", lambda e, b=b: e.tensor_scalar(
                T2B[:, b + 6:b + 7], T2B[:, b + 5:b + 6], float(NTOK), None,
                ALU.is_lt), reads=(nm,), writes=(nm,))
            op("ve", lambda e, b=b: e.tensor_scalar(
                T2B[:, b + 4:b + 5], T2B[:, b + 4:b + 5], float(2 * NTOK),
                None, ALU.subtract), reads=(nm,), writes=(nm,))
            op("ve", lambda e, b=b: e.tensor_mul(
                T2B[:, b + 4:b + 5], T2B[:, b + 4:b + 5], T2B[:, b + 6:b + 7]),
                reads=(nm,), writes=(nm,))
            op("ve", lambda e, b=b: e.tensor_scalar(
                T2B[:, b + 4:b + 5], T2B[:, b + 4:b + 5], float(2 * NTOK),
                None, ALU.add), reads=(nm,), writes=(nm,))
            op("ve", lambda e, gidx=gidx, b=b: e.tensor_copy(
                TKI[:, 2 + gidx % 2:3 + gidx % 2], T2B[:, b + 4:b + 5]),
                reads=(nm,), writes=(f"ysd{gidx % 2}",))

    def emit_w2(guni):
        e_x = guni // NUNIT
        upar = guni % 2
        hsg = _whg(H2_HSC + upar * 1024, H2_HSC + (upar + 1) * 1024)
        for sub in range(2):
            gidx = guni * 2 + sub
            pY, pYn = (pY0, "pY0") if sub == 0 else (pY1, "pY1")
            fns = []
            for h in range(2):
                for jc in range(JCN):
                    a0 = A_W2 + e_x * 4096 + jc * 1024 + h * 512
                    fns.append(lambda e, jc=jc, h=h, a0=a0, pY=pY,
                               upar=upar, sub=sub: e.matmul(
                        pY[:, h * 512:(h + 1) * 512],
                        WH[:, H2_HSC + upar * 1024 + jc * 256 + sub * 128:
                           H2_HSC + upar * 1024 + jc * 256 + sub * 128 + 128],
                        AR[:, a0:a0 + 512],
                        start=(jc == 0), stop=(jc == JCN - 1)))
            grp("pe", fns,
                reads=hsg + _arg(A_W2 + e_x * 4096, A_W2 + (e_x + 1) * 4096),
                writes=(pYn,))
            ysc = F2_YS + (gidx % 2) * 1024
            ysg = (f"ys:{gidx % 2}", "wf:3" if gidx % 2 == 0 else "wf:4")
            op("ac", lambda e, pY=pY, ysc=ysc, sub=sub: e.activation(
                WF[:, ysc:ysc + 1024], pY[:], AF.Copy,
                scale=T2B[:, 8 * sub + 2:8 * sub + 3]),
                reads=(pYn, f"spm{sub}"), writes=ysg)
            op("gp", lambda e, ysc=ysc, gidx=gidx: e.indirect_dma_start(
                yab_scr, bass.IndirectOffsetOnAxis(
                    TKI[:, 2 + gidx % 2:3 + gidx % 2], 0),
                WF[:, ysc:ysc + 1024], None),
                reads=ysg + (f"ysd{gidx % 2}",), writes=(f"yab{gidx % 2}",),
                ch=f"ysc{gidx % 2}")

    for sub in range(4):
        emit_gather(sub)
    emit_tx(0)
    for guni in range(NGU):
        emit_w1(guni)
        if guni + 1 < NGU:
            emit_tx(guni + 1)
        emit_meta(guni)
        if guni + 2 < NGU:
            for sub in range(2):
                emit_gather((guni + 2) * 2 + sub)
        emit_w2(guni)

    def sh_load(ti):
        si, c0 = ti // T2PS, (ti % T2PS) * 256
        par = ti % 2
        u0 = c0 // 128
        xn2g = _whg(H2_XNF + par * 2048, H2_XNF + (par + 1) * 2048)
        op("sy", lambda e, si=si, c0=c0, par=par: e.dma_start(
            WH[:, H2_XNF + par * 2048:H2_XNF + (par + 1) * 2048].rearrange(
                "p (c t) -> p c t", t=256),
            xn_scr[si, :, :, c0:c0 + 256].rearrange("c p t -> p c t")),
            reads=(f"xnscr:{si}:{u0}", f"xnscr:{si}:{u0 + 1}"), writes=xn2g,
            ch=f"xn2l{par}")
        op("sy", lambda e, si=si, c0=c0, par=par: e.dma_start(
            GT9[0:NE, par * 256:(par + 1) * 256],
            gate_scr[si, :, c0:c0 + 256]),
            reads=(f"gscr:{si}",), writes=(f"gt9:{par}",), ch=f"gtl{par}")

    sh_load(0)
    for ti in range(NT2):
        if ti + 1 < NT2:
            sh_load(ti + 1)
        si, c0 = ti // T2PS, (ti % T2PS) * 256
        par = ti % 2
        xn2g = _whg(H2_XNF + par * 2048, H2_XNF + (par + 1) * 2048)
        gtg = (f"gt9:{par}",)
        hsg = _whg(H2_HSC + par * 1024, H2_HSC + (par + 1) * 1024)
        for half, pp, pname in ((0, pfa, "pfa"), (1, pfb, "pfb")):
            fns = []
            for jo in range(2):
                jc = half * 2 + jo
                for c in range(DC):
                    a0 = A_WS1 + c * 512 + jc * 128
                    fns.append(lambda e, jo=jo, c=c, a0=a0, pp=pp, par=par:
                        e.matmul(
                            pp[:, jo * 256:(jo + 1) * 256],
                            AR[:, a0:a0 + 128],
                            WH[:, H2_XNF + par * 2048 + c * 256:
                               H2_XNF + par * 2048 + (c + 1) * 256],
                            start=(c == 0), stop=(c == DC - 1)))
            grp("pe", fns, reads=xn2g + _arg(A_WS1, A_WS1 + 4096),
                writes=(pname,))
            for jo in range(2):
                jc = half * 2 + jo
                op("ac", lambda e, jc=jc, jo=jo, pp=pp, par=par: e.activation(
                    WH[:, H2_HSC + par * 1024 + jc * 256:
                       H2_HSC + par * 1024 + (jc + 1) * 256],
                    pp[:, jo * 256:(jo + 1) * 256], AF.Gelu,
                    bias=bs1T[:, jc:jc + 1]),
                    reads=(pname, "bs1T"), writes=hsg)
        for sub, pY, pYn in ((0, pY0, "pY0"), (1, pY1, "pY1")):
            fns = []
            for h in range(2):
                for jc in range(JCN):
                    a0 = A_WS2 + jc * 1024 + h * 512
                    fns.append(lambda e, jc=jc, h=h, a0=a0, pY=pY, par=par,
                               sub=sub: e.matmul(
                        pY[:, h * 512:(h + 1) * 512],
                        WH[:, H2_HSC + par * 1024 + jc * 256 + sub * 128:
                           H2_HSC + par * 1024 + jc * 256 + sub * 128 + 128],
                        AR[:, a0:a0 + 512], start=(jc == 0), stop=False))
                fns.append(lambda e, h=h, pY=pY, par=par, sub=sub: e.matmul(
                    pY[:, h * 512:(h + 1) * 512],
                    GT9[0:9, par * 256 + sub * 128:par * 256 + sub * 128 + 128],
                    b2all[0:9, h * 512:(h + 1) * 512], start=False, stop=True))
            grp("pe", fns,
                reads=hsg + _arg(A_WS2, A_WS2 + 4096) + gtg +
                ("b2sb", "bs2sb", "gt9ones"), writes=(pYn,))
            yc = F2_Y + sub * 1024
            yg = _wfg(yc, yc + 1024)
            op("ac", lambda e, pY=pY, yc=yc: e.activation(
                WF[:, yc:yc + 1024], pY[:], AF.Copy), reads=(pYn,), writes=yg)
            gsub = (si * S + c0) // 128 + sub
            op("sy", lambda e, yc=yc, gsub=gsub: e.dma_start(
                y_scr[gsub], WF[:, yc:yc + 1024]),
                reads=yg, writes=(f"yscr:{gsub}",), ch=f"yst{sub}")

    if phases < 3:
        _finish(em)
        return _realize(nc, em)
    for sub_ in range(2):
        ch = f"yst{sub_}"
        if ch in em.cnt:
            em.retag([k for k in em.wtok if k.startswith("yscr:")
                      and em.wtok[k][0] == ch], (ch, em.cnt[ch]))
    load_nat(Wv, A_NATV, "natq")
    load_nat(Wo, A_NATO, "natk")
    transpose_weight(A_NATV, A_WVT)
    transpose_weight(A_NATO, A_WOT)

    def p3load_xn(ti):
        si, c0 = ti // T3PS, (ti % T3PS) * 256
        par = ti % 2
        u0 = c0 // 128
        xn3g = _whg(H3_XNF + par * 2048, H3_XNF + (par + 1) * 2048)
        op("sy", lambda e, si=si, c0=c0, par=par: e.dma_start(
            WH[:, H3_XNF + par * 2048:H3_XNF + (par + 1) * 2048].rearrange(
                "p (c t) -> p c t", t=256),
            xn_scr[si, :, :, c0:c0 + 256].rearrange("c p t -> p c t")),
            reads=(f"xnscr:{si}:{u0}", f"xnscr:{si}:{u0 + 1}"), writes=xn3g,
            ch=f"xn2l{par}")

    def p3load_sub(ti, sub):
        si, c0 = ti // T3PS, (ti % T3PS) * 256
        gsub = (si * S + c0) // 128 + sub
        spar = gsub % 2
        xg = _wfg(F3_X + spar * 1024, F3_X + (spar + 1) * 1024)
        ygd = _wfg(F3_Y + spar * 1024, F3_Y + (spar + 1) * 1024)
        op("sy", lambda e, gsub=gsub, spar=spar: e.dma_start(
            WF[:, F3_Y + spar * 1024:F3_Y + (spar + 1) * 1024],
            y_scr[gsub]), reads=(f"yscr:{gsub}",), writes=ygd,
            ch=f"yl{spar}")
        op("gp", lambda e, gsub=gsub, spar=spar: e.dma_start(
            WF[:, F3_Y + spar * 1024:F3_Y + (spar + 1) * 1024],
            yab_scr[gsub * 128:gsub * 128 + 128, :],
            accum_op=ALU.add),
            reads=("yab0", "yab1"), writes=ygd, ch=f"yaa{spar}")
        op("gp", lambda e, gsub=gsub, spar=spar: e.dma_start(
            WF[:, F3_Y + spar * 1024:F3_Y + (spar + 1) * 1024],
            yab_scr[NTOK + gsub * 128:NTOK + gsub * 128 + 128, :],
            accum_op=ALU.add),
            reads=("yab0", "yab1"), writes=ygd, ch=f"yab{spar}")
        op("sy", lambda e, si=si, c0=c0, sub=sub, spar=spar: e.dma_start(
            WF[:, F3_X + spar * 1024:F3_X + (spar + 1) * 1024],
            x_in[si, c0 + sub * 128:c0 + sub * 128 + 128, :]), writes=xg,
            ch=f"xl{spar}")

    p3load_xn(0)
    p3load_sub(0, 0)
    p3load_sub(0, 1)
    for ti in range(NT3):
        si, c0 = ti // T3PS, (ti % T3PS) * 256
        par = ti % 2
        xn3g = _whg(H3_XNF + par * 2048, H3_XNF + (par + 1) * 2048)
        if p3cut < 2:
            continue
        vg = _whg(H3_V, H3_V + 2048)
        for ic in range(DC):
            pp, pname = (pfa, "pfa") if ic % 2 == 0 else (pfb, "pfb")
            fns = []
            for c in range(DC):
                fns.append(lambda e, ic=ic, c=c, par=par, pp=pp: e.matmul(
                    pp[:, 0:256],
                    AR[:, A_WVT + c * 1024 + ic * 128:
                       A_WVT + c * 1024 + (ic + 1) * 128],
                    WH[:, H3_XNF + par * 2048 + c * 256:
                       H3_XNF + par * 2048 + (c + 1) * 256],
                    start=(c == 0), stop=(c == DC - 1)))
            grp("pe", fns, reads=xn3g + _arg(A_WVT, A_WVT + 8192),
                writes=(pname,))
            op("ac", lambda e, ic=ic, pp=pp, pname=pname, si=si: e.activation(
                WH[:, H3_V + ic * 256:H3_V + (ic + 1) * 256], pp[:, 0:256],
                AF.Copy, scale=RSM[:, si * 8 + ic:si * 8 + ic + 1]),
                reads=(pname, f"rsm:{si}"), writes=vg)
        if p3cut < 3:
            continue
        atg = _whg(H3_AT, H3_AT + 2048)
        for p in range(NPAIR):
            pp, pname = (pfa, "pfa") if p % 2 == 0 else (pfb, "pfb")
            fns = [lambda e, p=p, si=si, pp=pp: e.matmul(
                pp[:, 0:256],
                ASM[:, si * 1024 + p * 128:si * 1024 + (p + 1) * 128],
                WH[:, H3_V + p * 256:H3_V + (p + 1) * 256],
                start=True, stop=True)]
            grp("pe", fns, reads=(f"asm:{si}",) + vg, writes=(pname,))
            if p3cut == 30:
                continue
            op("ac", lambda e, p=p, pp=pp, pname=pname: e.activation(
                WH[:, H3_AT + p * 256:H3_AT + (p + 1) * 256],
                pp[:, 0:256], AF.Copy),
                reads=(pname,), writes=atg)
        if p3cut < 4 or p3cut == 30:
            continue
        for sub in range(2):
            gsub = (si * S + c0) // 128 + sub
            spar = gsub % 2
            pY, pYn = (pY0, "pY0") if sub == 0 else (pY1, "pY1")
            xg = _wfg(F3_X + spar * 1024, F3_X + (spar + 1) * 1024)
            ygd = _wfg(F3_Y + spar * 1024, F3_Y + (spar + 1) * 1024)
            og = _wfg(F3_O + spar * 1024, F3_O + (spar + 1) * 1024)
            fns = []
            for p in range(NPAIR):
                for h in range(2):
                    fns.append(lambda e, p=p, h=h, sub=sub, pY=pY: e.matmul(
                        pY[:, h * 512:(h + 1) * 512],
                        WH[:, H3_AT + p * 256 + sub * 128:
                           H3_AT + p * 256 + sub * 128 + 128],
                        AR[:, A_WOT + p * 1024 + h * 512:
                           A_WOT + p * 1024 + (h + 1) * 512],
                        start=(p == 0), stop=(p == NPAIR - 1)))
            grp("pe", fns, reads=atg + _arg(A_WOT, A_WOT + 8192), writes=(pYn,))
            if p3cut < 5:
                continue
            op("ac", lambda e, spar=spar: e.activation(
                WF[:, F3_O + spar * 1024:F3_O + (spar + 1) * 1024],
                WF[:, F3_Y + spar * 1024:F3_Y + (spar + 1) * 1024], AF.Silu),
                reads=ygd, writes=og)
            if p3cut < 6:
                continue
            op("ve", lambda e, spar=spar, pY=pY: e.tensor_tensor(
                WF[:, F3_O + spar * 1024:F3_O + (spar + 1) * 1024],
                WF[:, F3_O + spar * 1024:F3_O + (spar + 1) * 1024],
                pY[:], ALU.mult), reads=og + (pYn,), writes=og)
            op("ve", lambda e, spar=spar: e.tensor_tensor(
                WF[:, F3_O + spar * 1024:F3_O + (spar + 1) * 1024],
                WF[:, F3_O + spar * 1024:F3_O + (spar + 1) * 1024],
                WF[:, F3_X + spar * 1024:F3_X + (spar + 1) * 1024], ALU.add),
                reads=og + xg, writes=og)
            op("gp", lambda e, si=si, c0=c0, sub=sub, spar=spar: e.dma_start(
                out[si, c0 + sub * 128:c0 + sub * 128 + 128, :],
                WF[:, F3_O + spar * 1024:F3_O + (spar + 1) * 1024]),
                reads=og, ch=f"outst{spar}")
            if ti + 1 < NT3:
                if sub == 0:
                    p3load_xn(ti + 1)
                p3load_sub(ti + 1, sub)

    if dbg:
        _dbg_dump(nc, em, op, locals())
    _finish(em)
    return _realize(nc, em)


def _dbg_dump(nc, em, op, L):
    xm_scr, idx_scr, yab_scr, y_scr, gate_scr = (
        L["xm_scr"], L["idx_scr"], L["yab_scr"], L["y_scr"], L["gate_scr"])
    for name, ap, shp, dt in (
            ("d_xm", xm_scr, list(xm_scr.shape), F32),
            ("d_idx", idx_scr, list(idx_scr.shape), mybir.dt.int32),
            ("d_yab", yab_scr, list(yab_scr.shape), F32),
            ("d_ysc", y_scr, list(y_scr.shape), F32),
            ("d_gate", gate_scr, list(gate_scr.shape), F32)):
        o = nc.dram_tensor(name, shp, dt, kind="ExternalOutput").ap()
        op("gp", lambda e, o=o, ap=ap: e.dma_start(o, ap),
            reads=(), writes=(), ch="dbgd")


def _finish(em):
    for ch, v in sorted(em.cnt.items()):
        if ch in ("pe", "ve", "ac"):
            continue
        em._wait("gp", (ch, v))
        em._wait("sy", (ch, v))


def _realize(nc, em):
    with nc.Block() as block:
        block.sync(lambda e: [f(e) for f in em.sched["sy"]])
        block.gpsimd(lambda e: [f(e) for f in em.sched["gp"]])
        block.tensor(lambda e: [f(e) for f in em.sched["pe"]])
        block.vector(lambda e: [f(e) for f in em.sched["ve"]])
        block.scalar(lambda e: [f(e) for f in em.sched["ac"]])
    return nc


def kernel(**inputs):
    f = np.float32
    x = np.ascontiguousarray(np.asarray(inputs["L0_attn_x"], dtype=f))
    rs = np.asarray(inputs["L0_ffn_route_scale"], dtype=f).reshape(1, 1)
    rs_col = np.ascontiguousarray(np.broadcast_to(rs, (NE, 1)))
    shared = {}
    for k in ("L0_attn_Wq", "L0_attn_Wk", "L0_attn_Wv", "L0_attn_Wo",
              "L0_ffn_gate_W", "L0_ffn_W1", "L0_ffn_b1", "L0_ffn_W2",
              "L0_ffn_b2", "L0_ffn_Ws1", "L0_ffn_bs1", "L0_ffn_Ws2",
              "L0_ffn_bs2"):
        shared[k] = np.ascontiguousarray(np.asarray(inputs[k], dtype=f))
    nseq = x.shape[0] // NCORES
    S = x.shape[1]
    nc = build_nc(nseq, S)
    in_maps = []
    for c in range(NCORES):
        m = dict(shared)
        m["L0_attn_x"] = np.ascontiguousarray(x[c * nseq:(c + 1) * nseq])
        m["route_scale_col"] = rs_col
        in_maps.append(m)
    res = run_bass_kernel_spmd(nc, in_maps, core_ids=list(range(NCORES)))
    outs = [np.asarray(r["out"]) for r in res.results]
    return np.concatenate(outs, axis=0).astype(np.float32)
